# version 05
# baseline (speedup 1.0000x reference)
import numpy as np
import ml_dtypes
from contextlib import ExitStack

import concourse.bass as bass
import concourse.mybir as mybir
from concourse.bass_utils import run_bass_kernel_spmd

F32 = mybir.dt.float32
BF16 = mybir.dt.bfloat16
AF = mybir.ActivationFunctionType
ALU = mybir.AluOpType
BF16NP = ml_dtypes.bfloat16

T = 2048
DM = 1024
NT = T // 128
ND = DM // 128
LAT = 256
NH = 16
HD = 64
SCALE = 0.125
DI = 2048
NDI = DI // 128
DS = 16
H1 = 1024
NE = 8
LN_EPS = 1e-5
RMS_EPS = 1e-6

ALL_BLOCKS = ("l0a", "l0f", "l1a", "l1f")


class Builder:
    def __init__(self, blocks, debug=False):
        self.debug = debug
        self.blocks = list(blocks)
        nc = bass.Bass("TRN2", target_bir_lowering=False, debug=False)
        self.nc = nc
        self.ctx = ExitStack()
        d = {}
        self.d = d

        def dram(name, shape, kind="ExternalInput", dtype=F32):
            d[name] = nc.dram_tensor(name, list(shape), dtype, kind=kind).ap()
        blocks = self.blocks
        dram("x_in", [T, DM])
        dram("y_out", [T, DM], kind="ExternalOutput")
        if "l0a" in blocks:
            for n, s in [("wq_lat", [DM, LAT]), ("wk_lat", [DM, LAT]), ("wv_lat", [DM, LAT]),
                         ("wq_heads", [LAT, DM]), ("wk_heads", [LAT, DM]), ("wv_heads", [LAT, DM]),
                         ("w_proj", [DM, DM]), ("b_proj", [1, DM])]:
                dram(n, s)
        if "l0f" in blocks:
            dram("f0_scale", [ND, 128])
            dram("f0_router", [DM, NE])
            dram("f0_up", [NE, DM, 2 * DI])
            dram("f0_down", [NE, DI, DM])
        if "l1a" in blocks:
            dram("a1_Win", [DM, 2 * DI])
            dram("a1_bin", [1, 2 * DI])
            dram("a1_Wx", [DI, 2 * DS])
            dram("a1_bx", [1, 2 * DS])
            dram("a1_Wdt", [DI, DI])
            dram("a1_bdt", [1, DI])
            dram("a1_Wout", [DI, DM])
            dram("a1_bout", [1, DM])
            dram("a1_Alog", [DI, DS])
            dram("a1_D", [NDI, 128])
            dram("sp_dt", [DI, T], kind="Internal", dtype=BF16)
            dram("sp_dtu", [DI, T], kind="Internal", dtype=BF16)
            dram("sp_u", [DI, T], kind="Internal", dtype=BF16)
            dram("sp_sz", [DI, T], kind="Internal", dtype=BF16)
        if "l1f" in blocks:
            dram("f1_gate", [DM, NE])
            dram("f1_up", [NE, DM, H1])
            dram("f1_upb", [NE * ND, 128])
            dram("f1_down", [NE, H1, DM])
            dram("f1_downb", [NE, DM])
            dram("f1_shup", [DM, H1])
            dram("f1_shupb", [1, H1])
            dram("f1_shdown", [H1, DM])
            dram("f1_shdownb", [1, DM])
        if debug:
            dram("dbg_a", [128, 2048], kind="ExternalOutput")
            dram("dbg_b", [128, 2048], kind="ExternalOutput")
            dram("dbg_xn", [DM, T], kind="ExternalOutput", dtype=BF16)
        dram("identF", [128, 128])
        dram("onesrow", [1, T])
        dram("onesrowF", [1, 128])
        dram("cvals", [128, 8])
        dram("identB", [128, 128], dtype=BF16)
        if "l0a" in blocks:
            dram("maskT", [128, 128], dtype=BF16)
        if "l0f" in blocks or "l1f" in blocks:
            dram("sel8", [NE, NE * 128], dtype=BF16)
        if "l1a" in blocks:
            dram("sel16", [DS, DS * 128], dtype=BF16)

        self.sem = {}
        self.cnt = {}
        for s in ("sW", "sX", "sPE", "sDV", "sAC", "sPO", "sBAR"):
            self.sem[s] = self.ctx.enter_context(nc.semaphore(s))
            self.cnt[s] = 0
        self.last_inst = {}
        self.eng_sem = {"tensor": "sPE", "vector": "sDV", "scalar": "sAC", "gpsimd": "sPO"}
        self.ac_evac_hist = []
        self.sz_spill_hist = []
        self.dt_spill_hist = []
        self.dt_dv_hist = []
        self.cnt_sDV_bb = []
        self.ssm_dv_hist = []
        self.wout_dv_hist = []

        sb = self.sb = lambda name, shape, dt: self.ctx.enter_context(
            nc.sbuf_tensor(name, list(shape), dt))
        self.cvals_list = [0.0, 1.0, -0.5, LN_EPS, RMS_EPS, SCALE, -SCALE, 1.0 / DM]
        self.cvals = sb("c_cvals", [128, len(self.cvals_list)], F32)
        for j, v in enumerate(self.cvals_list):
            nc.const_aps.register(F32, v, self.cvals[:, j:j + 1])
        self.resid = [sb(f"resid{i}", [128, T], F32) for i in range(ND)]
        self.identF = sb("c_identF", [128, 128], F32)
        self.ones = sb("c_ones", [1, T], BF16)
        self.onesrF = sb("c_onesrF", [1, 128], F32)
        self.identB = sb("c_identB", [128, 128], BF16)
        if "l0a" in blocks:
            self.maskT = sb("c_maskT", [128, 128], BF16)
        if "l0f" in blocks or "l1f" in blocks:
            self.sel8 = sb("c_sel8", [NE, NE * 128], BF16)
        if "l1a" in blocks:
            self.sel16 = sb("c_sel16", [DS, DS * 128], BF16)
        self.block = None

    def inc(self, inst, s, k=1):
        inst.then_inc(self.sem[s], k)
        self.cnt[s] += k
        return inst

    def note(self, eng, inst):
        self.last_inst[eng] = inst
        return inst

    def mark(self, eng, inst):
        self.note(eng, inst)
        s = self.eng_sem[eng]
        return self.inc(inst, s)

    def barrier(self):
        self.last_inst = {}
        tgts = {s: self.cnt[s] for s in ("sPE", "sDV", "sAC", "sPO", "sW", "sX")}
        for name in ("tensor", "vector", "scalar", "gpsimd", "sync"):
            def body(e, tgts=tgts):
                for s, v in tgts.items():
                    if v > 0:
                        e.wait_ge(self.sem[s], v)
            getattr(self.block, name)(body)

    def sb_local(self, stack, name, shape, dt):
        return stack.enter_context(self.nc.sbuf_tensor(name, list(shape), dt))

    def loadW(self, g, dst_ap, src_ap):
        return self.inc(g.dma_start(out=dst_ap, in_=src_ap), "sW", 16)

    def loadX(self, s, dst_ap, src_ap):
        return self.inc(s.dma_start(out=dst_ap, in_=src_ap), "sX", 16)

    def load_wchunks(self, g, dst_sb, src, K, M):
        nk = K // 128
        src_r = src.rearrange("(kj p) m -> p kj m", p=128)
        dst = dst_sb[:, 0:nk * M].rearrange("p (kj m) -> p kj m", kj=nk)
        return self.loadW(g, dst, src_r)

    def build(self):
        nc = self.nc
        self.block = self.ctx.enter_context(nc.Block())
        self.phase_consts()
        self.phase_in()
        for b in self.blocks:
            getattr(self, "phase_" + b)()
        self.phase_out()
        self.ctx.close()
        return nc

    def phase_consts(self):
        blk, d = self.block, self.d

        def s_body(s):
            self.loadX(s, self.cvals[:], d["cvals"][:])
            self.loadX(s, self.identF[:], d["identF"][:])
            self.loadX(s, self.onesrF[:], d["onesrowF"][:])
            self.loadX(s, self.identB[:], d["identB"][:])
            if "l0a" in self.blocks:
                self.loadX(s, self.maskT[:], d["maskT"][:])
            if "l0f" in self.blocks or "l1f" in self.blocks:
                self.loadX(s, self.sel8[:], d["sel8"][:])
            if "l1a" in self.blocks:
                self.loadX(s, self.sel16[:], d["sel16"][:])
        blk.sync(s_body)

        def g_body(g):
            self.loadW(g, self.ones[:], d["onesrow"][:])
        blk.gpsimd(g_body)
        self.barrier()

    def phase_in(self):
        nc, blk, d = self.nc, self.block, self.d
        with ExitStack() as ph:
            xtok = [ph.enter_context(nc.sbuf_tensor(f"in_xtok{i}", [128, DM], F32)) for i in range(2)]
            pst = [ph.enter_context(nc.psum_tensor(f"in_ps{i}", [128, 512], F32)) for i in range(2)]
            x0, pe0, dv0 = self.cnt["sX"], self.cnt["sPE"], self.cnt["sDV"]

            def s_body(s):
                for ti in range(NT):
                    if ti >= 2:
                        s.wait_ge(self.sem["sPE"], pe0 + 2 * ti - 2)
                    self.loadX(s, xtok[ti % 2][:], d["x_in"][ti * 128:(ti + 1) * 128, :])
            blk.sync(s_body)

            def pe_body(t):
                for ti in range(NT):
                    t.wait_ge(self.sem["sX"], x0 + 16 * (ti + 1))
                    for half in range(2):
                        if ti >= 1:
                            t.wait_ge(self.sem["sDV"], dv0 + 2 * (ti - 1) + half + 1)
                        for j in range(4):
                            dj = half * 4 + j
                            inst = t.transpose(pst[half][:, j * 128:(j + 1) * 128],
                                               xtok[ti % 2][:, dj * 128:(dj + 1) * 128],
                                               self.identF[:])
                        self.mark("tensor", inst)
            blk.tensor(pe_body)

            def dv_body(v):
                for ti in range(NT):
                    for half in range(2):
                        v.wait_ge(self.sem["sPE"], pe0 + 2 * ti + half + 1)
                        inst = None
                        for j in range(4):
                            dj = half * 4 + j
                            inst = v.tensor_copy(self.resid[dj][:, ti * 128:(ti + 1) * 128],
                                                 pst[half][:, j * 128:(j + 1) * 128])
                        self.mark("vector", inst)
            blk.vector(dv_body)
        self.barrier()

    def emit_ln(self, ph, xn, rms_scale_cols=None, skip_add=False, scratch=None):
        nc, blk = self.nc, self.block
        self.ln_id = getattr(self, "ln_id", 0) + 1
        u = self.ln_id
        with ExitStack() as ln:
            if scratch is None:
                sq = [ln.enter_context(nc.sbuf_tensor(f"ln{u}_sq{i}", [128, 512], F32))[:] for i in range(2)]
                tn = [ln.enter_context(nc.sbuf_tensor(f"ln{u}_tn{i}", [128, 512], F32))[:] for i in range(2)]
                mrow = ln.enter_context(nc.sbuf_tensor(f"ln{u}_m", [1, 512], F32))[:]
                vrow = ln.enter_context(nc.sbuf_tensor(f"ln{u}_v", [1, 512], F32))[:]
                rrow = ln.enter_context(nc.sbuf_tensor(f"ln{u}_r", [1, 512], F32))[:]
                mrrow = ln.enter_context(nc.sbuf_tensor(f"ln{u}_mr", [1, 512], F32))[:]
            else:
                sv = scratch
                sq = [sv[:, 0:512], sv[:, 512:1024]]
                tn = [sv[:, 1024:1536], sv[:, 1536:2048]]
                mrow = sv[0:1, 2048:2560]
                vrow = sv[0:1, 2560:3072]
                rrow = sv[0:1, 3072:3584]
                mrrow = sv[0:1, 3584:4096]
            onesc = self.cvals[:, 1:2]
            onesr = self.onesrF
            ps_s = ln.enter_context(nc.psum_tensor(f"ln{u}_ps_s", [128, 512], F32))
            ps_q = ln.enter_context(nc.psum_tensor(f"ln{u}_ps_q", [128, 512], F32))
            ps_b = [ln.enter_context(nc.psum_tensor(f"ln{u}_ps_b{i}", [128, 512], F32)) for i in range(2)]
            peLN0 = self.cnt["sPE"]

            for tc in range(4):
                sl = slice(tc * 512, (tc + 1) * 512)
                ac_base = self.cnt["sAC"]
                pe_tc0 = self.cnt["sPE"]

                def ac_sq(a, sl=sl, tc=tc, pe_tc0=pe_tc0):
                    for dj in range(ND):
                        if dj < 2:
                            if tc > 0:
                                a.wait_ge(self.sem["sPE"], pe_tc0)
                        else:
                            a.wait_ge(self.sem["sPE"], pe_tc0 + dj - 1)
                        self.mark("scalar", a.activation(sq[dj % 2], self.resid[dj][:, sl], AF.Square))
                blk.scalar(ac_sq)

                dv_prev_stats = self.cnt["sDV"]

                def pe_body(t, sl=sl, ac_base=ac_base, dv_prev_stats=dv_prev_stats):
                    t.wait_ge(self.sem["sDV"], dv_prev_stats)
                    for dj in range(ND):
                        t.matmul(ps_s[0:1, :], onesc, self.resid[dj][:, sl],
                                 start=(dj == 0), stop=(dj == ND - 1))
                    for dj in range(ND):
                        t.wait_ge(self.sem["sAC"], ac_base + dj + 1)
                        inst = t.matmul(ps_q[0:1, :], onesc, sq[dj % 2],
                                        start=(dj == 0), stop=(dj == ND - 1))
                        self.mark("tensor", inst)
                blk.tensor(pe_body)
                pe_done = self.cnt["sPE"]

                def dv_stats(v, sl=sl, pe_done=pe_done):
                    v.wait_ge(self.sem["sPE"], pe_done)
                    v.tensor_scalar_mul(mrow, ps_s[0:1, :], 1.0 / DM)
                    v.tensor_scalar_mul(vrow, ps_q[0:1, :], 1.0 / DM)
                    v.tensor_tensor(ps_s[0:1, :], mrow, mrow, ALU.mult)
                    self.mark("vector", v.tensor_tensor(vrow, vrow, ps_s[0:1, :], ALU.subtract))
                blk.vector(dv_stats)
                dv_done = self.cnt["sDV"]

                def ac_rstd(a, sl=sl, dv_done=dv_done):
                    a.wait_ge(self.sem["sDV"], dv_done)
                    a.activation(rrow, vrow, AF.Ln, bias=LN_EPS)
                    self.mark("scalar", a.activation(rrow, rrow, AF.Exp, scale=-0.5))
                blk.scalar(ac_rstd)
                ac_done = self.cnt["sAC"]

                def dv_mr(v, sl=sl, ac_done=ac_done):
                    v.wait_ge(self.sem["sAC"], ac_done)
                    self.mark("vector", v.tensor_tensor(mrrow, mrow, rrow, ALU.mult))
                blk.vector(dv_mr)
                dv_done2 = self.cnt["sDV"]

                def pe_bcast(t, sl=sl, dv_done2=dv_done2):
                    t.wait_ge(self.sem["sDV"], dv_done2)
                    t.matmul(ps_b[0][:], onesr[:], rrow, start=True, stop=True)
                    self.mark("tensor", t.matmul(ps_b[1][:], onesr[:], mrrow, start=True, stop=True))
                blk.tensor(pe_bcast)
                pe_done2 = self.cnt["sPE"]

                def dv_norm(v, sl=sl, pe_done2=pe_done2):
                    v.wait_ge(self.sem["sPE"], pe_done2)
                    for dj in range(ND):
                        v.tensor_tensor(tn[dj % 2], self.resid[dj][:, sl], ps_b[0][:], ALU.mult)
                        inst = v.tensor_tensor(xn[dj][:, sl], tn[dj % 2], ps_b[1][:], ALU.subtract)
                        self.mark("vector", inst)
                blk.vector(dv_norm)

            if skip_add:
                def dv_skip(v):
                    for dj in range(ND):
                        inst = v.tensor_tensor(self.resid[dj][:], self.resid[dj][:],
                                               xn[dj][:], ALU.add)
                    self.mark("vector", inst)
                blk.vector(dv_skip)

            if rms_scale_cols is not None:
                for tc in range(4):
                    sl = slice(tc * 512, (tc + 1) * 512)
                    dv_n = self.cnt["sDV"]
                    ac_base2 = self.cnt["sAC"]
                    pe_tc0 = self.cnt["sPE"]

                    def ac_sq2(a, sl=sl, dv_n=dv_n, tc=tc, pe_tc0=pe_tc0):
                        a.wait_ge(self.sem["sDV"], dv_n)
                        for dj in range(ND):
                            if dj < 2:
                                a.wait_ge(self.sem["sPE"], pe_tc0)
                            else:
                                a.wait_ge(self.sem["sPE"], pe_tc0 + dj - 1)
                            self.mark("scalar", a.activation(sq[dj % 2], xn[dj][:, sl], AF.Square))
                    blk.scalar(ac_sq2)

                    dv_cur = self.cnt["sDV"]

                    def pe_sum2(t, sl=sl, ac_base2=ac_base2, dv_cur=dv_cur):
                        t.wait_ge(self.sem["sDV"], dv_cur)
                        for dj in range(ND):
                            t.wait_ge(self.sem["sAC"], ac_base2 + dj + 1)
                            inst = t.matmul(ps_q[0:1, :], onesc, sq[dj % 2],
                                            start=(dj == 0), stop=(dj == ND - 1))
                            self.mark("tensor", inst)
                    blk.tensor(pe_sum2)
                    pe_d = self.cnt["sPE"]

                    def dv_ms(v, sl=sl, pe_d=pe_d):
                        v.wait_ge(self.sem["sPE"], pe_d)
                        self.mark("vector", v.tensor_scalar_mul(vrow, ps_q[0:1, :], 1.0 / DM))
                    blk.vector(dv_ms)
                    dv_d = self.cnt["sDV"]

                    def ac_r2(a, sl=sl, dv_d=dv_d):
                        a.wait_ge(self.sem["sDV"], dv_d)
                        a.activation(rrow, vrow, AF.Ln, bias=RMS_EPS)
                        self.mark("scalar", a.activation(rrow, rrow, AF.Exp, scale=-0.5))
                    blk.scalar(ac_r2)
                    ac_d = self.cnt["sAC"]

                    dv_cur2 = self.cnt["sDV"]

                    def pe_b2(t, sl=sl, ac_d=ac_d, dv_cur2=dv_cur2):
                        t.wait_ge(self.sem["sAC"], ac_d)
                        t.wait_ge(self.sem["sDV"], dv_cur2)
                        self.mark("tensor", t.matmul(ps_b[0][:], onesr[:], rrow, start=True, stop=True))
                    blk.tensor(pe_b2)
                    pe_d2 = self.cnt["sPE"]

                    def dv_apply(v, sl=sl, pe_d2=pe_d2):
                        v.wait_ge(self.sem["sPE"], pe_d2)
                        for dj in range(ND):
                            inst = v.scalar_tensor_tensor(
                                xn[dj][:, sl], xn[dj][:, sl], rms_scale_cols[:, dj:dj + 1],
                                ps_b[0][:], ALU.mult, ALU.mult)
                            self.mark("vector", inst)
                    blk.vector(dv_apply)
        self.barrier()

    def phase_l0a(self):
        nc, blk, d = self.nc, self.block, self.d
        NG = 4
        HPG = 4
        with ExitStack() as ph:
            xn = [ph.enter_context(nc.sbuf_tensor(f"a0_xn{i}", [128, T], BF16)) for i in range(ND)]
            wql = ph.enter_context(nc.sbuf_tensor("a0_wql", [128, ND * LAT], BF16))
            wkl = ph.enter_context(nc.sbuf_tensor("a0_wkl", [128, ND * LAT], BF16))
            wvl = ph.enter_context(nc.sbuf_tensor("a0_wvl", [128, ND * LAT], BF16))
            wqh = ph.enter_context(nc.sbuf_tensor("a0_wqh", [128, 2 * DM], BF16))
            wkh = ph.enter_context(nc.sbuf_tensor("a0_wkh", [128, 2 * DM], BF16))
            wvh = ph.enter_context(nc.sbuf_tensor("a0_wvh", [128, 2 * DM], BF16))
            wpr = ph.enter_context(nc.sbuf_tensor("a0_wpr", [128, 2 * DM], BF16))
            bpr = ph.enter_context(nc.sbuf_tensor("a0_bpr", [1, DM], BF16))

            def g_w(g):
                self.load_wchunks(g, wql, d["wq_lat"], DM, LAT)
                self.load_wchunks(g, wkl, d["wk_lat"], DM, LAT)
                self.load_wchunks(g, wvl, d["wv_lat"], DM, LAT)
                self.load_wchunks(g, wqh, d["wq_heads"], LAT, DM)
                self.load_wchunks(g, wkh, d["wk_heads"], LAT, DM)
                self.load_wchunks(g, wvh, d["wv_heads"], LAT, DM)
                self.loadW(g, bpr[:], d["b_proj"][:])
            blk.gpsimd(g_w)
            w_all = self.cnt["sW"]

            self.emit_ln(ph, xn)

            q1 = [ph.enter_context(nc.sbuf_tensor(f"a0_q1_{i}", [128, T], BF16)) for i in range(2)]
            k1 = [ph.enter_context(nc.sbuf_tensor(f"a0_k1_{i}", [128, T], BF16)) for i in range(2)]
            v1 = [ph.enter_context(nc.sbuf_tensor(f"a0_v1_{i}", [128, T], BF16)) for i in range(2)]
            qTg = [ph.enter_context(nc.sbuf_tensor(f"a0_qT{i}", [128, T], BF16)) for i in range(2)]
            kTg = [ph.enter_context(nc.sbuf_tensor(f"a0_kT{i}", [128, T], BF16)) for i in range(2)]
            vtokg = ph.enter_context(nc.sbuf_tensor("a0_vtok", [128, NT * 256], BF16))
            yTg = [ph.enter_context(nc.sbuf_tensor(f"a0_yT{i}", [128, T], BF16)) for i in range(2)]
            P = ph.enter_context(nc.sbuf_tensor("a0_P", [128, T], BF16))
            PT = ph.enter_context(nc.sbuf_tensor("a0_PT", [128, T], BF16))
            rmax = ph.enter_context(nc.sbuf_tensor("a0_rmax", [128, 1], F32))
            nbias = ph.enter_context(nc.sbuf_tensor("a0_nbias", [128, 1], F32))
            rsum = ph.enter_context(nc.sbuf_tensor("a0_rsum", [128, 1], F32))
            rrec = ph.enter_context(nc.sbuf_tensor("a0_rrec", [128, 1], F32))
            ps_rot = [ph.enter_context(nc.psum_tensor(f"a0_rot{i}", [128, 512], F32)) for i in range(2)]
            ps_S = ph.enter_context(nc.psum_tensor("a0_S", [128, T], F32))
            ps_pt = ph.enter_context(nc.psum_tensor("a0_pt", [128, 512], BF16))
            ps_y = ph.enter_context(nc.psum_tensor("a0_y", [64, 128], F32))

            def mm_evac_seq(jobs):
                pe0, dv0 = self.cnt["sPE"], self.cnt["sDV"]
                for i, (mm, ev) in enumerate(jobs):
                    r = i % 2

                    def pe_j(t, i=i, r=r, mm=mm):
                        t.wait_ge(self.sem["sDV"], dv0 + i - 1 if i >= 2 else dv0)
                        self.mark("tensor", mm(t, ps_rot[r]))
                    blk.tensor(pe_j)

                    def dv_j(v, i=i, r=r, ev=ev):
                        v.wait_ge(self.sem["sPE"], pe0 + i + 1)
                        self.mark("vector", ev(v, ps_rot[r]))
                    blk.vector(dv_j)

            jobs = []
            for (wsb, dst) in ((wql, q1), (wkl, k1), (wvl, v1)):
                for m2 in range(2):
                    for tc in range(4):
                        def mm(t, ps, wsb=wsb, m2=m2, tc=tc):
                            t.wait_ge(self.sem["sW"], w_all)
                            inst = None
                            for kj in range(ND):
                                inst = t.matmul(ps[:], wsb[:, kj * LAT + m2 * 128: kj * LAT + (m2 + 1) * 128],
                                                xn[kj][:, tc * 512:(tc + 1) * 512],
                                                start=(kj == 0), stop=(kj == ND - 1))
                            return inst

                        def ev(v, ps, dst=dst, m2=m2, tc=tc):
                            return v.tensor_copy(dst[m2][:, tc * 512:(tc + 1) * 512], ps[:])
                        jobs.append((mm, ev))
            mm_evac_seq(jobs)
            self.barrier()

            for g in range(NG):
                gofs = g * 256
                jobs = []
                for (wsb, dst) in ((wqh, qTg), (wkh, kTg)):
                    for mi in range(2):
                        for tc in range(4):
                            def mm(t, ps, wsb=wsb, mi=mi, tc=tc):
                                inst = None
                                for kj in range(2):
                                    inst = t.matmul(
                                        ps[:],
                                        wsb[:, kj * DM + gofs + mi * 128: kj * DM + gofs + (mi + 1) * 128],
                                        (q1 if wsb is wqh else k1)[kj][:, tc * 512:(tc + 1) * 512],
                                        start=(kj == 0), stop=(kj == 1))
                                return inst

                            def ev(v, ps, dst=dst, mi=mi, tc=tc):
                                return v.tensor_copy(dst[mi][:, tc * 512:(tc + 1) * 512], ps[:])
                            jobs.append((mm, ev))
                for ti2 in range(NT // 2):
                    def mm(t, ps, ti2=ti2):
                        inst = None
                        for half in range(2):
                            ti = ti2 * 2 + half
                            for kj in range(2):
                                inst = t.matmul(
                                    ps[:, half * 256:(half + 1) * 256],
                                    v1[kj][:, ti * 128:(ti + 1) * 128],
                                    wvh[:, kj * DM + gofs: kj * DM + gofs + 256],
                                    start=(kj == 0), stop=(kj == 1))
                        return inst

                    def ev(v, ps, ti2=ti2):
                        return v.tensor_copy(vtokg[:, ti2 * 512:(ti2 + 1) * 512], ps[:])
                    jobs.append((mm, ev))
                mm_evac_seq(jobs)

                for h in range(HPG):
                    qrow = (h % 2) * 64
                    mi = h // 2
                    for qc in range(NT):
                        W = (qc + 1) * 128
                        ac_prev = self.cnt["sAC"]

                        def pe_S(t, qc=qc, mi=mi, qrow=qrow, ac_prev=ac_prev):
                            t.wait_ge(self.sem["sAC"], ac_prev)
                            wts = qTg[mi][qrow:qrow + 64, qc * 128:(qc + 1) * 128]
                            nch = (qc + 1) * 128
                            for c0 in range(0, nch, 512):
                                cw = min(512, nch - c0)
                                contains_diag = (c0 + cw == nch)
                                t.matmul(ps_S[:, c0:c0 + cw],
                                         wts, kTg[mi][qrow:qrow + 64, c0:c0 + cw],
                                         start=True, stop=not contains_diag)
                            inst = t.matmul(ps_S[:, qc * 128:(qc + 1) * 128],
                                            self.maskT[:], self.identB[:],
                                            start=False, stop=True)
                            self.mark("tensor", inst)
                        blk.tensor(pe_S)
                        peS = self.cnt["sPE"]

                        def dv_max(v, qc=qc, peS=peS):
                            v.wait_ge(self.sem["sPE"], peS)
                            v.reduce_max(rmax[:], ps_S[:, 0:(qc + 1) * 128], axis=mybir.AxisListType.X)
                            self.mark("vector", v.tensor_scalar_mul(nbias[:], rmax[:], -SCALE))
                        blk.vector(dv_max)
                        dvM = self.cnt["sDV"]

                        def ac_exp(a, qc=qc, dvM=dvM):
                            a.wait_ge(self.sem["sDV"], dvM)
                            inst = a.activation(P[:, 0:(qc + 1) * 128], ps_S[:, 0:(qc + 1) * 128],
                                                AF.Exp, scale=SCALE, bias=nbias[:],
                                                accum_out=rsum[:])
                            self.mark("scalar", inst)
                        blk.scalar(ac_exp)
                        acE = self.cnt["sAC"]

                        def dv_norm(v, qc=qc, acE=acE):
                            v.wait_ge(self.sem["sAC"], acE)
                            v.reciprocal(rrec[:], rsum[:])
                            self.inc(v.tensor_copy(nbias[:], rrec[:]), "sDV")
                            v.wait_ge(self.sem["sDV"], self.cnt["sDV"])
                            inst = v.tensor_scalar_mul(P[:, 0:(qc + 1) * 128], P[:, 0:(qc + 1) * 128], rrec[:])
                            self.mark("vector", inst)
                        blk.vector(dv_norm)
                        dvN = self.cnt["sDV"]

                        nblk = qc + 1
                        for jb in range(0, nblk, 4):
                            nj = min(4, nblk - jb)
                            dv_at = self.cnt["sDV"]

                            def pe_pt(t, jb=jb, nj=nj, dvN=dvN, dv_at=dv_at):
                                t.wait_ge(self.sem["sDV"], dvN if jb == 0 else dv_at)
                                for j in range(nj):
                                    inst = t.transpose(ps_pt[:, j * 128:(j + 1) * 128],
                                                       P[:, (jb + j) * 128:(jb + j + 1) * 128],
                                                       self.identB[:])
                                self.mark("tensor", inst)
                            blk.tensor(pe_pt)
                            pePT = self.cnt["sPE"]

                            def dv_pt(v, jb=jb, nj=nj, pePT=pePT):
                                v.wait_ge(self.sem["sPE"], pePT)
                                self.mark("vector", v.tensor_copy(PT[:, jb * 128:(jb + nj) * 128],
                                                                  ps_pt[:, 0:nj * 128]))
                            blk.vector(dv_pt)

                        dvPT = self.cnt["sDV"]

                        def pe_pv(t, qc=qc, h=h, mi=mi, qrow=qrow, dvPT=dvPT):
                            t.wait_ge(self.sem["sDV"], dvPT)
                            for j in range(qc + 1):
                                inst = t.matmul(ps_y[:],
                                                vtokg[:, j * 256 + h * 64: j * 256 + (h + 1) * 64],
                                                PT[:, j * 128:(j + 1) * 128],
                                                start=(j == 0), stop=(j == qc))
                            self.mark("tensor", inst)
                        blk.tensor(pe_pv)
                        pePV = self.cnt["sPE"]

                        def dv_y(v, qc=qc, mi=mi, qrow=qrow, pePV=pePV):
                            v.wait_ge(self.sem["sPE"], pePV)
                            self.mark("vector", v.tensor_copy(
                                yTg[mi][qrow:qrow + 64, qc * 128:(qc + 1) * 128], ps_y[:]))
                        blk.vector(dv_y)

                def g_pr(g_, g=g):
                    g_.wait_ge(self.sem["sPE"], self.cnt["sPE"])
                    self.load_wchunks(g_, wpr, d["w_proj"][g * 256:(g + 1) * 256, :], 256, DM)
                blk.gpsimd(g_pr)
                wpr_c = self.cnt["sW"]
                jobs = []
                for m in range(ND):
                    for tc in range(4):
                        def mm(t, ps, m=m, tc=tc, g=g, wpr_c=wpr_c):
                            t.wait_ge(self.sem["sW"], wpr_c)
                            for kj in range(2):
                                inst = t.matmul(ps[:],
                                                wpr[:, kj * DM + m * 128: kj * DM + (m + 1) * 128],
                                                yTg[kj][:, tc * 512:(tc + 1) * 512],
                                                start=(kj == 0), stop=(kj == 1 and g != 0))
                            if g == 0:
                                inst = t.matmul(ps[:], bpr[0:1, m * 128:(m + 1) * 128],
                                                self.ones[0:1, tc * 512:(tc + 1) * 512],
                                                start=False, stop=True)
                            return inst

                        def ev(v, ps, m=m, tc=tc):
                            return v.tensor_tensor(self.resid[m][:, tc * 512:(tc + 1) * 512],
                                                   self.resid[m][:, tc * 512:(tc + 1) * 512],
                                                   ps[:], ALU.add)
                        jobs.append((mm, ev))
                mm_evac_seq(jobs)
        self.barrier()

    def phase_l0f(self):
        nc, blk, d = self.nc, self.block, self.d
        with ExitStack() as ph:
            xn = [ph.enter_context(nc.sbuf_tensor(f"f0_xn{i}", [128, T], BF16)) for i in range(ND)]
            scale_rows = ph.enter_context(nc.sbuf_tensor("f0_sr", [ND, 128], F32))
            scale_cols = ph.enter_context(nc.sbuf_tensor("f0_sc", [128, ND], F32))
            wr = ph.enter_context(nc.sbuf_tensor("f0_wr", [128, ND * NE], BF16))
            scores = ph.enter_context(nc.sbuf_tensor("f0_scores", [128, NT * NE], F32))
            top8 = ph.enter_context(nc.sbuf_tensor("f0_top8", [128, NT * NE], F32))
            dif = ph.enter_context(nc.sbuf_tensor("f0_dif", [128, NT], F32))
            w2c = ph.enter_context(nc.sbuf_tensor("f0_w2", [128, NT], F32))
            w1c = ph.enter_context(nc.sbuf_tensor("f0_w1", [128, NT], F32))
            eq2 = ph.enter_context(nc.sbuf_tensor("f0_eq2", [128, NT * NE], F32))
            msb = ph.enter_context(nc.sbuf_tensor("f0_m", [128, NT * NE], F32))
            mTb = ph.enter_context(nc.sbuf_tensor("f0_mTb", [NE, T], BF16))
            mb = [ph.enter_context(nc.sbuf_tensor(f"f0_mb{i}", [128, T], BF16)) for i in range(2)]
            sg = [ph.enter_context(nc.sbuf_tensor(f"f0_sg{i}", [128, 512], BF16)) for i in range(2)]
            hsb = [ph.enter_context(nc.sbuf_tensor(f"f0_h{i}", [128, T], BF16)) for i in range(4)]
            wu = [ph.enter_context(nc.sbuf_tensor(f"f0_wu{i}", [128, ND * 512], BF16)) for i in range(2)]
            wg = [ph.enter_context(nc.sbuf_tensor(f"f0_wg{i}", [128, ND * 512], BF16)) for i in range(2)]
            wd = [ph.enter_context(nc.sbuf_tensor(f"f0_wd{i}", [128, 4 * DM], BF16)) for i in range(2)]

            def s_pre(s):
                self.loadX(s, scale_rows[:], d["f0_scale"][:])
            blk.sync(s_pre)
            x_sr = self.cnt["sX"]

            def g_pre(g):
                self.load_wchunks(g, wr, d["f0_router"], DM, NE)
            blk.gpsimd(g_pre)

            with nc.psum_tensor("f0_ps_sc", [128, 128], F32) as ps_sc:
                def pe_sc(t):
                    t.wait_ge(self.sem["sX"], x_sr)
                    self.mark("tensor", t.transpose(ps_sc[:, 0:ND], scale_rows[:],
                                                    self.identF[0:ND, 0:ND]))
                blk.tensor(pe_sc)
                pe_sc_c = self.cnt["sPE"]

                def dv_sc(v):
                    v.wait_ge(self.sem["sPE"], pe_sc_c)
                    self.mark("vector", v.tensor_copy(scale_cols[:], ps_sc[:, 0:ND]))
                blk.vector(dv_sc)
                self.barrier()

            self.emit_ln(ph, xn, rms_scale_cols=scale_cols, skip_add=True)
            ps_rot = [ph.enter_context(nc.psum_tensor(f"f0_rot{i}", [128, 512], F32)) for i in range(4)]
            ps_d = [ph.enter_context(nc.psum_tensor(f"f0_psd{i}", [128, 512], F32)) for i in range(4)]

            pe0, dv0 = self.cnt["sPE"], self.cnt["sDV"]
            for ti in range(NT):
                r = ti % 4

                def pe_s(t, ti=ti, r=r):
                    t.wait_ge(self.sem["sDV"], dv0 + ti - 3 if ti >= 4 else dv0)
                    for kj in range(ND):
                        inst = t.matmul(ps_rot[r][:, 0:NE], xn[kj][:, ti * 128:(ti + 1) * 128],
                                        wr[:, kj * NE:(kj + 1) * NE],
                                        start=(kj == 0), stop=(kj == ND - 1))
                    self.mark("tensor", inst)
                blk.tensor(pe_s)
                pec = self.cnt["sPE"]

                def dv_s(v, ti=ti, r=r, pec=pec):
                    v.wait_ge(self.sem["sPE"], pec)
                    self.mark("vector", v.tensor_copy(scores[:, ti * NE:(ti + 1) * NE], ps_rot[r][:, 0:NE]))
                blk.vector(dv_s)

            def dv_g1(v):
                for ti in range(NT):
                    v.max(top8[:, ti * NE:(ti + 1) * NE], scores[:, ti * NE:(ti + 1) * NE])
                t8v = top8[:].rearrange("p (ti e) -> p ti e", ti=NT)
                inst = v.tensor_tensor(dif[:], t8v[:, :, 1], t8v[:, :, 0], ALU.subtract)
                self.mark("vector", inst)
            blk.vector(dv_g1)
            dvg1 = self.cnt["sDV"]

            def ac_sig(a):
                a.wait_ge(self.sem["sDV"], dvg1)
                self.mark("scalar", a.activation(w2c[:], dif[:], AF.Sigmoid))
            blk.scalar(ac_sig)
            acsig = self.cnt["sAC"]

            def dv_g1b(v):
                v.wait_ge(self.sem["sAC"], acsig)
                t8v = top8[:].rearrange("p (ti e) -> p ti e", ti=NT)
                self.mark("vector", v.tensor_tensor(dif[:], t8v[:, :, 0], t8v[:, :, 1], ALU.subtract))
            blk.vector(dv_g1b)
            dvg1b = self.cnt["sDV"]

            def ac_sig2(a):
                a.wait_ge(self.sem["sDV"], dvg1b)
                self.mark("scalar", a.activation(w1c[:], dif[:], AF.Sigmoid))
            blk.scalar(ac_sig2)
            acsig2 = self.cnt["sAC"]

            def dv_g2(v):
                v.wait_ge(self.sem["sAC"], acsig2)
                for ti in range(NT):
                    v.tensor_scalar(msb[:, ti * NE:(ti + 1) * NE], scores[:, ti * NE:(ti + 1) * NE],
                                    top8[:, ti * NE:ti * NE + 1], None, ALU.is_equal)
                for ti in range(NT):
                    v.tensor_scalar(eq2[:, ti * NE:(ti + 1) * NE], scores[:, ti * NE:(ti + 1) * NE],
                                    top8[:, ti * NE + 1:ti * NE + 2], None, ALU.is_equal)
                for ti in range(NT):
                    v.tensor_scalar(msb[:, ti * NE:(ti + 1) * NE], msb[:, ti * NE:(ti + 1) * NE],
                                    w1c[:, ti:ti + 1], None, ALU.mult)
                for ti in range(NT):
                    v.tensor_scalar(eq2[:, ti * NE:(ti + 1) * NE], eq2[:, ti * NE:(ti + 1) * NE],
                                    w2c[:, ti:ti + 1], None, ALU.mult)
                inst = v.tensor_tensor(msb[:], msb[:], eq2[:], ALU.add)
                self.mark("vector", inst)
            blk.vector(dv_g2)
            dvg2 = self.cnt["sDV"]

            for tq in range(4):
                def pe_mt(t, tq=tq):
                    t.wait_ge(self.sem["sDV"], dvg2 if tq == 0 else self.cnt["sDV"])
                    for j in range(4):
                        ti = tq * 4 + j
                        inst = t.transpose(ps_rot[0][0:NE, j * 128:(j + 1) * 128],
                                           msb[:, ti * NE:(ti + 1) * NE], self.identF[:])
                    self.mark("tensor", inst)
                blk.tensor(pe_mt)
                pemt = self.cnt["sPE"]

                def dv_mt(v, tq=tq, pemt=pemt):
                    v.wait_ge(self.sem["sPE"], pemt)
                    self.mark("vector", v.tensor_copy(mTb[0:NE, tq * 512:(tq + 1) * 512],
                                                      ps_rot[0][0:NE, 0:512]))
                blk.vector(dv_mt)

            if self.debug:
                dvd = self.cnt["sDV"]

                def s_dbg(s):
                    s.wait_ge(self.sem["sDV"], dvd)
                    self.loadX(s, d["dbg_a"][:, 0:NT * NE], scores[:])
                    self.loadX(s, d["dbg_b"][:, 0:NT * NE], msb[:])
                    for dj in range(ND):
                        self.loadX(s, d["dbg_xn"][dj * 128:(dj + 1) * 128, :], xn[dj][:])
                blk.sync(s_dbg)

            slab_pe_done = {}
            slab_w = {}
            self.dv_psd_free = 0
            sg_hist = []

            def load_slab(i):
                e, j = divmod(i, 4)
                p = i % 2

                def g_l(g, e=e, j=j, p=p, i=i):
                    if i >= 2:
                        g.wait_ge(self.sem["sPE"], slab_pe_done[i - 2])
                    self.load_wchunks(g, wu[p], d["f0_up"][e][:, j * 512:(j + 1) * 512], DM, 512)
                    self.load_wchunks(g, wg[p], d["f0_up"][e][:, DI + j * 512:DI + (j + 1) * 512], DM, 512)
                    self.load_wchunks(g, wd[p], d["f0_down"][e][j * 512:(j + 1) * 512, :], 512, DM)
                blk.gpsimd(g_l)
                slab_w[i] = self.cnt["sW"]

            load_slab(0)
            load_slab(1)

            for e in range(NE):
                pb = e % 2
                dv_mbw = self.cnt["sDV"]
                for tc in range(4):
                    def pe_mb(t, e=e, tc=tc, dv_mbw=dv_mbw):
                        t.wait_ge(self.sem["sDV"], dv_mbw)
                        self.mark("tensor", t.matmul(ps_rot[tc][:], self.sel8[:, e * 128:(e + 1) * 128],
                                                     mTb[0:NE, tc * 512:(tc + 1) * 512],
                                                     start=True, stop=True))
                    blk.tensor(pe_mb)
                    pemb = self.cnt["sPE"]

                    def dv_mb(v, e=e, tc=tc, pemb=pemb, pb=pb):
                        v.wait_ge(self.sem["sPE"], pemb)
                        self.mark("vector", v.tensor_copy(mb[pb][:, tc * 512:(tc + 1) * 512], ps_rot[tc][:]))
                    blk.vector(dv_mb)

                dv_xre_c = self.cnt["sDV"]

                for j in range(4):
                    i = 4 * e + j
                    p = i % 2
                    upe_dv = {}
                    for mj in range(4):
                        for tc in range(4):
                            it = mj * 4 + tc
                            rot = it % 2

                            ngu = len(sg_hist)

                            def pe_u(t, p=p, mj=mj, tc=tc, it=it, rot=rot,
                                     dv_xre_c=dv_xre_c, i=i, ngu=ngu):
                                t.wait_ge(self.sem["sW"], slab_w[i])
                                t.wait_ge(self.sem["sDV"], dv_xre_c)
                                if ngu >= 2:
                                    t.wait_ge(self.sem["sDV"], sg_hist[ngu - 2])
                                for kj in range(ND):
                                    t.matmul(ps_rot[rot * 2][:],
                                             wu[p][:, kj * 512 + mj * 128: kj * 512 + (mj + 1) * 128],
                                             xn[kj][:, tc * 512:(tc + 1) * 512],
                                             start=(kj == 0), stop=(kj == ND - 1))
                                for kj in range(ND):
                                    inst = t.matmul(ps_rot[rot * 2 + 1][:],
                                                    wg[p][:, kj * 512 + mj * 128: kj * 512 + (mj + 1) * 128],
                                                    xn[kj][:, tc * 512:(tc + 1) * 512],
                                                    start=(kj == 0), stop=(kj == ND - 1))
                                self.mark("tensor", inst)
                            blk.tensor(pe_u)
                            peu = self.cnt["sPE"]

                            ng = len(sg_hist)

                            def ac_silu(a, rot=rot, it=it, peu=peu, ng=ng):
                                a.wait_ge(self.sem["sPE"], peu)
                                if ng >= 2:
                                    a.wait_ge(self.sem["sDV"], sg_hist[ng - 2])
                                self.mark("scalar", a.activation(sg[it % 2][:], ps_rot[rot * 2 + 1][:], AF.Silu))
                            blk.scalar(ac_silu)
                            acs = self.cnt["sAC"]

                            def dv_h(v, rot=rot, it=it, mj=mj, tc=tc, acs=acs, pb=pb):
                                v.wait_ge(self.sem["sAC"], acs)
                                hs = hsb[mj][:, tc * 512:(tc + 1) * 512]
                                v.tensor_tensor(hs, ps_rot[rot * 2][:], sg[it % 2][:], ALU.mult)
                                inst = v.tensor_tensor(hs, hs, mb[pb][:, tc * 512:(tc + 1) * 512], ALU.mult)
                                self.mark("vector", inst)
                            blk.vector(dv_h)
                            upe_dv[it] = self.cnt["sDV"]
                            sg_hist.append(self.cnt["sDV"])

                    dv_hdone = self.cnt["sDV"]
                    for tc in range(4):
                        for mg in range(2):
                            dv_psd_free = self.dv_psd_free

                            def pe_d(t, p=p, tc=tc, mg=mg, dv_hdone=dv_hdone,
                                     dv_psd_free=dv_psd_free):
                                t.wait_ge(self.sem["sDV"], max(dv_hdone, dv_psd_free))
                                for mi in range(4):
                                    m = mg * 4 + mi
                                    for kj in range(4):
                                        inst = t.matmul(
                                            ps_d[mi][:],
                                            wd[p][:, kj * DM + m * 128:kj * DM + (m + 1) * 128],
                                            hsb[kj][:, tc * 512:(tc + 1) * 512],
                                            start=(kj == 0), stop=(kj == 3))
                                self.mark("tensor", inst)
                            blk.tensor(pe_d)
                            ped = self.cnt["sPE"]

                            def dv_addd(v, tc=tc, mg=mg, ped=ped):
                                v.wait_ge(self.sem["sPE"], ped)
                                for mi in range(4):
                                    m = mg * 4 + mi
                                    inst = v.tensor_tensor(
                                        self.resid[m][:, tc * 512:(tc + 1) * 512],
                                        self.resid[m][:, tc * 512:(tc + 1) * 512],
                                        ps_d[mi][:], ALU.add)
                                self.mark("vector", inst)
                            blk.vector(dv_addd)
                            self.dv_psd_free = self.cnt["sDV"]

                    slab_pe_done[i] = self.cnt["sPE"]
                    if i + 2 < 4 * NE:
                        load_slab(i + 2)
        self.barrier()

    def phase_l1a(self):
        nc, blk, d = self.nc, self.block, self.d
        with ExitStack() as ph:
            wbig = ph.enter_context(nc.sbuf_tensor("a1_wbig", [128, 8 * 1024], BF16))
            BTb = ph.enter_context(nc.sbuf_tensor("a1_BTb", [DS, T], BF16))
            CTb = ph.enter_context(nc.sbuf_tensor("a1_CTb", [DS, T], BF16))
            bin_sb = ph.enter_context(nc.sbuf_tensor("a1_binb", [1, DI], BF16))
            bx_sb = ph.enter_context(nc.sbuf_tensor("a1_bxb", [1, 2 * DS], BF16))
            acol = ph.enter_context(nc.sbuf_tensor("a1_acol", [128, NDI * DS], F32))
            dcol = ph.enter_context(nc.sbuf_tensor("a1_dcol", [128, NDI], F32))
            carry = ph.enter_context(nc.sbuf_tensor("a1_carry", [128, NDI * DS], F32))
            ps_rot = [ph.enter_context(nc.psum_tensor(f"a1_rot{i}", [128, 512], F32)) for i in range(4)]

            def g_pre(g):
                self.loadW(g, bx_sb[:], d["a1_bx"][:])
            blk.gpsimd(g_pre)

            drows_cm = nc.sbuf_tensor("a1_drows", [NDI, 128], F32)
            drows = drows_cm.__enter__()

            def s_pre(s):
                self.loadX(s, drows[:], d["a1_D"][:])
                self.loadX(s, acol[:].rearrange("p (db s) -> p db s", db=NDI),
                           d["a1_Alog"][:].rearrange("(db p) s -> p db s", p=128))
            blk.sync(s_pre)
            x_pre = self.cnt["sX"]

            def ac_a(a):
                a.wait_ge(self.sem["sX"], x_pre)
                self.mark("scalar", a.activation(acol[:], acol[:], AF.Exp))
            blk.scalar(ac_a)
            ac_a_c = self.cnt["sAC"]

            def dv_a(v):
                v.wait_ge(self.sem["sAC"], ac_a_c)
                v.tensor_scalar_mul(acol[:], acol[:], -1.0)
                self.mark("vector", v.memset(carry[:], 0.0))
            blk.vector(dv_a)

            def pe_dc(t):
                t.wait_ge(self.sem["sX"], x_pre)
                self.mark("tensor", t.transpose(ps_rot[0][:, 0:NDI], drows[:],
                                                self.identF[0:NDI, 0:NDI]))
            blk.tensor(pe_dc)
            pedc = self.cnt["sPE"]

            def dv_dc(v):
                v.wait_ge(self.sem["sPE"], pedc)
                self.mark("vector", v.tensor_copy(dcol[:], ps_rot[0][:, 0:NDI]))
            blk.vector(dv_dc)
            self.barrier()
            drows_cm.__exit__(None, None, None)

            with ExitStack() as ph0:
                uT = [ph0.enter_context(nc.sbuf_tensor(f"a1_u{i}", [128, T], BF16)) for i in range(NDI)]
                szt = [ph0.enter_context(nc.sbuf_tensor(f"a1_szt{i}", [128, 512], BF16)) for i in range(2)]

                with ExitStack() as ph1:
                    xn = [ph1.enter_context(nc.sbuf_tensor(f"a1_xn{i}", [128, T], BF16)) for i in range(ND)]
                    self.emit_ln(ph1, xn, scratch=wbig[:].bitcast(F32))

                    for half, (cofs, spill) in enumerate([(DI, "sp_sz"), (0, "sp_u")]):
                        for q in range(4):
                            def g_w(g, cofs=cofs, q=q):
                                g.wait_ge(self.sem["sPE"], self.cnt["sPE"])
                                self.load_wchunks(g, wbig,
                                                  d["a1_Win"][:, cofs + q * 512: cofs + (q + 1) * 512],
                                                  DM, 512)
                                if q == 0:
                                    self.loadW(g, bin_sb[:], d["a1_bin"][:, cofs:cofs + DI])
                            blk.gpsimd(g_w)
                            w_c = self.cnt["sW"]
                            for mjl in range(4):
                                mj = q * 4 + mjl
                                for tc in range(4):
                                    it = mj * 4 + tc
                                    r = it % 2
                                    ac_free = self.ac_evac_hist[-2] if len(self.ac_evac_hist) >= 2 else 0

                                    def pe_b(t, mj=mj, mjl=mjl, tc=tc, r=r, w_c=w_c, cofs=cofs, ac_free=ac_free):
                                        t.wait_ge(self.sem["sW"], w_c)
                                        t.wait_ge(self.sem["sAC"], ac_free)
                                        for kj in range(ND):
                                            t.matmul(ps_rot[r][:],
                                                     wbig[:, kj * 512 + mjl * 128: kj * 512 + (mjl + 1) * 128],
                                                     xn[kj][:, tc * 512:(tc + 1) * 512],
                                                     start=(kj == 0), stop=False)
                                        inst = t.matmul(ps_rot[r][:],
                                                        bin_sb[0:1, mj * 128:(mj + 1) * 128],
                                                        self.ones[0:1, tc * 512:(tc + 1) * 512],
                                                        start=False, stop=True)
                                        self.mark("tensor", inst)
                                    blk.tensor(pe_b)
                                    peb = self.cnt["sPE"]

                                    if half == 0:
                                        sz_guard = (self.sz_spill_hist[-2]
                                                    if len(self.sz_spill_hist) >= 2 else 0)

                                        def ac_ev(a, r=r, it=it, peb=peb, sz_guard=sz_guard):
                                            a.wait_ge(self.sem["sPE"], peb)
                                            if sz_guard:
                                                a.wait_ge(self.sem["sX"], sz_guard)
                                            self.mark("scalar", a.activation(szt[it % 2][:],
                                                                             ps_rot[r][:], AF.Silu))
                                        blk.scalar(ac_ev)
                                        self.ac_evac_hist.append(self.cnt["sAC"])
                                        ac_done = self.cnt["sAC"]

                                        def s_sp(s, mj=mj, tc=tc, it=it, ac_done=ac_done):
                                            s.wait_ge(self.sem["sAC"], ac_done)
                                            self.loadX(s, d["sp_sz"][mj * 128:(mj + 1) * 128,
                                                                     tc * 512:(tc + 1) * 512],
                                                       szt[it % 2][:])
                                        blk.sync(s_sp)
                                        self.sz_spill_hist.append(self.cnt["sX"])
                                    else:
                                        def ac_ev(a, r=r, mj=mj, tc=tc, peb=peb):
                                            a.wait_ge(self.sem["sPE"], peb)
                                            self.mark("scalar", a.activation(
                                                uT[mj][:, tc * 512:(tc + 1) * 512],
                                                ps_rot[r][:], AF.Silu))
                                        blk.scalar(ac_ev)
                                        self.ac_evac_hist.append(self.cnt["sAC"])
                                if half == 1:
                                    ac_done = self.cnt["sAC"]

                                    def s_sp(s, mj=mj, ac_done=ac_done):
                                        s.wait_ge(self.sem["sAC"], ac_done)
                                        self.loadX(s, d["sp_u"][mj * 128:(mj + 1) * 128, :], uT[mj][:])
                                    blk.sync(s_sp)
                    self.barrier()

                wx = self.sb_local(ph0, "a1_wx", [128, NDI * 2 * DS], BF16)

                def g_wx(g):
                    self.load_wchunks(g, wx, d["a1_Wx"], DI, 2 * DS)
                blk.gpsimd(g_wx)
                wx_c = self.cnt["sW"]
                for tc in range(4):
                    dv_prev = self.cnt["sDV"]

                    def pe_bc(t, tc=tc, dv_prev=dv_prev):
                        t.wait_ge(self.sem["sW"], wx_c)
                        t.wait_ge(self.sem["sDV"], dv_prev)
                        for part, c0 in ((0, 0), (1, DS)):
                            for kj in range(NDI):
                                t.matmul(ps_rot[part][0:DS, :],
                                         wx[:, kj * 2 * DS + c0: kj * 2 * DS + c0 + DS],
                                         uT[kj][:, tc * 512:(tc + 1) * 512],
                                         start=(kj == 0), stop=False)
                            inst = t.matmul(ps_rot[part][0:DS, :],
                                            bx_sb[0:1, c0:c0 + DS],
                                            self.ones[0:1, tc * 512:(tc + 1) * 512],
                                            start=False, stop=True)
                        self.mark("tensor", inst)
                    blk.tensor(pe_bc)
                    pebc = self.cnt["sPE"]

                    def dv_bc(v, tc=tc, pebc=pebc):
                        v.wait_ge(self.sem["sPE"], pebc)
                        v.tensor_copy(BTb[0:DS, tc * 512:(tc + 1) * 512], ps_rot[0][0:DS, :])
                        inst = v.tensor_copy(CTb[0:DS, tc * 512:(tc + 1) * 512], ps_rot[1][0:DS, :])
                        self.mark("vector", inst)
                    blk.vector(dv_bc)

                bdt_sb = self.sb_local(ph0, "a1_bdtb", [1, DI], BF16)

                def g_bdt(g):
                    self.loadW(g, bdt_sb[:], d["a1_bdt"][:])
                blk.gpsimd(g_bdt)
                dtt = [self.sb_local(ph0, f"a1_dtt{i}", [128, T], BF16) for i in range(2)]
                dut = [self.sb_local(ph0, f"a1_dut{i}", [128, T], BF16) for i in range(2)]
                et = [self.sb_local(ph0, f"a1_et{i}", [128, 512], F32) for i in range(2)]
                for mh in range(4):
                    def g_wdt(g, mh=mh):
                        g.wait_ge(self.sem["sPE"], self.cnt["sPE"])
                        self.load_wchunks(g, wbig, d["a1_Wdt"][:, mh * 512:(mh + 1) * 512], DI, 512)
                    blk.gpsimd(g_wdt)
                    wdt_c = self.cnt["sW"]
                    for mjh in range(4):
                        mj = mh * 4 + mjh
                        pp = mj % 2
                        sp_prev = self.dt_spill_hist[-2] if len(self.dt_spill_hist) >= 2 else 0
                        for tc in range(4):
                            r = tc % 2
                            dv_free = self.dt_dv_hist[-2] if len(self.dt_dv_hist) >= 2 else 0

                            def pe_dt(t, mjh=mjh, mj=mj, tc=tc, r=r, wdt_c=wdt_c, dv_free=dv_free):
                                t.wait_ge(self.sem["sW"], wdt_c)
                                t.wait_ge(self.sem["sDV"], dv_free)
                                for kj in range(NDI):
                                    t.matmul(ps_rot[r][:],
                                             wbig[:, kj * 512 + mjh * 128: kj * 512 + (mjh + 1) * 128],
                                             uT[kj][:, tc * 512:(tc + 1) * 512],
                                             start=(kj == 0), stop=False)
                                inst = t.matmul(ps_rot[r][:],
                                                bdt_sb[0:1, mj * 128:(mj + 1) * 128],
                                                self.ones[0:1, tc * 512:(tc + 1) * 512],
                                                start=False, stop=True)
                                self.mark("tensor", inst)
                            blk.tensor(pe_dt)
                            pedt = self.cnt["sPE"]

                            def ac_sp(a, r=r, tc=tc, pp=pp, pedt=pedt, sp_prev=sp_prev):
                                a.wait_ge(self.sem["sPE"], pedt)
                                a.wait_ge(self.sem["sX"], sp_prev)
                                a.activation(et[tc % 2][:], ps_rot[r][:], AF.Exp)
                                self.mark("scalar", a.activation(dtt[pp][:, tc * 512:(tc + 1) * 512],
                                                                 et[tc % 2][:], AF.Ln, bias=1.0))
                            blk.scalar(ac_sp)
                            acdt = self.cnt["sAC"]

                            def dv_du(v, pp=pp, mj=mj, tc=tc, acdt=acdt):
                                v.wait_ge(self.sem["sAC"], acdt)
                                inst = v.tensor_tensor(dut[pp][:, tc * 512:(tc + 1) * 512],
                                                       dtt[pp][:, tc * 512:(tc + 1) * 512],
                                                       uT[mj][:, tc * 512:(tc + 1) * 512], ALU.mult)
                                self.mark("vector", inst)
                            blk.vector(dv_du)
                            self.dt_dv_hist.append(self.cnt["sDV"])
                        dv_mj = self.cnt["sDV"]

                        def s_spdt(s, mj=mj, pp=pp, dv_mj=dv_mj):
                            s.wait_ge(self.sem["sDV"], dv_mj)
                            self.loadX(s, d["sp_dt"][mj * 128:(mj + 1) * 128, :], dtt[pp][:])
                            self.loadX(s, d["sp_dtu"][mj * 128:(mj + 1) * 128, :], dut[pp][:])
                        blk.sync(s_spdt)
                        self.dt_spill_hist.append(self.cnt["sX"])
                self.barrier()


            bout_sb = self.sb_local(ph, "a1_boutb", [1, DM], BF16)

            def g_bout(g):
                self.loadW(g, bout_sb[:], d["a1_bout"][:])
            blk.gpsimd(g_bout)
            Bball = self.sb_local(ph, "a1_Bball", [128, DS * 512], BF16)
            Cball = self.sb_local(ph, "a1_Cball", [128, DS * 512], BF16)
            Xall = self.sb_local(ph, "a1_Xall", [128, DS * 512], BF16)
            hall = self.sb_local(ph, "a1_hall", [128, DS * 512], BF16)
            dA = [self.sb_local(ph, f"a1_dA{i}", [128, 512], BF16) for i in range(2)]
            ygT = [self.sb_local(ph, f"a1_yg{i}", [128, 512], BF16) for i in range(NDI)]
            sl_dt = [self.sb_local(ph, f"a1_sldt{i}", [128, 512], BF16) for i in range(2)]
            sl_du = [self.sb_local(ph, f"a1_sldu{i}", [128, 512], BF16) for i in range(2)]
            sl_u = [self.sb_local(ph, f"a1_slu{i}", [128, 512], BF16) for i in range(2)]
            sl_sz = [self.sb_local(ph, f"a1_slsz{i}", [128, 512], BF16) for i in range(2)]

            dv_slice_done = {}
            for tc in range(4):
                dv_bc_prev = self.cnt["sDV"]
                for s16 in range(DS):
                    for which, (ball, src) in enumerate(((Bball, BTb), (Cball, CTb))):
                        r = (s16 * 2 + which) % 2

                        def pe_bb(t, s16=s16, r=r, src=src, dv_bc_prev=dv_bc_prev):
                            t.wait_ge(self.sem["sDV"], dv_bc_prev if s16 < 1 else self.cnt_sDV_bb[-2])
                            self.mark("tensor", t.matmul(
                                ps_rot[r][:], self.sel16[:, s16 * 128:(s16 + 1) * 128],
                                src[0:DS, tc * 512:(tc + 1) * 512], start=True, stop=True))
                        blk.tensor(pe_bb)
                        pebb = self.cnt["sPE"]

                        def dv_bb(v, s16=s16, r=r, ball=ball, pebb=pebb):
                            v.wait_ge(self.sem["sPE"], pebb)
                            self.mark("vector", v.tensor_copy(ball[:, s16 * 512:(s16 + 1) * 512],
                                                              ps_rot[r][:]))
                        blk.vector(dv_bb)
                        self.cnt_sDV_bb.append(self.cnt["sDV"])

                for db in range(NDI):
                    pp = db % 2
                    prev = dv_slice_done.get((tc, db - 2), dv_slice_done.get((tc - 1, NDI - 2 + db), 0))

                    def s_sl(s, tc=tc, db=db, pp=pp, prev=prev):
                        s.wait_ge(self.sem["sDV"], prev)
                        csl = slice(tc * 512, (tc + 1) * 512)
                        self.loadX(s, sl_dt[pp][:], d["sp_dt"][db * 128:(db + 1) * 128, csl])
                        self.loadX(s, sl_du[pp][:], d["sp_dtu"][db * 128:(db + 1) * 128, csl])
                        self.loadX(s, sl_u[pp][:], d["sp_u"][db * 128:(db + 1) * 128, csl])
                        self.loadX(s, sl_sz[pp][:], d["sp_sz"][db * 128:(db + 1) * 128, csl])
                    blk.sync(s_sl)
                    x_sl = self.cnt["sX"]

                    dv_bb_done = self.cnt["sDV"]

                    def dv_x(v, pp=pp, x_sl=x_sl, dv_bb_done=dv_bb_done):
                        v.wait_ge(self.sem["sX"], x_sl)
                        rep = sl_du[pp][:].unsqueeze(1).broadcast_to([128, DS, 512])
                        inst = v.tensor_tensor(
                            Xall[:].rearrange("p (s t) -> p s t", s=DS), rep,
                            Bball[:].rearrange("p (s t) -> p s t", s=DS), ALU.mult)
                        self.mark("vector", inst)
                    blk.vector(dv_x)
                    dvx = self.cnt["sDV"]

                    for s16 in range(DS):
                        acw = self.ssm_dv_hist[-2] if len(self.ssm_dv_hist) >= 2 else 0

                        def ac_da(a, tc=tc, db=db, s16=s16, acw=acw, x_sl=x_sl, pp=pp):
                            a.wait_ge(self.sem["sX"], x_sl)
                            if acw:
                                a.wait_ge(self.sem["sDV"], acw)
                            inst = a.activation(dA[s16 % 2][:], sl_dt[pp][:], AF.Exp,
                                                scale=acol[:, db * DS + s16: db * DS + s16 + 1])
                            self.mark("scalar", inst)
                        blk.scalar(ac_da)
                        acda = self.cnt["sAC"]

                        def dv_scan(v, db=db, s16=s16, acda=acda, dvx=dvx):
                            v.wait_ge(self.sem["sAC"], acda)
                            inst = v.tensor_tensor_scan(
                                hall[:, s16 * 512:(s16 + 1) * 512], dA[s16 % 2][:],
                                Xall[:, s16 * 512:(s16 + 1) * 512],
                                carry[:, db * DS + s16: db * DS + s16 + 1],
                                ALU.mult, ALU.add)
                            self.mark("vector", inst)
                        blk.vector(dv_scan)
                        self.ssm_dv_hist.append(self.cnt["sDV"])

                    def dv_post(v, tc=tc, db=db, pp=pp):
                        v.tensor_copy(carry[:, db * DS:(db + 1) * DS],
                                      hall[:].rearrange("p (s t) -> p s t", s=DS)[:, :, 511])
                        v.tensor_tensor(Xall[:].rearrange("p (s t) -> p s t", s=DS),
                                        hall[:].rearrange("p (s t) -> p s t", s=DS),
                                        Cball[:].rearrange("p (s t) -> p s t", s=DS), ALU.mult)
                        v.tensor_tensor(Xall[:, 0:8 * 512], Xall[:, 0:8 * 512], Xall[:, 8 * 512:16 * 512], ALU.add)
                        v.tensor_tensor(Xall[:, 0:4 * 512], Xall[:, 0:4 * 512], Xall[:, 4 * 512:8 * 512], ALU.add)
                        v.tensor_tensor(Xall[:, 0:2 * 512], Xall[:, 0:2 * 512], Xall[:, 2 * 512:4 * 512], ALU.add)
                        v.tensor_tensor(Xall[:, 0:512], Xall[:, 0:512], Xall[:, 512:1024], ALU.add)
                        v.scalar_tensor_tensor(Xall[:, 0:512], sl_u[pp][:], dcol[:, db:db + 1],
                                               Xall[:, 0:512], ALU.mult, ALU.add)
                        inst = v.tensor_tensor(ygT[db][:], Xall[:, 0:512], sl_sz[pp][:], ALU.mult)
                        self.mark("vector", inst)
                    blk.vector(dv_post)
                    dv_slice_done[(tc, db)] = self.cnt["sDV"]

                dv_yg = self.cnt["sDV"]
                for mh in range(2):
                    def g_wo(g, mh=mh):
                        g.wait_ge(self.sem["sPE"], self.cnt["sPE"])
                        self.load_wchunks(g, wbig, d["a1_Wout"][:, mh * 512:(mh + 1) * 512], DI, 512)
                    blk.gpsimd(g_wo)
                    wo_c = self.cnt["sW"]
                    for ml in range(4):
                        m = mh * 4 + ml
                        r = m % 2
                        dvw = self.wout_dv_hist[-2] if len(self.wout_dv_hist) >= 2 else 0

                        def pe_wo(t, m=m, ml=ml, r=r, tc=tc, dv_yg=dv_yg, dvw=dvw, wo_c=wo_c):
                            t.wait_ge(self.sem["sW"], wo_c)
                            t.wait_ge(self.sem["sDV"], max(dv_yg, dvw))
                            for kj in range(NDI):
                                t.matmul(ps_rot[2 + r][:],
                                         wbig[:, kj * 512 + ml * 128: kj * 512 + (ml + 1) * 128],
                                         ygT[kj][:], start=(kj == 0), stop=False)
                            inst = t.matmul(ps_rot[2 + r][:],
                                            bout_sb[0:1, m * 128:(m + 1) * 128],
                                            self.ones[0:1, tc * 512:(tc + 1) * 512],
                                            start=False, stop=True)
                            self.mark("tensor", inst)
                        blk.tensor(pe_wo)
                        pewo = self.cnt["sPE"]

                        def dv_wo(v, m=m, r=r, tc=tc, pewo=pewo):
                            v.wait_ge(self.sem["sPE"], pewo)
                            inst = v.tensor_tensor(self.resid[m][:, tc * 512:(tc + 1) * 512],
                                                   self.resid[m][:, tc * 512:(tc + 1) * 512],
                                                   ps_rot[2 + r][:], ALU.add)
                            self.mark("vector", inst)
                        blk.vector(dv_wo)
                        self.wout_dv_hist.append(self.cnt["sDV"])
        self.barrier()

    def phase_l1f(self):
        nc, blk, d = self.nc, self.block, self.d
        with ExitStack() as ph:
            xn = [ph.enter_context(nc.sbuf_tensor(f"f1_xn{i}", [128, T], BF16)) for i in range(ND)]
            wr = ph.enter_context(nc.sbuf_tensor("f1_wr", [128, ND * NE], BF16))
            scores = ph.enter_context(nc.sbuf_tensor("f1_scores", [128, NT * NE], F32))
            top8 = ph.enter_context(nc.sbuf_tensor("f1_top8", [128, NT * NE], F32))
            dif = ph.enter_context(nc.sbuf_tensor("f1_dif", [128, NT], F32))
            w2c = ph.enter_context(nc.sbuf_tensor("f1_w2", [128, NT], F32))
            w1c = ph.enter_context(nc.sbuf_tensor("f1_w1", [128, NT], F32))
            eq2 = ph.enter_context(nc.sbuf_tensor("f1_eq2", [128, NT * NE], F32))
            msb = ph.enter_context(nc.sbuf_tensor("f1_m", [128, NT * NE], F32))
            mTb = ph.enter_context(nc.sbuf_tensor("f1_mTb", [NE, T], BF16))
            mb = [ph.enter_context(nc.sbuf_tensor(f"f1_mb{i}", [128, T], BF16)) for i in range(2)]
            upbr = ph.enter_context(nc.sbuf_tensor("f1_upbr", [NE * ND, 128], BF16))
            upbc = ph.enter_context(nc.sbuf_tensor("f1_upbc", [128, NE * ND], F32))
            dnb = ph.enter_context(nc.sbuf_tensor("f1_dnb", [NE, DM], BF16))
            shupb = ph.enter_context(nc.sbuf_tensor("f1_shupbb", [1, H1], BF16))
            shdnb = ph.enter_context(nc.sbuf_tensor("f1_shdnbb", [1, DM], BF16))
            wA = ph.enter_context(nc.sbuf_tensor("f1_wA", [128, ND * H1], BF16))
            wB = ph.enter_context(nc.sbuf_tensor("f1_wB", [128, ND * DM], BF16))
            hbuf = [ph.enter_context(nc.sbuf_tensor(f"f1_h{i}", [128, T], BF16)) for i in range(ND)]

            def g_pre(g):
                self.load_wchunks(g, wr, d["f1_gate"], DM, NE)
                self.loadW(g, dnb[:], d["f1_downb"][:])
                self.loadW(g, shupb[:], d["f1_shupb"][:])
                self.loadW(g, shdnb[:], d["f1_shdownb"][:])
                self.load_wchunks(g, wA, d["f1_shup"], DM, H1)
                self.load_wchunks(g, wB, d["f1_shdown"], H1, DM)
            blk.gpsimd(g_pre)
            w_pre = self.cnt["sW"]

            self.emit_ln(ph, xn, scratch=wA[:].bitcast(F32))
            ps_rot = [ph.enter_context(nc.psum_tensor(f"f1_rot{i}", [128, 512], F32)) for i in range(4)]
            ps_d = [ph.enter_context(nc.psum_tensor(f"f1_psd{i}", [128, 512], F32)) for i in range(4)]

            pe0, dv0 = self.cnt["sPE"], self.cnt["sDV"]
            for ti in range(NT):
                r = ti % 4

                def pe_s(t, ti=ti, r=r):
                    t.wait_ge(self.sem["sW"], w_pre)
                    t.wait_ge(self.sem["sDV"], dv0 + ti - 3 if ti >= 4 else dv0)
                    for kj in range(ND):
                        inst = t.matmul(ps_rot[r][:, 0:NE], xn[kj][:, ti * 128:(ti + 1) * 128],
                                        wr[:, kj * NE:(kj + 1) * NE],
                                        start=(kj == 0), stop=(kj == ND - 1))
                    self.mark("tensor", inst)
                blk.tensor(pe_s)
                pec = self.cnt["sPE"]

                def dv_s(v, ti=ti, r=r, pec=pec):
                    v.wait_ge(self.sem["sPE"], pec)
                    self.mark("vector", v.tensor_copy(scores[:, ti * NE:(ti + 1) * NE], ps_rot[r][:, 0:NE]))
                blk.vector(dv_s)

            def dv_g1(v):
                for ti in range(NT):
                    v.max(top8[:, ti * NE:(ti + 1) * NE], scores[:, ti * NE:(ti + 1) * NE])
                t8v = top8[:].rearrange("p (ti e) -> p ti e", ti=NT)
                inst = v.tensor_tensor(dif[:], t8v[:, :, 1], t8v[:, :, 0], ALU.subtract)
                self.mark("vector", inst)
            blk.vector(dv_g1)
            dvg1 = self.cnt["sDV"]

            def ac_sig(a):
                a.wait_ge(self.sem["sDV"], dvg1)
                self.mark("scalar", a.activation(w2c[:], dif[:], AF.Sigmoid))
            blk.scalar(ac_sig)
            acsig = self.cnt["sAC"]

            def dv_g1b(v):
                v.wait_ge(self.sem["sAC"], acsig)
                t8v = top8[:].rearrange("p (ti e) -> p ti e", ti=NT)
                self.mark("vector", v.tensor_tensor(dif[:], t8v[:, :, 0], t8v[:, :, 1], ALU.subtract))
            blk.vector(dv_g1b)
            dvg1b = self.cnt["sDV"]

            def ac_sig2(a):
                a.wait_ge(self.sem["sDV"], dvg1b)
                self.mark("scalar", a.activation(w1c[:], dif[:], AF.Sigmoid))
            blk.scalar(ac_sig2)
            acsig2 = self.cnt["sAC"]

            def dv_g2(v):
                v.wait_ge(self.sem["sAC"], acsig2)
                for ti in range(NT):
                    v.tensor_scalar(msb[:, ti * NE:(ti + 1) * NE], scores[:, ti * NE:(ti + 1) * NE],
                                    top8[:, ti * NE:ti * NE + 1], None, ALU.is_equal)
                for ti in range(NT):
                    v.tensor_scalar(eq2[:, ti * NE:(ti + 1) * NE], scores[:, ti * NE:(ti + 1) * NE],
                                    top8[:, ti * NE + 1:ti * NE + 2], None, ALU.is_equal)
                for ti in range(NT):
                    v.tensor_scalar(msb[:, ti * NE:(ti + 1) * NE], msb[:, ti * NE:(ti + 1) * NE],
                                    w1c[:, ti:ti + 1], None, ALU.mult)
                for ti in range(NT):
                    v.tensor_scalar(eq2[:, ti * NE:(ti + 1) * NE], eq2[:, ti * NE:(ti + 1) * NE],
                                    w2c[:, ti:ti + 1], None, ALU.mult)
                inst = v.tensor_tensor(msb[:], msb[:], eq2[:], ALU.add)
                self.mark("vector", inst)
            blk.vector(dv_g2)
            dvg2 = self.cnt["sDV"]

            for tq in range(4):
                def pe_mt(t, tq=tq):
                    t.wait_ge(self.sem["sDV"], dvg2 if tq == 0 else self.cnt["sDV"])
                    for j in range(4):
                        ti = tq * 4 + j
                        inst = t.transpose(ps_rot[0][0:NE, j * 128:(j + 1) * 128],
                                           msb[:, ti * NE:(ti + 1) * NE], self.identF[:])
                    self.mark("tensor", inst)
                blk.tensor(pe_mt)
                pemt = self.cnt["sPE"]

                def dv_mt(v, tq=tq, pemt=pemt):
                    v.wait_ge(self.sem["sPE"], pemt)
                    self.mark("vector", v.tensor_copy(mTb[0:NE, tq * 512:(tq + 1) * 512],
                                                      ps_rot[0][0:NE, 0:512]))
                blk.vector(dv_mt)

            def g_sh(g):
                g.wait_ge(self.sem["sDV"], self.cnt["sDV"])
                self.load_wchunks(g, wA, d["f1_shup"], DM, H1)
                self.load_wchunks(g, wB, d["f1_shdown"], H1, DM)
                self.loadW(g, upbr[:], d["f1_upb"][:])
            blk.gpsimd(g_sh)
            w_sh = self.cnt["sW"]

            ps_ub = ps_rot[0][:].bitcast(BF16)[:, 0:NE * ND]

            def pe_ub(t):
                t.wait_ge(self.sem["sW"], w_sh)
                self.mark("tensor", t.transpose(ps_ub, upbr[:],
                                                self.identB[0:NE * ND, 0:NE * ND]))
            blk.tensor(pe_ub)
            peub = self.cnt["sPE"]

            def dv_ub(v):
                v.wait_ge(self.sem["sPE"], peub)
                self.mark("vector", v.tensor_copy(upbc[:], ps_ub))
            blk.vector(dv_ub)

            ac_s1_hist = []
            for mj in range(ND):
                for tc in range(4):
                    it = mj * 4 + tc
                    r = it % 2
                    ac_free = ac_s1_hist[it - 2] if it >= 2 else 0

                    def pe_s1(t, mj=mj, tc=tc, r=r, ac_free=ac_free):
                        t.wait_ge(self.sem["sW"], w_sh)
                        if ac_free:
                            t.wait_ge(self.sem["sAC"], ac_free)
                        for kj in range(ND):
                            t.matmul(ps_rot[r][:],
                                     wA[:, kj * H1 + mj * 128: kj * H1 + (mj + 1) * 128],
                                     xn[kj][:, tc * 512:(tc + 1) * 512],
                                     start=(kj == 0), stop=False)
                        inst = t.matmul(ps_rot[r][:],
                                        shupb[0:1, mj * 128:(mj + 1) * 128],
                                        self.ones[0:1, tc * 512:(tc + 1) * 512],
                                        start=False, stop=True)
                        self.mark("tensor", inst)
                    blk.tensor(pe_s1)
                    pes1 = self.cnt["sPE"]

                    def ac_s1(a, mj=mj, tc=tc, r=r, pes1=pes1):
                        a.wait_ge(self.sem["sPE"], pes1)
                        self.mark("scalar", a.activation(hbuf[mj][:, tc * 512:(tc + 1) * 512],
                                                         ps_rot[r][:], AF.Silu))
                    blk.scalar(ac_s1)
                    ac_s1_hist.append(self.cnt["sAC"])
            ac_s1_done = self.cnt["sAC"]

            dv_s2_hist = []
            for m in range(ND):
                for tc in range(4):
                    it = m * 4 + tc
                    r = 2 + it % 2
                    dvf = dv_s2_hist[it - 2] if it >= 2 else 0

                    def pe_s2(t, m=m, tc=tc, r=r, dvf=dvf):
                        t.wait_ge(self.sem["sAC"], ac_s1_done)
                        if dvf:
                            t.wait_ge(self.sem["sDV"], dvf)
                        for kj in range(ND):
                            t.matmul(ps_rot[r][:],
                                     wB[:, kj * DM + m * 128: kj * DM + (m + 1) * 128],
                                     hbuf[kj][:, tc * 512:(tc + 1) * 512],
                                     start=(kj == 0), stop=False)
                        inst = t.matmul(ps_rot[r][:],
                                        shdnb[0:1, m * 128:(m + 1) * 128],
                                        self.ones[0:1, tc * 512:(tc + 1) * 512],
                                        start=False, stop=True)
                        self.mark("tensor", inst)
                    blk.tensor(pe_s2)
                    pes2 = self.cnt["sPE"]

                    def dv_s2(v, m=m, tc=tc, r=r, pes2=pes2):
                        v.wait_ge(self.sem["sPE"], pes2)
                        inst = v.tensor_tensor(self.resid[m][:, tc * 512:(tc + 1) * 512],
                                               self.resid[m][:, tc * 512:(tc + 1) * 512],
                                               ps_rot[r][:], ALU.add)
                        self.mark("vector", inst)
                    blk.vector(dv_s2)
                    dv_s2_hist.append(self.cnt["sDV"])

            exp_pe_done = 0
            self.dv_psd_free = self.cnt["sDV"]
            h_dv_hist = []
            for e in range(NE):
                pb = e % 2
                dv_mbw = self.cnt["sDV"]
                for tc in range(4):
                    def pe_mb(t, e=e, tc=tc, dv_mbw=dv_mbw):
                        t.wait_ge(self.sem["sDV"], dv_mbw)
                        self.mark("tensor", t.matmul(ps_rot[tc][:], self.sel8[:, e * 128:(e + 1) * 128],
                                                     mTb[0:NE, tc * 512:(tc + 1) * 512],
                                                     start=True, stop=True))
                    blk.tensor(pe_mb)
                    pemb = self.cnt["sPE"]

                    def dv_mb(v, tc=tc, pemb=pemb, pb=pb):
                        v.wait_ge(self.sem["sPE"], pemb)
                        self.mark("vector", v.tensor_copy(mb[pb][:, tc * 512:(tc + 1) * 512], ps_rot[tc][:]))
                    blk.vector(dv_mb)
                dv_mb_done = self.cnt["sDV"]

                def g_e(g, e=e, exp_pe_done=exp_pe_done):
                    g.wait_ge(self.sem["sPE"], self.cnt["sPE"])
                    self.load_wchunks(g, wA, d["f1_up"][e], DM, H1)
                    self.load_wchunks(g, wB, d["f1_down"][e], H1, DM)
                blk.gpsimd(g_e)
                w_e = self.cnt["sW"]

                for mj in range(ND):
                    for tc in range(4):
                        it = mj * 4 + tc
                        r = it % 2
                        dvf = h_dv_hist[-2] if len(h_dv_hist) >= 2 else 0

                        def pe_up(t, e=e, mj=mj, tc=tc, r=r, w_e=w_e, dvf=dvf):
                            t.wait_ge(self.sem["sW"], w_e)
                            t.wait_ge(self.sem["sDV"], dvf)
                            for kj in range(ND):
                                inst = t.matmul(ps_rot[r][:],
                                                wA[:, kj * H1 + mj * 128: kj * H1 + (mj + 1) * 128],
                                                xn[kj][:, tc * 512:(tc + 1) * 512],
                                                start=(kj == 0), stop=(kj == ND - 1))
                            self.mark("tensor", inst)
                        blk.tensor(pe_up)
                        peup = self.cnt["sPE"]

                        def ac_up(a, e=e, mj=mj, tc=tc, r=r, peup=peup):
                            a.wait_ge(self.sem["sPE"], peup)
                            self.mark("scalar", a.activation(hbuf[mj][:, tc * 512:(tc + 1) * 512],
                                                             ps_rot[r][:], AF.Silu,
                                                             bias=upbc[:, e * ND + mj: e * ND + mj + 1]))
                        blk.scalar(ac_up)
                        acup = self.cnt["sAC"]

                        def dv_hc(v, mj=mj, tc=tc, pb=pb, acup=acup, dv_mb_done=dv_mb_done):
                            v.wait_ge(self.sem["sAC"], acup)
                            v.wait_ge(self.sem["sDV"], dv_mb_done)
                            hs = hbuf[mj][:, tc * 512:(tc + 1) * 512]
                            inst = v.tensor_tensor(hs, hs, mb[pb][:, tc * 512:(tc + 1) * 512], ALU.mult)
                            self.mark("vector", inst)
                        blk.vector(dv_hc)
                        h_dv_hist.append(self.cnt["sDV"])
                dv_h_done = self.cnt["sDV"]

                for tc in range(4):
                    for mg in range(2):
                        dv_psd_free = self.dv_psd_free

                        def pe_dn(t, tc=tc, mg=mg, dv_h_done=dv_h_done, dv_psd_free=dv_psd_free, w_e=w_e):
                            t.wait_ge(self.sem["sW"], w_e)
                            t.wait_ge(self.sem["sDV"], max(dv_h_done, dv_psd_free))
                            for mi in range(4):
                                m = mg * 4 + mi
                                for kj in range(ND):
                                    inst = t.matmul(ps_d[mi][:],
                                                    wB[:, kj * DM + m * 128: kj * DM + (m + 1) * 128],
                                                    hbuf[kj][:, tc * 512:(tc + 1) * 512],
                                                    start=(kj == 0), stop=(kj == ND - 1))
                            self.mark("tensor", inst)
                        blk.tensor(pe_dn)
                        pedn = self.cnt["sPE"]

                        def dv_dn(v, tc=tc, mg=mg, pedn=pedn):
                            v.wait_ge(self.sem["sPE"], pedn)
                            for mi in range(4):
                                m = mg * 4 + mi
                                inst = v.tensor_tensor(self.resid[m][:, tc * 512:(tc + 1) * 512],
                                                       self.resid[m][:, tc * 512:(tc + 1) * 512],
                                                       ps_d[mi][:], ALU.add)
                            self.mark("vector", inst)
                        blk.vector(dv_dn)
                        self.dv_psd_free = self.cnt["sDV"]
                exp_pe_done = self.cnt["sPE"]

            for m in range(ND):
                for tc in range(4):
                    it = m * 4 + tc
                    r = it % 4
                    dvf = self.cnt["sDV"]

                    def pe_db(t, m=m, tc=tc, r=r, dvf=dvf):
                        t.wait_ge(self.sem["sDV"], dvf)
                        self.mark("tensor", t.matmul(ps_rot[r][:],
                                                     dnb[0:NE, m * 128:(m + 1) * 128],
                                                     mTb[0:NE, tc * 512:(tc + 1) * 512],
                                                     start=True, stop=True))
                    blk.tensor(pe_db)
                    pedb = self.cnt["sPE"]

                    def dv_db(v, m=m, tc=tc, r=r, pedb=pedb):
                        v.wait_ge(self.sem["sPE"], pedb)
                        inst = v.tensor_tensor(self.resid[m][:, tc * 512:(tc + 1) * 512],
                                               self.resid[m][:, tc * 512:(tc + 1) * 512],
                                               ps_rot[r][:], ALU.add)
                        self.mark("vector", inst)
                    blk.vector(dv_db)
        self.barrier()

    def phase_out(self):
        nc, blk, d = self.nc, self.block, self.d
        with ExitStack() as ph:
            ytok = [ph.enter_context(nc.sbuf_tensor(f"out_ytok{i}", [128, DM], F32)) for i in range(2)]
            pst = [ph.enter_context(nc.psum_tensor(f"out_ps{i}", [128, 512], F32)) for i in range(2)]
            pe0, dv0, x0 = self.cnt["sPE"], self.cnt["sDV"], self.cnt["sX"]

            def pe_body(t):
                for ti in range(NT):
                    for half in range(2):
                        if ti >= 1:
                            t.wait_ge(self.sem["sDV"], dv0 + 2 * (ti - 1) + half + 1)
                        for j in range(4):
                            dj = half * 4 + j
                            inst = t.transpose(pst[half][:, j * 128:(j + 1) * 128],
                                               self.resid[dj][:, ti * 128:(ti + 1) * 128],
                                               self.identF[:])
                        self.mark("tensor", inst)
            blk.tensor(pe_body)

            def dv_body(v):
                for ti in range(NT):
                    if ti >= 2:
                        v.wait_ge(self.sem["sX"], x0 + 16 * (ti - 1))
                    for half in range(2):
                        v.wait_ge(self.sem["sPE"], pe0 + 2 * ti + half + 1)
                        inst = v.tensor_copy(ytok[ti % 2][:, half * 512:(half + 1) * 512],
                                             pst[half][:, 0:512])
                        self.mark("vector", inst)
            blk.vector(dv_body)

            def s_body(s):
                for ti in range(NT):
                    s.wait_ge(self.sem["sDV"], dv0 + 2 * ti + 2)
                    self.loadX(s, d["y_out"][ti * 128:(ti + 1) * 128, :], ytok[ti % 2][:])
                s.wait_ge(self.sem["sX"], self.cnt["sX"])
            blk.sync(s_body)


def host_constants(blocks):
    c = {}
    cvals = [0.0, 1.0, -0.5, LN_EPS, RMS_EPS, SCALE, -SCALE, 1.0 / DM]
    c["cvals"] = np.broadcast_to(np.asarray(cvals, np.float32), (128, len(cvals))).copy()
    c["identF"] = np.eye(128, dtype=np.float32)
    c["onesrowF"] = np.ones((1, 128), np.float32)
    c["onesrow"] = np.ones((1, T), np.float32)
    c["identB"] = np.eye(128, dtype=BF16NP)
    if "l0a" in blocks:
        q = np.arange(128)
        m = np.where(q[None, :] <= q[:, None], 0.0, -1e9).astype(np.float32)
        c["maskT"] = m.T.copy().astype(BF16NP)
    if "l0f" in blocks or "l1f" in blocks:
        s8 = np.zeros((NE, NE * 128), BF16NP)
        for e in range(NE):
            s8[e, e * 128:(e + 1) * 128] = 1.0
        c["sel8"] = s8
    if "l1a" in blocks:
        s16 = np.zeros((DS, DS * 128), BF16NP)
        for s in range(DS):
            s16[s, s * 128:(s + 1) * 128] = 1.0
        c["sel16"] = s16
    return c


def host_weights(inputs, blocks):
    w = {}
    f32 = np.float32
    asc = np.ascontiguousarray
    if "l0a" in blocks:
        w["wq_lat"] = asc(inputs["L0_attn_wq_lat"], dtype=f32)
        w["wk_lat"] = asc(inputs["L0_attn_wk_lat"], dtype=f32)
        w["wv_lat"] = asc(inputs["L0_attn_wv_lat"], dtype=f32)
        w["wq_heads"] = asc(inputs["L0_attn_wq_heads"], dtype=f32)
        w["wk_heads"] = asc(inputs["L0_attn_wk_heads"], dtype=f32)
        w["wv_heads"] = asc(inputs["L0_attn_wv_heads"], dtype=f32)
        w["w_proj"] = asc(inputs["L0_attn_w_proj"], dtype=f32)
        w["b_proj"] = asc(inputs["L0_attn_b_proj"].reshape(1, DM), dtype=f32)
    if "l0f" in blocks:
        w["f0_scale"] = asc(inputs["L0_ffn_norm_scale"].reshape(ND, 128), dtype=f32)
        w["f0_router"] = asc(inputs["L0_ffn_w_router"].T, dtype=f32)
        w["f0_up"] = asc(inputs["L0_ffn_w_up"].transpose(0, 2, 1), dtype=f32)
        w["f0_down"] = asc(inputs["L0_ffn_w_down"].transpose(0, 2, 1), dtype=f32)
    if "l1a" in blocks:
        w["a1_Win"] = asc(inputs["L1_attn_W_in"], dtype=f32)
        w["a1_bin"] = asc(inputs["L1_attn_b_in"].reshape(1, -1), dtype=f32)
        w["a1_Wx"] = asc(inputs["L1_attn_W_x"], dtype=f32)
        w["a1_bx"] = asc(inputs["L1_attn_b_x"].reshape(1, -1), dtype=f32)
        w["a1_Wdt"] = asc(inputs["L1_attn_W_dt"], dtype=f32)
        w["a1_bdt"] = asc(inputs["L1_attn_b_dt"].reshape(1, -1), dtype=f32)
        w["a1_Wout"] = asc(inputs["L1_attn_W_out"], dtype=f32)
        w["a1_bout"] = asc(inputs["L1_attn_b_out"].reshape(1, -1), dtype=f32)
        w["a1_Alog"] = asc(inputs["L1_attn_A_log"], dtype=f32)
        w["a1_D"] = asc(inputs["L1_attn_D_param"].reshape(NDI, 128), dtype=f32)
    if "l1f" in blocks:
        w["f1_gate"] = asc(inputs["L1_ffn_gate_w"].T, dtype=f32)
        w["f1_up"] = asc(inputs["L1_ffn_up_w"], dtype=f32)
        w["f1_upb"] = asc(inputs["L1_ffn_up_b"].reshape(NE * ND, 128), dtype=f32)
        w["f1_down"] = asc(inputs["L1_ffn_down_w"], dtype=f32)
        w["f1_downb"] = asc(inputs["L1_ffn_down_b"], dtype=f32)
        w["f1_shup"] = asc(inputs["L1_ffn_sh_up_w"], dtype=f32)
        w["f1_shupb"] = asc(inputs["L1_ffn_sh_up_b"].reshape(1, -1), dtype=f32)
        w["f1_shdown"] = asc(inputs["L1_ffn_sh_down_w"], dtype=f32)
        w["f1_shdownb"] = asc(inputs["L1_ffn_sh_down_b"].reshape(1, -1), dtype=f32)
    return w


def run_blocks(x_full, inputs, blocks):
    nc = Builder(blocks).build()
    w = host_weights(inputs, blocks)
    c = host_constants(blocks)
    in_maps = []
    for b in range(8):
        m = {"x_in": np.ascontiguousarray(x_full[b], dtype=np.float32)}
        m.update(w)
        m.update(c)
        in_maps.append(m)
    res = run_bass_kernel_spmd(nc, in_maps, core_ids=list(range(8)))
    return np.stack([r["y_out"] for r in res.results], axis=0)


def kernel(**inputs):
    x = np.asarray(inputs["L0_attn_x"], dtype=np.float32)
    return run_blocks(x, inputs, list(ALL_BLOCKS))


# version 06
# speedup vs baseline: 1.2184x; 1.2184x over previous
import numpy as np
import ml_dtypes
from contextlib import ExitStack

import concourse.bass as bass
import concourse.mybir as mybir
from concourse.bass_utils import run_bass_kernel_spmd

F32 = mybir.dt.float32
BF16 = mybir.dt.bfloat16
AF = mybir.ActivationFunctionType
ALU = mybir.AluOpType
BF16NP = ml_dtypes.bfloat16

T = 2048
DM = 1024
NT = T // 128
ND = DM // 128
LAT = 256
NH = 16
HD = 64
SCALE = 0.125
DI = 2048
NDI = DI // 128
DS = 16
H1 = 1024
NE = 8
LN_EPS = 1e-5
RMS_EPS = 1e-6

ALL_BLOCKS = ("l0a", "l0f", "l1a", "l1f")


class Builder:
    def __init__(self, blocks, debug=False):
        self.debug = debug
        self.blocks = list(blocks)
        nc = bass.Bass("TRN2", target_bir_lowering=False, debug=False)
        self.nc = nc
        self.ctx = ExitStack()
        d = {}
        self.d = d

        def dram(name, shape, kind="ExternalInput", dtype=F32):
            d[name] = nc.dram_tensor(name, list(shape), dtype, kind=kind).ap()
        blocks = self.blocks
        dram("x_in", [T, DM])
        dram("y_out", [T, DM], kind="ExternalOutput")
        if "l0a" in blocks:
            for n, s in [("wq_lat", [DM, LAT]), ("wk_lat", [DM, LAT]), ("wv_lat", [DM, LAT]),
                         ("wq_heads", [LAT, DM]), ("wk_heads", [LAT, DM]), ("wv_heads", [LAT, DM]),
                         ("w_proj", [DM, DM]), ("b_proj", [1, DM])]:
                dram(n, s)
        if "l0f" in blocks:
            dram("f0_scale", [ND, 128])
            dram("f0_router", [DM, NE])
            dram("f0_up", [NE, DM, 2 * DI])
            dram("f0_down", [NE, DI, DM])
        if "l1a" in blocks:
            dram("a1_Win", [DM, 2 * DI])
            dram("a1_bin", [1, 2 * DI])
            dram("a1_Wx", [DI, 2 * DS])
            dram("a1_bx", [1, 2 * DS])
            dram("a1_Wdt", [DI, DI])
            dram("a1_bdt", [1, DI])
            dram("a1_Wout", [DI, DM])
            dram("a1_bout", [1, DM])
            dram("a1_Alog", [DI, DS])
            dram("a1_D", [NDI, 128])
            dram("sp_dt", [DI, T], kind="Internal", dtype=BF16)
            dram("sp_dtu", [DI, T], kind="Internal", dtype=BF16)
            dram("sp_u", [DI, T], kind="Internal", dtype=BF16)
            dram("sp_sz", [DI, T], kind="Internal", dtype=BF16)
        if "l1f" in blocks:
            dram("f1_gate", [DM, NE])
            dram("f1_up", [NE, DM, H1])
            dram("f1_upb", [NE * ND, 128])
            dram("f1_down", [NE, H1, DM])
            dram("f1_downb", [NE, DM])
            dram("f1_shup", [DM, H1])
            dram("f1_shupb", [1, H1])
            dram("f1_shdown", [H1, DM])
            dram("f1_shdownb", [1, DM])
        if debug:
            dram("dbg_a", [128, 2048], kind="ExternalOutput")
            dram("dbg_b", [128, 2048], kind="ExternalOutput")
            dram("dbg_xn", [DM, T], kind="ExternalOutput", dtype=BF16)
        dram("identF", [128, 128])
        dram("onesrow", [1, T])
        dram("onesrowF", [1, 128])
        dram("cvals", [128, 8])
        dram("identB", [128, 128], dtype=BF16)
        if "l0a" in blocks:
            dram("maskT", [128, 128], dtype=BF16)
        if "l0f" in blocks or "l1f" in blocks:
            dram("sel8", [NE, NE * 128], dtype=BF16)
        if "l1a" in blocks:
            dram("sel16", [DS, DS * 128], dtype=BF16)

        self.sem = {}
        self.cnt = {}
        for s in ("sW", "sX", "sPE", "sDV", "sAC", "sPO", "sBAR"):
            self.sem[s] = self.ctx.enter_context(nc.semaphore(s))
            self.cnt[s] = 0
        self.last_inst = {}
        self.eng_sem = {"tensor": "sPE", "vector": "sDV", "scalar": "sAC", "gpsimd": "sPO"}
        self.ac_evac_hist = []
        self.sz_spill_hist = []
        self.dt_spill_hist = []
        self.dt_dv_hist = []
        self.cnt_sDV_bb = []
        self.ssm_dv_hist = []
        self.wout_dv_hist = []

        sb = self.sb = lambda name, shape, dt: self.ctx.enter_context(
            nc.sbuf_tensor(name, list(shape), dt))
        self.cvals_list = [0.0, 1.0, -0.5, LN_EPS, RMS_EPS, SCALE, -SCALE, 1.0 / DM]
        self.cvals = sb("c_cvals", [128, len(self.cvals_list)], F32)
        for j, v in enumerate(self.cvals_list):
            nc.const_aps.register(F32, v, self.cvals[:, j:j + 1])
        self.resid = [sb(f"resid{i}", [128, T], F32) for i in range(ND)]
        self.identF = sb("c_identF", [128, 128], F32)
        self.ones = sb("c_ones", [1, T], BF16)
        self.onesrF = sb("c_onesrF", [1, 128], F32)
        self.identB = sb("c_identB", [128, 128], BF16)
        if "l0a" in blocks:
            self.maskT = sb("c_maskT", [128, 128], BF16)
        if "l0f" in blocks or "l1f" in blocks:
            self.sel8 = sb("c_sel8", [NE, NE * 128], BF16)
        if "l1a" in blocks:
            self.sel16 = sb("c_sel16", [DS, DS * 128], BF16)
        self.block = None

    def inc(self, inst, s, k=1):
        inst.then_inc(self.sem[s], k)
        self.cnt[s] += k
        return inst

    def note(self, eng, inst):
        self.last_inst[eng] = inst
        return inst

    def mark(self, eng, inst):
        self.note(eng, inst)
        s = self.eng_sem[eng]
        return self.inc(inst, s)

    def barrier(self):
        self.last_inst = {}
        tgts = {s: self.cnt[s] for s in ("sPE", "sDV", "sAC", "sPO", "sW", "sX")}
        for name in ("tensor", "vector", "scalar", "gpsimd", "sync"):
            def body(e, tgts=tgts):
                for s, v in tgts.items():
                    if v > 0:
                        e.wait_ge(self.sem[s], v)
            getattr(self.block, name)(body)

    def sb_local(self, stack, name, shape, dt):
        return stack.enter_context(self.nc.sbuf_tensor(name, list(shape), dt))

    def loadW(self, g, dst_ap, src_ap):
        return self.inc(g.dma_start(out=dst_ap, in_=src_ap), "sW", 16)

    def loadX(self, s, dst_ap, src_ap):
        return self.inc(s.dma_start(out=dst_ap, in_=src_ap), "sX", 16)

    def load_wchunks(self, g, dst_sb, src, K, M):
        nk = K // 128
        src_r = src.rearrange("(kj p) m -> p kj m", p=128)
        dst = dst_sb[:, 0:nk * M].rearrange("p (kj m) -> p kj m", kj=nk)
        return self.loadW(g, dst, src_r)

    def build(self):
        nc = self.nc
        self.block = self.ctx.enter_context(nc.Block())
        self.phase_consts()
        self.phase_in()
        for b in self.blocks:
            getattr(self, "phase_" + b)()
        self.phase_out()
        self.ctx.close()
        return nc

    def phase_consts(self):
        blk, d = self.block, self.d

        def s_body(s):
            self.loadX(s, self.cvals[:], d["cvals"][:])
            self.loadX(s, self.identF[:], d["identF"][:])
            self.loadX(s, self.onesrF[:], d["onesrowF"][:])
            self.loadX(s, self.identB[:], d["identB"][:])
            if "l0a" in self.blocks:
                self.loadX(s, self.maskT[:], d["maskT"][:])
            if "l0f" in self.blocks or "l1f" in self.blocks:
                self.loadX(s, self.sel8[:], d["sel8"][:])
            if "l1a" in self.blocks:
                self.loadX(s, self.sel16[:], d["sel16"][:])
        blk.sync(s_body)

        def g_body(g):
            self.loadW(g, self.ones[:], d["onesrow"][:])
        blk.gpsimd(g_body)
        self.barrier()

    def phase_in(self):
        nc, blk, d = self.nc, self.block, self.d
        with ExitStack() as ph:
            xtok = [ph.enter_context(nc.sbuf_tensor(f"in_xtok{i}", [128, DM], F32)) for i in range(2)]
            pst = [ph.enter_context(nc.psum_tensor(f"in_ps{i}", [128, 512], F32)) for i in range(2)]
            x0, pe0, dv0 = self.cnt["sX"], self.cnt["sPE"], self.cnt["sDV"]

            def s_body(s):
                for ti in range(NT):
                    if ti >= 2:
                        s.wait_ge(self.sem["sPE"], pe0 + 2 * ti - 2)
                    self.loadX(s, xtok[ti % 2][:], d["x_in"][ti * 128:(ti + 1) * 128, :])
            blk.sync(s_body)

            def pe_body(t):
                for ti in range(NT):
                    t.wait_ge(self.sem["sX"], x0 + 16 * (ti + 1))
                    for half in range(2):
                        if ti >= 1:
                            t.wait_ge(self.sem["sDV"], dv0 + 2 * (ti - 1) + half + 1)
                        for j in range(4):
                            dj = half * 4 + j
                            inst = t.transpose(pst[half][:, j * 128:(j + 1) * 128],
                                               xtok[ti % 2][:, dj * 128:(dj + 1) * 128],
                                               self.identF[:])
                        self.mark("tensor", inst)
            blk.tensor(pe_body)

            def dv_body(v):
                for ti in range(NT):
                    for half in range(2):
                        v.wait_ge(self.sem["sPE"], pe0 + 2 * ti + half + 1)
                        inst = None
                        for j in range(4):
                            dj = half * 4 + j
                            inst = v.tensor_copy(self.resid[dj][:, ti * 128:(ti + 1) * 128],
                                                 pst[half][:, j * 128:(j + 1) * 128])
                        self.mark("vector", inst)
            blk.vector(dv_body)
        self.barrier()

    def emit_ln(self, ph, xn, rms_scale_cols=None, skip_add=False, scratch=None):
        nc, blk = self.nc, self.block
        self.ln_id = getattr(self, "ln_id", 0) + 1
        u = self.ln_id
        with ExitStack() as ln:
            if scratch is None:
                sq = [ln.enter_context(nc.sbuf_tensor(f"ln{u}_sq{i}", [128, 512], F32))[:] for i in range(2)]
                tn = [ln.enter_context(nc.sbuf_tensor(f"ln{u}_tn{i}", [128, 512], F32))[:] for i in range(2)]
                mrow = ln.enter_context(nc.sbuf_tensor(f"ln{u}_m", [1, 512], F32))[:]
                vrow = ln.enter_context(nc.sbuf_tensor(f"ln{u}_v", [1, 512], F32))[:]
                rrow = ln.enter_context(nc.sbuf_tensor(f"ln{u}_r", [1, 512], F32))[:]
                mrrow = ln.enter_context(nc.sbuf_tensor(f"ln{u}_mr", [1, 512], F32))[:]
            else:
                sv = scratch
                sq = [sv[:, 0:512], sv[:, 512:1024]]
                tn = [sv[:, 1024:1536], sv[:, 1536:2048]]
                mrow = sv[0:1, 2048:2560]
                vrow = sv[0:1, 2560:3072]
                rrow = sv[0:1, 3072:3584]
                mrrow = sv[0:1, 3584:4096]
            onesc = self.cvals[:, 1:2]
            onesr = self.onesrF
            ps_s = ln.enter_context(nc.psum_tensor(f"ln{u}_ps_s", [128, 512], F32))
            ps_q = ln.enter_context(nc.psum_tensor(f"ln{u}_ps_q", [128, 512], F32))
            ps_b = [ln.enter_context(nc.psum_tensor(f"ln{u}_ps_b{i}", [128, 512], F32)) for i in range(2)]
            peLN0 = self.cnt["sPE"]

            for tc in range(4):
                sl = slice(tc * 512, (tc + 1) * 512)
                ac_base = self.cnt["sAC"]
                pe_tc0 = self.cnt["sPE"]

                def ac_sq(a, sl=sl, tc=tc, pe_tc0=pe_tc0):
                    for dj in range(ND):
                        if dj < 2:
                            if tc > 0:
                                a.wait_ge(self.sem["sPE"], pe_tc0)
                        else:
                            a.wait_ge(self.sem["sPE"], pe_tc0 + dj - 1)
                        self.mark("scalar", a.activation(sq[dj % 2], self.resid[dj][:, sl], AF.Square))
                blk.scalar(ac_sq)

                dv_prev_stats = self.cnt["sDV"]

                def pe_body(t, sl=sl, ac_base=ac_base, dv_prev_stats=dv_prev_stats):
                    t.wait_ge(self.sem["sDV"], dv_prev_stats)
                    for dj in range(ND):
                        t.matmul(ps_s[0:1, :], onesc, self.resid[dj][:, sl],
                                 start=(dj == 0), stop=(dj == ND - 1))
                    for dj in range(ND):
                        t.wait_ge(self.sem["sAC"], ac_base + dj + 1)
                        inst = t.matmul(ps_q[0:1, :], onesc, sq[dj % 2],
                                        start=(dj == 0), stop=(dj == ND - 1))
                        self.mark("tensor", inst)
                blk.tensor(pe_body)
                pe_done = self.cnt["sPE"]

                def dv_stats(v, sl=sl, pe_done=pe_done):
                    v.wait_ge(self.sem["sPE"], pe_done)
                    v.tensor_scalar_mul(mrow, ps_s[0:1, :], 1.0 / DM)
                    v.tensor_scalar_mul(vrow, ps_q[0:1, :], 1.0 / DM)
                    v.tensor_tensor(ps_s[0:1, :], mrow, mrow, ALU.mult)
                    self.mark("vector", v.tensor_tensor(vrow, vrow, ps_s[0:1, :], ALU.subtract))
                blk.vector(dv_stats)
                dv_done = self.cnt["sDV"]

                def ac_rstd(a, sl=sl, dv_done=dv_done):
                    a.wait_ge(self.sem["sDV"], dv_done)
                    a.activation(rrow, vrow, AF.Ln, bias=LN_EPS)
                    self.mark("scalar", a.activation(rrow, rrow, AF.Exp, scale=-0.5))
                blk.scalar(ac_rstd)
                ac_done = self.cnt["sAC"]

                def dv_mr(v, sl=sl, ac_done=ac_done):
                    v.wait_ge(self.sem["sAC"], ac_done)
                    self.mark("vector", v.tensor_tensor(mrrow, mrow, rrow, ALU.mult))
                blk.vector(dv_mr)
                dv_done2 = self.cnt["sDV"]

                def pe_bcast(t, sl=sl, dv_done2=dv_done2):
                    t.wait_ge(self.sem["sDV"], dv_done2)
                    t.matmul(ps_b[0][:], onesr[:], rrow, start=True, stop=True)
                    self.mark("tensor", t.matmul(ps_b[1][:], onesr[:], mrrow, start=True, stop=True))
                blk.tensor(pe_bcast)
                pe_done2 = self.cnt["sPE"]

                def dv_norm(v, sl=sl, pe_done2=pe_done2):
                    v.wait_ge(self.sem["sPE"], pe_done2)
                    for dj in range(ND):
                        v.tensor_tensor(tn[dj % 2], self.resid[dj][:, sl], ps_b[0][:], ALU.mult)
                        inst = v.tensor_tensor(xn[dj][:, sl], tn[dj % 2], ps_b[1][:], ALU.subtract)
                        self.mark("vector", inst)
                blk.vector(dv_norm)

            if skip_add:
                def dv_skip(v):
                    for dj in range(ND):
                        inst = v.tensor_tensor(self.resid[dj][:], self.resid[dj][:],
                                               xn[dj][:], ALU.add)
                    self.mark("vector", inst)
                blk.vector(dv_skip)

            if rms_scale_cols is not None:
                for tc in range(4):
                    sl = slice(tc * 512, (tc + 1) * 512)
                    dv_n = self.cnt["sDV"]
                    ac_base2 = self.cnt["sAC"]
                    pe_tc0 = self.cnt["sPE"]

                    def ac_sq2(a, sl=sl, dv_n=dv_n, tc=tc, pe_tc0=pe_tc0):
                        a.wait_ge(self.sem["sDV"], dv_n)
                        for dj in range(ND):
                            if dj < 2:
                                a.wait_ge(self.sem["sPE"], pe_tc0)
                            else:
                                a.wait_ge(self.sem["sPE"], pe_tc0 + dj - 1)
                            self.mark("scalar", a.activation(sq[dj % 2], xn[dj][:, sl], AF.Square))
                    blk.scalar(ac_sq2)

                    dv_cur = self.cnt["sDV"]

                    def pe_sum2(t, sl=sl, ac_base2=ac_base2, dv_cur=dv_cur):
                        t.wait_ge(self.sem["sDV"], dv_cur)
                        for dj in range(ND):
                            t.wait_ge(self.sem["sAC"], ac_base2 + dj + 1)
                            inst = t.matmul(ps_q[0:1, :], onesc, sq[dj % 2],
                                            start=(dj == 0), stop=(dj == ND - 1))
                            self.mark("tensor", inst)
                    blk.tensor(pe_sum2)
                    pe_d = self.cnt["sPE"]

                    def dv_ms(v, sl=sl, pe_d=pe_d):
                        v.wait_ge(self.sem["sPE"], pe_d)
                        self.mark("vector", v.tensor_scalar_mul(vrow, ps_q[0:1, :], 1.0 / DM))
                    blk.vector(dv_ms)
                    dv_d = self.cnt["sDV"]

                    def ac_r2(a, sl=sl, dv_d=dv_d):
                        a.wait_ge(self.sem["sDV"], dv_d)
                        a.activation(rrow, vrow, AF.Ln, bias=RMS_EPS)
                        self.mark("scalar", a.activation(rrow, rrow, AF.Exp, scale=-0.5))
                    blk.scalar(ac_r2)
                    ac_d = self.cnt["sAC"]

                    dv_cur2 = self.cnt["sDV"]

                    def pe_b2(t, sl=sl, ac_d=ac_d, dv_cur2=dv_cur2):
                        t.wait_ge(self.sem["sAC"], ac_d)
                        t.wait_ge(self.sem["sDV"], dv_cur2)
                        self.mark("tensor", t.matmul(ps_b[0][:], onesr[:], rrow, start=True, stop=True))
                    blk.tensor(pe_b2)
                    pe_d2 = self.cnt["sPE"]

                    def dv_apply(v, sl=sl, pe_d2=pe_d2):
                        v.wait_ge(self.sem["sPE"], pe_d2)
                        for dj in range(ND):
                            inst = v.scalar_tensor_tensor(
                                xn[dj][:, sl], xn[dj][:, sl], rms_scale_cols[:, dj:dj + 1],
                                ps_b[0][:], ALU.mult, ALU.mult)
                            self.mark("vector", inst)
                    blk.vector(dv_apply)
        self.barrier()

    def phase_l0a(self):
        nc, blk, d = self.nc, self.block, self.d
        NG = 4
        HPG = 4
        with ExitStack() as ph:
            xn = [ph.enter_context(nc.sbuf_tensor(f"a0_xn{i}", [128, T], BF16)) for i in range(ND)]
            wql = ph.enter_context(nc.sbuf_tensor("a0_wql", [128, ND * LAT], BF16))
            wkl = ph.enter_context(nc.sbuf_tensor("a0_wkl", [128, ND * LAT], BF16))
            wvl = ph.enter_context(nc.sbuf_tensor("a0_wvl", [128, ND * LAT], BF16))
            wqh = ph.enter_context(nc.sbuf_tensor("a0_wqh", [128, 2 * DM], BF16))
            wkh = ph.enter_context(nc.sbuf_tensor("a0_wkh", [128, 2 * DM], BF16))
            wvh = ph.enter_context(nc.sbuf_tensor("a0_wvh", [128, 2 * DM], BF16))
            wpr = ph.enter_context(nc.sbuf_tensor("a0_wpr", [128, 2 * DM], BF16))
            bpr = ph.enter_context(nc.sbuf_tensor("a0_bpr", [1, DM], BF16))

            def g_w(g):
                self.load_wchunks(g, wql, d["wq_lat"], DM, LAT)
                self.load_wchunks(g, wkl, d["wk_lat"], DM, LAT)
                self.load_wchunks(g, wvl, d["wv_lat"], DM, LAT)
                self.load_wchunks(g, wqh, d["wq_heads"], LAT, DM)
                self.load_wchunks(g, wkh, d["wk_heads"], LAT, DM)
                self.load_wchunks(g, wvh, d["wv_heads"], LAT, DM)
                self.loadW(g, bpr[:], d["b_proj"][:])
            blk.gpsimd(g_w)
            w_all = self.cnt["sW"]

            self.emit_ln(ph, xn)

            q1 = [ph.enter_context(nc.sbuf_tensor(f"a0_q1_{i}", [128, T], BF16)) for i in range(2)]
            k1 = [ph.enter_context(nc.sbuf_tensor(f"a0_k1_{i}", [128, T], BF16)) for i in range(2)]
            v1 = [ph.enter_context(nc.sbuf_tensor(f"a0_v1_{i}", [128, T], BF16)) for i in range(2)]
            qTg = [ph.enter_context(nc.sbuf_tensor(f"a0_qT{i}", [128, T], BF16)) for i in range(2)]
            kTg = [ph.enter_context(nc.sbuf_tensor(f"a0_kT{i}", [128, T], BF16)) for i in range(2)]
            vtokg = ph.enter_context(nc.sbuf_tensor("a0_vtok", [128, NT * 256], BF16))
            yTg = [ph.enter_context(nc.sbuf_tensor(f"a0_yT{i}", [128, T], BF16)) for i in range(2)]
            P = ph.enter_context(nc.sbuf_tensor("a0_P", [128, T], BF16))
            PT = ph.enter_context(nc.sbuf_tensor("a0_PT", [128, T], BF16))
            rmax = ph.enter_context(nc.sbuf_tensor("a0_rmax", [128, 1], F32))
            nbias = ph.enter_context(nc.sbuf_tensor("a0_nbias", [128, 1], F32))
            rsum = ph.enter_context(nc.sbuf_tensor("a0_rsum", [128, 1], F32))
            rrec = ph.enter_context(nc.sbuf_tensor("a0_rrec", [128, 1], F32))
            ps_rot = [ph.enter_context(nc.psum_tensor(f"a0_rot{i}", [128, 512], F32)) for i in range(2)]
            ps_S = ph.enter_context(nc.psum_tensor("a0_S", [128, T], F32))
            ps_pt = ph.enter_context(nc.psum_tensor("a0_pt", [128, 512], BF16))
            ps_y = ph.enter_context(nc.psum_tensor("a0_y", [64, 128], F32))

            def mm_evac_seq(jobs):
                pe0, dv0 = self.cnt["sPE"], self.cnt["sDV"]
                for i, (mm, ev) in enumerate(jobs):
                    r = i % 2

                    def pe_j(t, i=i, r=r, mm=mm):
                        t.wait_ge(self.sem["sDV"], dv0 + i - 1 if i >= 2 else dv0)
                        self.mark("tensor", mm(t, ps_rot[r]))
                    blk.tensor(pe_j)

                    def dv_j(v, i=i, r=r, ev=ev):
                        v.wait_ge(self.sem["sPE"], pe0 + i + 1)
                        self.mark("vector", ev(v, ps_rot[r]))
                    blk.vector(dv_j)

            jobs = []
            for (wsb, dst) in ((wql, q1), (wkl, k1), (wvl, v1)):
                for m2 in range(2):
                    for tc in range(4):
                        def mm(t, ps, wsb=wsb, m2=m2, tc=tc):
                            t.wait_ge(self.sem["sW"], w_all)
                            inst = None
                            for kj in range(ND):
                                inst = t.matmul(ps[:], wsb[:, kj * LAT + m2 * 128: kj * LAT + (m2 + 1) * 128],
                                                xn[kj][:, tc * 512:(tc + 1) * 512],
                                                start=(kj == 0), stop=(kj == ND - 1))
                            return inst

                        def ev(v, ps, dst=dst, m2=m2, tc=tc):
                            return v.tensor_copy(dst[m2][:, tc * 512:(tc + 1) * 512], ps[:])
                        jobs.append((mm, ev))
            mm_evac_seq(jobs)
            self.barrier()

            for g in range(NG):
                gofs = g * 256
                jobs = []
                for (wsb, dst) in ((wqh, qTg), (wkh, kTg)):
                    for mi in range(2):
                        for tc in range(4):
                            def mm(t, ps, wsb=wsb, mi=mi, tc=tc):
                                inst = None
                                for kj in range(2):
                                    inst = t.matmul(
                                        ps[:],
                                        wsb[:, kj * DM + gofs + mi * 128: kj * DM + gofs + (mi + 1) * 128],
                                        (q1 if wsb is wqh else k1)[kj][:, tc * 512:(tc + 1) * 512],
                                        start=(kj == 0), stop=(kj == 1))
                                return inst

                            def ev(v, ps, dst=dst, mi=mi, tc=tc):
                                return v.tensor_copy(dst[mi][:, tc * 512:(tc + 1) * 512], ps[:])
                            jobs.append((mm, ev))
                for ti2 in range(NT // 2):
                    def mm(t, ps, ti2=ti2):
                        inst = None
                        for half in range(2):
                            ti = ti2 * 2 + half
                            for kj in range(2):
                                inst = t.matmul(
                                    ps[:, half * 256:(half + 1) * 256],
                                    v1[kj][:, ti * 128:(ti + 1) * 128],
                                    wvh[:, kj * DM + gofs: kj * DM + gofs + 256],
                                    start=(kj == 0), stop=(kj == 1))
                        return inst

                    def ev(v, ps, ti2=ti2):
                        return v.tensor_copy(vtokg[:, ti2 * 512:(ti2 + 1) * 512], ps[:])
                    jobs.append((mm, ev))
                mm_evac_seq(jobs)

                for h in range(HPG):
                    qrow = (h % 2) * 64
                    mi = h // 2
                    acH0 = self.cnt["sAC"]
                    dvH0 = self.cnt["sDV"]
                    exp_jobs = []
                    dv_pred = dvH0
                    for qc in range(NT):
                        W = (qc + 1) * 128

                        def pe_S(t, qc=qc, mi=mi, qrow=qrow, acH0=acH0):
                            t.wait_ge(self.sem["sAC"], acH0 + qc)
                            wts = qTg[mi][qrow:qrow + 64, qc * 128:(qc + 1) * 128]
                            nch = (qc + 1) * 128
                            for c0 in range(0, nch, 512):
                                cw = min(512, nch - c0)
                                contains_diag = (c0 + cw == nch)
                                t.matmul(ps_S[:, c0:c0 + cw],
                                         wts, kTg[mi][qrow:qrow + 64, c0:c0 + cw],
                                         start=True, stop=not contains_diag)
                            inst = t.matmul(ps_S[:, qc * 128:(qc + 1) * 128],
                                            self.maskT[:], self.identB[:],
                                            start=False, stop=True)
                            self.mark("tensor", inst)
                        blk.tensor(pe_S)
                        peS = self.cnt["sPE"]

                        def dv_max(v, qc=qc, peS=peS):
                            v.wait_ge(self.sem["sPE"], peS)
                            v.reduce_max(rmax[:], ps_S[:, 0:(qc + 1) * 128], axis=mybir.AxisListType.X)
                            self.mark("vector", v.tensor_scalar_mul(nbias[:], rmax[:], -SCALE))
                        blk.vector(dv_max)
                        dv_pred += 1
                        exp_jobs.append((qc, dv_pred))
                        acE = acH0 + qc + 1
                        self.cnt["sAC"] += 1

                        def dv_norm(v, qc=qc, acE=acE):
                            v.wait_ge(self.sem["sAC"], acE)
                            v.reciprocal(rrec[:], rsum[:])
                            self.inc(v.tensor_copy(nbias[:], rrec[:]), "sDV")
                            v.wait_ge(self.sem["sDV"], self.cnt["sDV"])
                            inst = v.tensor_scalar_mul(P[:, 0:(qc + 1) * 128], P[:, 0:(qc + 1) * 128], rrec[:])
                            self.mark("vector", inst)
                        blk.vector(dv_norm)
                        dv_pred += 2
                        dvN = self.cnt["sDV"]

                        nblk = qc + 1
                        for jb in range(0, nblk, 4):
                            nj = min(4, nblk - jb)
                            dv_at = self.cnt["sDV"]

                            def pe_pt(t, jb=jb, nj=nj, dvN=dvN, dv_at=dv_at):
                                t.wait_ge(self.sem["sDV"], dvN if jb == 0 else dv_at)
                                for j in range(nj):
                                    inst = t.transpose(ps_pt[:, j * 128:(j + 1) * 128],
                                                       P[:, (jb + j) * 128:(jb + j + 1) * 128],
                                                       self.identB[:])
                                self.mark("tensor", inst)
                            blk.tensor(pe_pt)
                            pePT = self.cnt["sPE"]

                            def dv_pt(v, jb=jb, nj=nj, pePT=pePT):
                                v.wait_ge(self.sem["sPE"], pePT)
                                self.mark("vector", v.tensor_copy(PT[:, jb * 128:(jb + nj) * 128],
                                                                  ps_pt[:, 0:nj * 128]))
                            blk.vector(dv_pt)
                            dv_pred += 1

                        dvPT = self.cnt["sDV"]

                        def pe_pv(t, qc=qc, h=h, dvPT=dvPT):
                            t.wait_ge(self.sem["sDV"], dvPT)
                            for j in range(qc + 1):
                                inst = t.matmul(ps_y[:],
                                                vtokg[:, j * 256 + h * 64: j * 256 + (h + 1) * 64],
                                                PT[:, j * 128:(j + 1) * 128],
                                                start=(j == 0), stop=(j == qc))
                            self.mark("tensor", inst)
                        blk.tensor(pe_pv)
                        pePV = self.cnt["sPE"]

                        def dv_y(v, qc=qc, mi=mi, qrow=qrow, pePV=pePV):
                            v.wait_ge(self.sem["sPE"], pePV)
                            self.mark("vector", v.tensor_copy(
                                yTg[mi][qrow:qrow + 64, qc * 128:(qc + 1) * 128], ps_y[:]))
                        blk.vector(dv_y)
                        dv_pred += 1

                    def ac_expB(a_, jobs=tuple(exp_jobs)):
                        for qc, dvm in jobs:
                            a_.wait_ge(self.sem["sDV"], dvm)
                            inst = a_.activation(P[:, 0:(qc + 1) * 128], ps_S[:, 0:(qc + 1) * 128],
                                                 AF.Exp, scale=SCALE, bias=nbias[:],
                                                 accum_out=rsum[:])
                            inst.then_inc(self.sem["sAC"], 1)
                            self.note("scalar", inst)
                    blk.scalar(ac_expB)

                def g_pr(g_, g=g):
                    g_.wait_ge(self.sem["sPE"], self.cnt["sPE"])
                    self.load_wchunks(g_, wpr, d["w_proj"][g * 256:(g + 1) * 256, :], 256, DM)
                blk.gpsimd(g_pr)
                wpr_c = self.cnt["sW"]
                jobs = []
                for m in range(ND):
                    for tc in range(4):
                        def mm(t, ps, m=m, tc=tc, g=g, wpr_c=wpr_c):
                            t.wait_ge(self.sem["sW"], wpr_c)
                            for kj in range(2):
                                inst = t.matmul(ps[:],
                                                wpr[:, kj * DM + m * 128: kj * DM + (m + 1) * 128],
                                                yTg[kj][:, tc * 512:(tc + 1) * 512],
                                                start=(kj == 0), stop=(kj == 1 and g != 0))
                            if g == 0:
                                inst = t.matmul(ps[:], bpr[0:1, m * 128:(m + 1) * 128],
                                                self.ones[0:1, tc * 512:(tc + 1) * 512],
                                                start=False, stop=True)
                            return inst

                        def ev(v, ps, m=m, tc=tc):
                            return v.tensor_tensor(self.resid[m][:, tc * 512:(tc + 1) * 512],
                                                   self.resid[m][:, tc * 512:(tc + 1) * 512],
                                                   ps[:], ALU.add)
                        jobs.append((mm, ev))
                mm_evac_seq(jobs)
        self.barrier()

    def phase_l0f(self):
        nc, blk, d = self.nc, self.block, self.d
        with ExitStack() as ph:
            xn = [ph.enter_context(nc.sbuf_tensor(f"f0_xn{i}", [128, T], BF16)) for i in range(ND)]
            scale_rows = ph.enter_context(nc.sbuf_tensor("f0_sr", [ND, 128], F32))
            scale_cols = ph.enter_context(nc.sbuf_tensor("f0_sc", [128, ND], F32))
            wr = ph.enter_context(nc.sbuf_tensor("f0_wr", [128, ND * NE], BF16))
            scores = ph.enter_context(nc.sbuf_tensor("f0_scores", [128, NT * NE], F32))
            top8 = ph.enter_context(nc.sbuf_tensor("f0_top8", [128, NT * NE], F32))
            dif = ph.enter_context(nc.sbuf_tensor("f0_dif", [128, NT], F32))
            w2c = ph.enter_context(nc.sbuf_tensor("f0_w2", [128, NT], F32))
            w1c = ph.enter_context(nc.sbuf_tensor("f0_w1", [128, NT], F32))
            eq2 = ph.enter_context(nc.sbuf_tensor("f0_eq2", [128, NT * NE], F32))
            msb = ph.enter_context(nc.sbuf_tensor("f0_m", [128, NT * NE], F32))
            mTb = ph.enter_context(nc.sbuf_tensor("f0_mTb", [NE, T], BF16))
            mb = [ph.enter_context(nc.sbuf_tensor(f"f0_mb{i}", [128, T], BF16)) for i in range(2)]
            sg = [ph.enter_context(nc.sbuf_tensor(f"f0_sg{i}", [128, 512], BF16)) for i in range(2)]
            hsb = [ph.enter_context(nc.sbuf_tensor(f"f0_h{i}", [128, T], BF16)) for i in range(4)]
            wu = [ph.enter_context(nc.sbuf_tensor(f"f0_wu{i}", [128, ND * 512], BF16)) for i in range(2)]
            wg = [ph.enter_context(nc.sbuf_tensor(f"f0_wg{i}", [128, ND * 512], BF16)) for i in range(2)]
            wd = [ph.enter_context(nc.sbuf_tensor(f"f0_wd{i}", [128, 4 * DM], BF16)) for i in range(2)]

            def s_pre(s):
                self.loadX(s, scale_rows[:], d["f0_scale"][:])
            blk.sync(s_pre)
            x_sr = self.cnt["sX"]

            def g_pre(g):
                self.load_wchunks(g, wr, d["f0_router"], DM, NE)
            blk.gpsimd(g_pre)

            with nc.psum_tensor("f0_ps_sc", [128, 128], F32) as ps_sc:
                def pe_sc(t):
                    t.wait_ge(self.sem["sX"], x_sr)
                    self.mark("tensor", t.transpose(ps_sc[:, 0:ND], scale_rows[:],
                                                    self.identF[0:ND, 0:ND]))
                blk.tensor(pe_sc)
                pe_sc_c = self.cnt["sPE"]

                def dv_sc(v):
                    v.wait_ge(self.sem["sPE"], pe_sc_c)
                    self.mark("vector", v.tensor_copy(scale_cols[:], ps_sc[:, 0:ND]))
                blk.vector(dv_sc)
                self.barrier()

            self.emit_ln(ph, xn, rms_scale_cols=scale_cols, skip_add=True)
            ps_rot = [ph.enter_context(nc.psum_tensor(f"f0_rot{i}", [128, 512], F32)) for i in range(4)]
            ps_d = [ph.enter_context(nc.psum_tensor(f"f0_psd{i}", [128, 512], F32)) for i in range(4)]

            pe0, dv0 = self.cnt["sPE"], self.cnt["sDV"]
            for ti in range(NT):
                r = ti % 4

                def pe_s(t, ti=ti, r=r):
                    t.wait_ge(self.sem["sDV"], dv0 + ti - 3 if ti >= 4 else dv0)
                    for kj in range(ND):
                        inst = t.matmul(ps_rot[r][:, 0:NE], xn[kj][:, ti * 128:(ti + 1) * 128],
                                        wr[:, kj * NE:(kj + 1) * NE],
                                        start=(kj == 0), stop=(kj == ND - 1))
                    self.mark("tensor", inst)
                blk.tensor(pe_s)
                pec = self.cnt["sPE"]

                def dv_s(v, ti=ti, r=r, pec=pec):
                    v.wait_ge(self.sem["sPE"], pec)
                    self.mark("vector", v.tensor_copy(scores[:, ti * NE:(ti + 1) * NE], ps_rot[r][:, 0:NE]))
                blk.vector(dv_s)

            def dv_g1(v):
                for ti in range(NT):
                    v.max(top8[:, ti * NE:(ti + 1) * NE], scores[:, ti * NE:(ti + 1) * NE])
                t8v = top8[:].rearrange("p (ti e) -> p ti e", ti=NT)
                inst = v.tensor_tensor(dif[:], t8v[:, :, 1], t8v[:, :, 0], ALU.subtract)
                self.mark("vector", inst)
            blk.vector(dv_g1)
            dvg1 = self.cnt["sDV"]

            def ac_sig(a):
                a.wait_ge(self.sem["sDV"], dvg1)
                self.mark("scalar", a.activation(w2c[:], dif[:], AF.Sigmoid))
            blk.scalar(ac_sig)
            acsig = self.cnt["sAC"]

            def dv_g1b(v):
                v.wait_ge(self.sem["sAC"], acsig)
                t8v = top8[:].rearrange("p (ti e) -> p ti e", ti=NT)
                self.mark("vector", v.tensor_tensor(dif[:], t8v[:, :, 0], t8v[:, :, 1], ALU.subtract))
            blk.vector(dv_g1b)
            dvg1b = self.cnt["sDV"]

            def ac_sig2(a):
                a.wait_ge(self.sem["sDV"], dvg1b)
                self.mark("scalar", a.activation(w1c[:], dif[:], AF.Sigmoid))
            blk.scalar(ac_sig2)
            acsig2 = self.cnt["sAC"]

            def dv_g2(v):
                v.wait_ge(self.sem["sAC"], acsig2)
                for ti in range(NT):
                    v.tensor_scalar(msb[:, ti * NE:(ti + 1) * NE], scores[:, ti * NE:(ti + 1) * NE],
                                    top8[:, ti * NE:ti * NE + 1], None, ALU.is_equal)
                for ti in range(NT):
                    v.tensor_scalar(eq2[:, ti * NE:(ti + 1) * NE], scores[:, ti * NE:(ti + 1) * NE],
                                    top8[:, ti * NE + 1:ti * NE + 2], None, ALU.is_equal)
                for ti in range(NT):
                    v.tensor_scalar(msb[:, ti * NE:(ti + 1) * NE], msb[:, ti * NE:(ti + 1) * NE],
                                    w1c[:, ti:ti + 1], None, ALU.mult)
                for ti in range(NT):
                    v.tensor_scalar(eq2[:, ti * NE:(ti + 1) * NE], eq2[:, ti * NE:(ti + 1) * NE],
                                    w2c[:, ti:ti + 1], None, ALU.mult)
                inst = v.tensor_tensor(msb[:], msb[:], eq2[:], ALU.add)
                self.mark("vector", inst)
            blk.vector(dv_g2)
            dvg2 = self.cnt["sDV"]

            for tq in range(4):
                def pe_mt(t, tq=tq):
                    t.wait_ge(self.sem["sDV"], dvg2 if tq == 0 else self.cnt["sDV"])
                    for j in range(4):
                        ti = tq * 4 + j
                        inst = t.transpose(ps_rot[0][0:NE, j * 128:(j + 1) * 128],
                                           msb[:, ti * NE:(ti + 1) * NE], self.identF[:])
                    self.mark("tensor", inst)
                blk.tensor(pe_mt)
                pemt = self.cnt["sPE"]

                def dv_mt(v, tq=tq, pemt=pemt):
                    v.wait_ge(self.sem["sPE"], pemt)
                    self.mark("vector", v.tensor_copy(mTb[0:NE, tq * 512:(tq + 1) * 512],
                                                      ps_rot[0][0:NE, 0:512]))
                blk.vector(dv_mt)

            if self.debug:
                dvd = self.cnt["sDV"]

                def s_dbg(s):
                    s.wait_ge(self.sem["sDV"], dvd)
                    self.loadX(s, d["dbg_a"][:, 0:NT * NE], scores[:])
                    self.loadX(s, d["dbg_b"][:, 0:NT * NE], msb[:])
                    for dj in range(ND):
                        self.loadX(s, d["dbg_xn"][dj * 128:(dj + 1) * 128, :], xn[dj][:])
                blk.sync(s_dbg)

            slab_pe_done = {}
            slab_w = {}
            self.dv_psd_free = 0
            sg_hist = []

            def load_slab(i):
                e, j = divmod(i, 4)
                p = i % 2

                def g_l(g, e=e, j=j, p=p, i=i):
                    if i >= 2:
                        g.wait_ge(self.sem["sPE"], slab_pe_done[i - 2])
                    self.load_wchunks(g, wu[p], d["f0_up"][e][:, j * 512:(j + 1) * 512], DM, 512)
                    self.load_wchunks(g, wg[p], d["f0_up"][e][:, DI + j * 512:DI + (j + 1) * 512], DM, 512)
                    self.load_wchunks(g, wd[p], d["f0_down"][e][j * 512:(j + 1) * 512, :], 512, DM)
                blk.gpsimd(g_l)
                slab_w[i] = self.cnt["sW"]

            load_slab(0)
            load_slab(1)

            for e in range(NE):
                pb = e % 2
                dv_mbw = self.cnt["sDV"]
                for tc in range(4):
                    def pe_mb(t, e=e, tc=tc, dv_mbw=dv_mbw):
                        t.wait_ge(self.sem["sDV"], dv_mbw)
                        self.mark("tensor", t.matmul(ps_rot[tc][:], self.sel8[:, e * 128:(e + 1) * 128],
                                                     mTb[0:NE, tc * 512:(tc + 1) * 512],
                                                     start=True, stop=True))
                    blk.tensor(pe_mb)
                    pemb = self.cnt["sPE"]

                    def dv_mb(v, e=e, tc=tc, pemb=pemb, pb=pb):
                        v.wait_ge(self.sem["sPE"], pemb)
                        self.mark("vector", v.tensor_copy(mb[pb][:, tc * 512:(tc + 1) * 512], ps_rot[tc][:]))
                    blk.vector(dv_mb)

                dv_xre_c = self.cnt["sDV"]

                for j in range(4):
                    i = 4 * e + j
                    p = i % 2
                    pe0, ac0, dv0 = self.cnt["sPE"], self.cnt["sAC"], self.cnt["sDV"]

                    def pe_uB(t, p=p, i=i, pe0=pe0, dv0=dv0):
                        t.wait_ge(self.sem["sW"], slab_w[i])
                        for it in range(16):
                            mj, tc = divmod(it, 4)
                            rot = it % 2
                            t.wait_ge(self.sem["sDV"], dv0 + it - 1 if it >= 2 else dv0)
                            for kj in range(ND):
                                t.matmul(ps_rot[rot * 2][:],
                                         wu[p][:, kj * 512 + mj * 128: kj * 512 + (mj + 1) * 128],
                                         xn[kj][:, tc * 512:(tc + 1) * 512],
                                         start=(kj == 0), stop=(kj == ND - 1))
                            for kj in range(ND):
                                inst = t.matmul(ps_rot[rot * 2 + 1][:],
                                                wg[p][:, kj * 512 + mj * 128: kj * 512 + (mj + 1) * 128],
                                                xn[kj][:, tc * 512:(tc + 1) * 512],
                                                start=(kj == 0), stop=(kj == ND - 1))
                            self.mark("tensor", inst)
                    blk.tensor(pe_uB)

                    def ac_siluB(a, pe0=pe0, dv0=dv0):
                        for it in range(16):
                            rot = it % 2
                            a.wait_ge(self.sem["sPE"], pe0 + it + 1)
                            a.wait_ge(self.sem["sDV"], dv0 + it - 1 if it >= 2 else dv0)
                            self.mark("scalar", a.activation(sg[it % 2][:], ps_rot[rot * 2 + 1][:], AF.Silu))
                    blk.scalar(ac_siluB)

                    def dv_hB(v, ac0=ac0, pb=pb):
                        for it in range(16):
                            mj, tc = divmod(it, 4)
                            rot = it % 2
                            v.wait_ge(self.sem["sAC"], ac0 + it + 1)
                            hs = hsb[mj][:, tc * 512:(tc + 1) * 512]
                            v.tensor_tensor(hs, ps_rot[rot * 2][:], sg[it % 2][:], ALU.mult)
                            inst = v.tensor_tensor(hs, hs, mb[pb][:, tc * 512:(tc + 1) * 512], ALU.mult)
                            self.mark("vector", inst)
                    blk.vector(dv_hB)

                    peD0, dvD0 = self.cnt["sPE"], self.cnt["sDV"]

                    def pe_dB(t, p=p, peD0=peD0, dvD0=dvD0):
                        for idx in range(16):
                            tc, mg = divmod(idx, 4)
                            pr = 2 * (idx % 2)
                            t.wait_ge(self.sem["sDV"], dvD0 + idx - 1 if idx >= 2 else dvD0)
                            for mi in range(2):
                                m = mg * 2 + mi
                                for kj in range(4):
                                    inst = t.matmul(
                                        ps_d[pr + mi][:],
                                        wd[p][:, kj * DM + m * 128:kj * DM + (m + 1) * 128],
                                        hsb[kj][:, tc * 512:(tc + 1) * 512],
                                        start=(kj == 0), stop=(kj == 3))
                            self.mark("tensor", inst)
                    blk.tensor(pe_dB)

                    def dv_addB(v, peD0=peD0):
                        for idx in range(16):
                            tc, mg = divmod(idx, 4)
                            pr = 2 * (idx % 2)
                            v.wait_ge(self.sem["sPE"], peD0 + idx + 1)
                            for mi in range(2):
                                m = mg * 2 + mi
                                inst = v.tensor_tensor(
                                    self.resid[m][:, tc * 512:(tc + 1) * 512],
                                    self.resid[m][:, tc * 512:(tc + 1) * 512],
                                    ps_d[pr + mi][:], ALU.add)
                            self.mark("vector", inst)
                    blk.vector(dv_addB)

                    slab_pe_done[i] = self.cnt["sPE"]
                    if i + 2 < 4 * NE:
                        load_slab(i + 2)
        self.barrier()

    def phase_l1a(self):
        nc, blk, d = self.nc, self.block, self.d
        with ExitStack() as ph:
            wbig = ph.enter_context(nc.sbuf_tensor("a1_wbig", [128, 8 * 1024], BF16))
            BTb = ph.enter_context(nc.sbuf_tensor("a1_BTb", [DS, T], BF16))
            CTb = ph.enter_context(nc.sbuf_tensor("a1_CTb", [DS, T], BF16))
            bin_sb = ph.enter_context(nc.sbuf_tensor("a1_binb", [1, DI], BF16))
            bx_sb = ph.enter_context(nc.sbuf_tensor("a1_bxb", [1, 2 * DS], BF16))
            acol = ph.enter_context(nc.sbuf_tensor("a1_acol", [128, NDI * DS], F32))
            dcol = ph.enter_context(nc.sbuf_tensor("a1_dcol", [128, NDI], F32))
            carry = ph.enter_context(nc.sbuf_tensor("a1_carry", [128, NDI * DS], F32))
            ps_rot = [ph.enter_context(nc.psum_tensor(f"a1_rot{i}", [128, 512], F32)) for i in range(4)]

            def g_pre(g):
                self.loadW(g, bx_sb[:], d["a1_bx"][:])
            blk.gpsimd(g_pre)

            drows_cm = nc.sbuf_tensor("a1_drows", [NDI, 128], F32)
            drows = drows_cm.__enter__()

            def s_pre(s):
                self.loadX(s, drows[:], d["a1_D"][:])
                self.loadX(s, acol[:].rearrange("p (db s) -> p db s", db=NDI),
                           d["a1_Alog"][:].rearrange("(db p) s -> p db s", p=128))
            blk.sync(s_pre)
            x_pre = self.cnt["sX"]

            def ac_a(a):
                a.wait_ge(self.sem["sX"], x_pre)
                self.mark("scalar", a.activation(acol[:], acol[:], AF.Exp))
            blk.scalar(ac_a)
            ac_a_c = self.cnt["sAC"]

            def dv_a(v):
                v.wait_ge(self.sem["sAC"], ac_a_c)
                v.tensor_scalar_mul(acol[:], acol[:], -1.0)
                self.mark("vector", v.memset(carry[:], 0.0))
            blk.vector(dv_a)

            def pe_dc(t):
                t.wait_ge(self.sem["sX"], x_pre)
                self.mark("tensor", t.transpose(ps_rot[0][:, 0:NDI], drows[:],
                                                self.identF[0:NDI, 0:NDI]))
            blk.tensor(pe_dc)
            pedc = self.cnt["sPE"]

            def dv_dc(v):
                v.wait_ge(self.sem["sPE"], pedc)
                self.mark("vector", v.tensor_copy(dcol[:], ps_rot[0][:, 0:NDI]))
            blk.vector(dv_dc)
            self.barrier()
            drows_cm.__exit__(None, None, None)

            with ExitStack() as ph0:
                uT = [ph0.enter_context(nc.sbuf_tensor(f"a1_u{i}", [128, T], BF16)) for i in range(NDI)]
                szt = [ph0.enter_context(nc.sbuf_tensor(f"a1_szt{i}", [128, 512], BF16)) for i in range(2)]

                with ExitStack() as ph1:
                    xn = [ph1.enter_context(nc.sbuf_tensor(f"a1_xn{i}", [128, T], BF16)) for i in range(ND)]
                    self.emit_ln(ph1, xn, scratch=wbig[:].bitcast(F32))

                    for half, (cofs, spill) in enumerate([(DI, "sp_sz"), (0, "sp_u")]):
                        for q in range(4):
                            def g_w(g, cofs=cofs, q=q):
                                g.wait_ge(self.sem["sPE"], self.cnt["sPE"])
                                self.load_wchunks(g, wbig,
                                                  d["a1_Win"][:, cofs + q * 512: cofs + (q + 1) * 512],
                                                  DM, 512)
                                if q == 0:
                                    self.loadW(g, bin_sb[:], d["a1_bin"][:, cofs:cofs + DI])
                            blk.gpsimd(g_w)
                            w_c = self.cnt["sW"]
                            pe0, ac0, x0 = self.cnt["sPE"], self.cnt["sAC"], self.cnt["sX"]

                            def pe_bB(t, q=q, cofs=cofs, w_c=w_c, pe0=pe0, ac0=ac0):
                                t.wait_ge(self.sem["sW"], w_c)
                                for it in range(16):
                                    mjl, tc = divmod(it, 4)
                                    mj = q * 4 + mjl
                                    r = it % 2
                                    if it >= 2:
                                        t.wait_ge(self.sem["sAC"], ac0 + it - 1)
                                    for kj in range(ND):
                                        t.matmul(ps_rot[r][:],
                                                 wbig[:, kj * 512 + mjl * 128: kj * 512 + (mjl + 1) * 128],
                                                 xn[kj][:, tc * 512:(tc + 1) * 512],
                                                 start=(kj == 0), stop=False)
                                    inst = t.matmul(ps_rot[r][:],
                                                    bin_sb[0:1, mj * 128:(mj + 1) * 128],
                                                    self.ones[0:1, tc * 512:(tc + 1) * 512],
                                                    start=False, stop=True)
                                    self.mark("tensor", inst)
                            blk.tensor(pe_bB)

                            if half == 0:
                                def ac_evB(a_, q=q, pe0=pe0, x0=x0):
                                    for it in range(16):
                                        r = it % 2
                                        a_.wait_ge(self.sem["sPE"], pe0 + it + 1)
                                        a_.wait_ge(self.sem["sX"], x0 + 16 * (it - 1) if it >= 2 else x0)
                                        self.mark("scalar", a_.activation(szt[it % 2][:],
                                                                          ps_rot[r][:], AF.Silu))
                                blk.scalar(ac_evB)

                                def s_spB(s, q=q, ac0=ac0):
                                    for it in range(16):
                                        mjl, tc = divmod(it, 4)
                                        mj = q * 4 + mjl
                                        s.wait_ge(self.sem["sAC"], ac0 + it + 1)
                                        self.loadX(s, d["sp_sz"][mj * 128:(mj + 1) * 128,
                                                                 tc * 512:(tc + 1) * 512],
                                                   szt[it % 2][:])
                                blk.sync(s_spB)
                            else:
                                def ac_evB(a_, q=q, pe0=pe0):
                                    for it in range(16):
                                        mjl, tc = divmod(it, 4)
                                        mj = q * 4 + mjl
                                        r = it % 2
                                        a_.wait_ge(self.sem["sPE"], pe0 + it + 1)
                                        self.mark("scalar", a_.activation(
                                            uT[mj][:, tc * 512:(tc + 1) * 512],
                                            ps_rot[r][:], AF.Silu))
                                blk.scalar(ac_evB)
                                ac_q = self.cnt["sAC"]

                                def s_spB(s, q=q, ac_q=ac_q):
                                    for mjl in range(4):
                                        mj = q * 4 + mjl
                                        s.wait_ge(self.sem["sAC"], ac_q - (3 - mjl) * 4)
                                        self.loadX(s, d["sp_u"][mj * 128:(mj + 1) * 128, :], uT[mj][:])
                                blk.sync(s_spB)
                    self.barrier()

                wx = self.sb_local(ph0, "a1_wx", [128, NDI * 2 * DS], BF16)

                def g_wx(g):
                    self.load_wchunks(g, wx, d["a1_Wx"], DI, 2 * DS)
                blk.gpsimd(g_wx)
                wx_c = self.cnt["sW"]
                for tc in range(4):
                    dv_prev = self.cnt["sDV"]

                    def pe_bc(t, tc=tc, dv_prev=dv_prev):
                        t.wait_ge(self.sem["sW"], wx_c)
                        t.wait_ge(self.sem["sDV"], dv_prev)
                        for part, c0 in ((0, 0), (1, DS)):
                            for kj in range(NDI):
                                t.matmul(ps_rot[part][0:DS, :],
                                         wx[:, kj * 2 * DS + c0: kj * 2 * DS + c0 + DS],
                                         uT[kj][:, tc * 512:(tc + 1) * 512],
                                         start=(kj == 0), stop=False)
                            inst = t.matmul(ps_rot[part][0:DS, :],
                                            bx_sb[0:1, c0:c0 + DS],
                                            self.ones[0:1, tc * 512:(tc + 1) * 512],
                                            start=False, stop=True)
                        self.mark("tensor", inst)
                    blk.tensor(pe_bc)
                    pebc = self.cnt["sPE"]

                    def dv_bc(v, tc=tc, pebc=pebc):
                        v.wait_ge(self.sem["sPE"], pebc)
                        v.tensor_copy(BTb[0:DS, tc * 512:(tc + 1) * 512], ps_rot[0][0:DS, :])
                        inst = v.tensor_copy(CTb[0:DS, tc * 512:(tc + 1) * 512], ps_rot[1][0:DS, :])
                        self.mark("vector", inst)
                    blk.vector(dv_bc)

                bdt_sb = self.sb_local(ph0, "a1_bdtb", [1, DI], BF16)

                def g_bdt(g):
                    self.loadW(g, bdt_sb[:], d["a1_bdt"][:])
                blk.gpsimd(g_bdt)
                dtt = [self.sb_local(ph0, f"a1_dtt{i}", [128, T], BF16) for i in range(2)]
                dut = [self.sb_local(ph0, f"a1_dut{i}", [128, T], BF16) for i in range(2)]
                et = [self.sb_local(ph0, f"a1_et{i}", [128, 512], F32) for i in range(2)]
                for mh in range(4):
                    def g_wdt(g, mh=mh):
                        g.wait_ge(self.sem["sPE"], self.cnt["sPE"])
                        self.load_wchunks(g, wbig, d["a1_Wdt"][:, mh * 512:(mh + 1) * 512], DI, 512)
                    blk.gpsimd(g_wdt)
                    wdt_c = self.cnt["sW"]
                    peE0, acE0, dvE0, xE0 = (self.cnt["sPE"], self.cnt["sAC"],
                                             self.cnt["sDV"], self.cnt["sX"])

                    def pe_dtB(t, mh=mh, wdt_c=wdt_c, dvE0=dvE0):
                        t.wait_ge(self.sem["sW"], wdt_c)
                        for it in range(16):
                            mjh, tc = divmod(it, 4)
                            mj = mh * 4 + mjh
                            r = tc % 2
                            t.wait_ge(self.sem["sDV"], dvE0 + it - 1 if it >= 2 else dvE0)
                            for kj in range(NDI):
                                t.matmul(ps_rot[r][:],
                                         wbig[:, kj * 512 + mjh * 128: kj * 512 + (mjh + 1) * 128],
                                         uT[kj][:, tc * 512:(tc + 1) * 512],
                                         start=(kj == 0), stop=False)
                            inst = t.matmul(ps_rot[r][:],
                                            bdt_sb[0:1, mj * 128:(mj + 1) * 128],
                                            self.ones[0:1, tc * 512:(tc + 1) * 512],
                                            start=False, stop=True)
                            self.mark("tensor", inst)
                    blk.tensor(pe_dtB)

                    def ac_spB(a_, mh=mh, peE0=peE0, xE0=xE0, dvE0=dvE0):
                        for it in range(16):
                            mjh, tc = divmod(it, 4)
                            pp = (mh * 4 + mjh) % 2
                            r = tc % 2
                            a_.wait_ge(self.sem["sPE"], peE0 + it + 1)
                            if mjh >= 2 and tc == 0:
                                a_.wait_ge(self.sem["sX"], xE0 + 32 * (mjh - 1))
                            elif mjh < 2 and tc == 0:
                                a_.wait_ge(self.sem["sX"], xE0)
                            a_.activation(et[it % 2][:], ps_rot[r][:], AF.Exp)
                            self.mark("scalar", a_.activation(dtt[pp][:, tc * 512:(tc + 1) * 512],
                                                              et[it % 2][:], AF.Ln, bias=1.0))
                    blk.scalar(ac_spB)

                    def dv_duB(v, mh=mh, acE0=acE0):
                        for it in range(16):
                            mjh, tc = divmod(it, 4)
                            mj = mh * 4 + mjh
                            pp = mj % 2
                            v.wait_ge(self.sem["sAC"], acE0 + it + 1)
                            inst = v.tensor_tensor(dut[pp][:, tc * 512:(tc + 1) * 512],
                                                   dtt[pp][:, tc * 512:(tc + 1) * 512],
                                                   uT[mj][:, tc * 512:(tc + 1) * 512], ALU.mult)
                            self.mark("vector", inst)
                    blk.vector(dv_duB)

                    def s_spdtB(s, mh=mh, dvE0=dvE0):
                        for mjh in range(4):
                            mj = mh * 4 + mjh
                            pp = mj % 2
                            s.wait_ge(self.sem["sDV"], dvE0 + (mjh + 1) * 4)
                            self.loadX(s, d["sp_dt"][mj * 128:(mj + 1) * 128, :], dtt[pp][:])
                            self.loadX(s, d["sp_dtu"][mj * 128:(mj + 1) * 128, :], dut[pp][:])
                    blk.sync(s_spdtB)
                self.barrier()


            bout_sb = self.sb_local(ph, "a1_boutb", [1, DM], BF16)

            def g_bout(g):
                self.loadW(g, bout_sb[:], d["a1_bout"][:])
            blk.gpsimd(g_bout)
            Bball = self.sb_local(ph, "a1_Bball", [128, DS * 512], BF16)
            Cball = self.sb_local(ph, "a1_Cball", [128, DS * 512], BF16)
            Xall = self.sb_local(ph, "a1_Xall", [128, DS * 512], BF16)
            hall = self.sb_local(ph, "a1_hall", [128, DS * 512], BF16)
            dA = [self.sb_local(ph, f"a1_dA{i}", [128, 512], BF16) for i in range(2)]
            ygT = [self.sb_local(ph, f"a1_yg{i}", [128, 512], BF16) for i in range(NDI)]
            sl_dt = [self.sb_local(ph, f"a1_sldt{i}", [128, 512], BF16) for i in range(2)]
            sl_du = [self.sb_local(ph, f"a1_sldu{i}", [128, 512], BF16) for i in range(2)]
            sl_u = [self.sb_local(ph, f"a1_slu{i}", [128, 512], BF16) for i in range(2)]
            sl_sz = [self.sb_local(ph, f"a1_slsz{i}", [128, 512], BF16) for i in range(2)]

            dv_slice_done = {}
            for tc in range(4):
                dv_bc_prev = self.cnt["sDV"]
                for s16 in range(DS):
                    for which, (ball, src) in enumerate(((Bball, BTb), (Cball, CTb))):
                        r = (s16 * 2 + which) % 2

                        def pe_bb(t, s16=s16, r=r, src=src, dv_bc_prev=dv_bc_prev):
                            t.wait_ge(self.sem["sDV"], dv_bc_prev if s16 < 1 else self.cnt_sDV_bb[-2])
                            self.mark("tensor", t.matmul(
                                ps_rot[r][:], self.sel16[:, s16 * 128:(s16 + 1) * 128],
                                src[0:DS, tc * 512:(tc + 1) * 512], start=True, stop=True))
                        blk.tensor(pe_bb)
                        pebb = self.cnt["sPE"]

                        def dv_bb(v, s16=s16, r=r, ball=ball, pebb=pebb):
                            v.wait_ge(self.sem["sPE"], pebb)
                            self.mark("vector", v.tensor_copy(ball[:, s16 * 512:(s16 + 1) * 512],
                                                              ps_rot[r][:]))
                        blk.vector(dv_bb)
                        self.cnt_sDV_bb.append(self.cnt["sDV"])

                for db in range(NDI):
                    pp = db % 2
                    prev = dv_slice_done.get((tc, db - 2), dv_slice_done.get((tc - 1, NDI - 2 + db), 0))

                    def s_sl(s, tc=tc, db=db, pp=pp, prev=prev):
                        s.wait_ge(self.sem["sDV"], prev)
                        csl = slice(tc * 512, (tc + 1) * 512)
                        self.loadX(s, sl_dt[pp][:], d["sp_dt"][db * 128:(db + 1) * 128, csl])
                        self.loadX(s, sl_du[pp][:], d["sp_dtu"][db * 128:(db + 1) * 128, csl])
                        self.loadX(s, sl_u[pp][:], d["sp_u"][db * 128:(db + 1) * 128, csl])
                        self.loadX(s, sl_sz[pp][:], d["sp_sz"][db * 128:(db + 1) * 128, csl])
                    blk.sync(s_sl)
                    x_sl = self.cnt["sX"]

                    dvDB0, acDB0 = self.cnt["sDV"], self.cnt["sAC"]

                    def ac_daB(a_, tc=tc, db=db, pp=pp, x_sl=x_sl, dvDB0=dvDB0):
                        a_.wait_ge(self.sem["sX"], x_sl)
                        for s16 in range(DS):
                            a_.wait_ge(self.sem["sDV"], dvDB0 + s16 if s16 >= 2 else dvDB0)
                            inst = a_.activation(dA[s16 % 2][:], sl_dt[pp][:], AF.Exp,
                                                 scale=acol[:, db * DS + s16: db * DS + s16 + 1])
                            self.mark("scalar", inst)
                    blk.scalar(ac_daB)

                    def dv_dbB(v, tc=tc, db=db, pp=pp, x_sl=x_sl, acDB0=acDB0):
                        v.wait_ge(self.sem["sX"], x_sl)
                        rep = sl_du[pp][:].unsqueeze(1).broadcast_to([128, DS, 512])
                        inst = v.tensor_tensor(
                            Xall[:].rearrange("p (s t) -> p s t", s=DS), rep,
                            Bball[:].rearrange("p (s t) -> p s t", s=DS), ALU.mult)
                        self.mark("vector", inst)
                        for s16 in range(DS):
                            v.wait_ge(self.sem["sAC"], acDB0 + s16 + 1)
                            inst = v.tensor_tensor_scan(
                                hall[:, s16 * 512:(s16 + 1) * 512], dA[s16 % 2][:],
                                Xall[:, s16 * 512:(s16 + 1) * 512],
                                carry[:, db * DS + s16: db * DS + s16 + 1],
                                ALU.mult, ALU.add)
                            self.mark("vector", inst)
                        v.tensor_copy(carry[:, db * DS:(db + 1) * DS],
                                      hall[:].rearrange("p (s t) -> p s t", s=DS)[:, :, 511])
                        v.tensor_tensor(Xall[:].rearrange("p (s t) -> p s t", s=DS),
                                        hall[:].rearrange("p (s t) -> p s t", s=DS),
                                        Cball[:].rearrange("p (s t) -> p s t", s=DS), ALU.mult)
                        v.tensor_tensor(Xall[:, 0:8 * 512], Xall[:, 0:8 * 512], Xall[:, 8 * 512:16 * 512], ALU.add)
                        v.tensor_tensor(Xall[:, 0:4 * 512], Xall[:, 0:4 * 512], Xall[:, 4 * 512:8 * 512], ALU.add)
                        v.tensor_tensor(Xall[:, 0:2 * 512], Xall[:, 0:2 * 512], Xall[:, 2 * 512:4 * 512], ALU.add)
                        v.tensor_tensor(Xall[:, 0:512], Xall[:, 0:512], Xall[:, 512:1024], ALU.add)
                        v.scalar_tensor_tensor(Xall[:, 0:512], sl_u[pp][:], dcol[:, db:db + 1],
                                               Xall[:, 0:512], ALU.mult, ALU.add)
                        inst = v.tensor_tensor(ygT[db][:], Xall[:, 0:512], sl_sz[pp][:], ALU.mult)
                        self.mark("vector", inst)
                    blk.vector(dv_dbB)
                    dv_slice_done[(tc, db)] = self.cnt["sDV"]

                dv_yg = self.cnt["sDV"]
                for mh in range(2):
                    def g_wo(g, mh=mh):
                        g.wait_ge(self.sem["sPE"], self.cnt["sPE"])
                        self.load_wchunks(g, wbig, d["a1_Wout"][:, mh * 512:(mh + 1) * 512], DI, 512)
                    blk.gpsimd(g_wo)
                    wo_c = self.cnt["sW"]
                    for ml in range(4):
                        m = mh * 4 + ml
                        r = m % 2
                        dvw = self.wout_dv_hist[-2] if len(self.wout_dv_hist) >= 2 else 0

                        def pe_wo(t, m=m, ml=ml, r=r, tc=tc, dv_yg=dv_yg, dvw=dvw, wo_c=wo_c):
                            t.wait_ge(self.sem["sW"], wo_c)
                            t.wait_ge(self.sem["sDV"], max(dv_yg, dvw))
                            for kj in range(NDI):
                                t.matmul(ps_rot[2 + r][:],
                                         wbig[:, kj * 512 + ml * 128: kj * 512 + (ml + 1) * 128],
                                         ygT[kj][:], start=(kj == 0), stop=False)
                            inst = t.matmul(ps_rot[2 + r][:],
                                            bout_sb[0:1, m * 128:(m + 1) * 128],
                                            self.ones[0:1, tc * 512:(tc + 1) * 512],
                                            start=False, stop=True)
                            self.mark("tensor", inst)
                        blk.tensor(pe_wo)
                        pewo = self.cnt["sPE"]

                        def dv_wo(v, m=m, r=r, tc=tc, pewo=pewo):
                            v.wait_ge(self.sem["sPE"], pewo)
                            inst = v.tensor_tensor(self.resid[m][:, tc * 512:(tc + 1) * 512],
                                                   self.resid[m][:, tc * 512:(tc + 1) * 512],
                                                   ps_rot[2 + r][:], ALU.add)
                            self.mark("vector", inst)
                        blk.vector(dv_wo)
                        self.wout_dv_hist.append(self.cnt["sDV"])
        self.barrier()

    def phase_l1f(self):
        nc, blk, d = self.nc, self.block, self.d
        with ExitStack() as ph:
            xn = [ph.enter_context(nc.sbuf_tensor(f"f1_xn{i}", [128, T], BF16)) for i in range(ND)]
            wr = ph.enter_context(nc.sbuf_tensor("f1_wr", [128, ND * NE], BF16))
            scores = ph.enter_context(nc.sbuf_tensor("f1_scores", [128, NT * NE], F32))
            top8 = ph.enter_context(nc.sbuf_tensor("f1_top8", [128, NT * NE], F32))
            dif = ph.enter_context(nc.sbuf_tensor("f1_dif", [128, NT], F32))
            w2c = ph.enter_context(nc.sbuf_tensor("f1_w2", [128, NT], F32))
            w1c = ph.enter_context(nc.sbuf_tensor("f1_w1", [128, NT], F32))
            eq2 = ph.enter_context(nc.sbuf_tensor("f1_eq2", [128, NT * NE], F32))
            msb = ph.enter_context(nc.sbuf_tensor("f1_m", [128, NT * NE], F32))
            mTb = ph.enter_context(nc.sbuf_tensor("f1_mTb", [NE, T], BF16))
            mb = [ph.enter_context(nc.sbuf_tensor(f"f1_mb{i}", [128, T], BF16)) for i in range(2)]
            upbr = ph.enter_context(nc.sbuf_tensor("f1_upbr", [NE * ND, 128], BF16))
            upbc = ph.enter_context(nc.sbuf_tensor("f1_upbc", [128, NE * ND], F32))
            dnb = ph.enter_context(nc.sbuf_tensor("f1_dnb", [NE, DM], BF16))
            shupb = ph.enter_context(nc.sbuf_tensor("f1_shupbb", [1, H1], BF16))
            shdnb = ph.enter_context(nc.sbuf_tensor("f1_shdnbb", [1, DM], BF16))
            wA = ph.enter_context(nc.sbuf_tensor("f1_wA", [128, ND * H1], BF16))
            wB = ph.enter_context(nc.sbuf_tensor("f1_wB", [128, ND * DM], BF16))
            hbuf = [ph.enter_context(nc.sbuf_tensor(f"f1_h{i}", [128, T], BF16)) for i in range(ND)]

            def g_pre(g):
                self.load_wchunks(g, wr, d["f1_gate"], DM, NE)
                self.loadW(g, dnb[:], d["f1_downb"][:])
                self.loadW(g, shupb[:], d["f1_shupb"][:])
                self.loadW(g, shdnb[:], d["f1_shdownb"][:])
                self.load_wchunks(g, wA, d["f1_shup"], DM, H1)
                self.load_wchunks(g, wB, d["f1_shdown"], H1, DM)
            blk.gpsimd(g_pre)
            w_pre = self.cnt["sW"]

            self.emit_ln(ph, xn, scratch=wA[:].bitcast(F32))
            ps_rot = [ph.enter_context(nc.psum_tensor(f"f1_rot{i}", [128, 512], F32)) for i in range(4)]
            ps_d = [ph.enter_context(nc.psum_tensor(f"f1_psd{i}", [128, 512], F32)) for i in range(4)]

            pe0, dv0 = self.cnt["sPE"], self.cnt["sDV"]
            for ti in range(NT):
                r = ti % 4

                def pe_s(t, ti=ti, r=r):
                    t.wait_ge(self.sem["sW"], w_pre)
                    t.wait_ge(self.sem["sDV"], dv0 + ti - 3 if ti >= 4 else dv0)
                    for kj in range(ND):
                        inst = t.matmul(ps_rot[r][:, 0:NE], xn[kj][:, ti * 128:(ti + 1) * 128],
                                        wr[:, kj * NE:(kj + 1) * NE],
                                        start=(kj == 0), stop=(kj == ND - 1))
                    self.mark("tensor", inst)
                blk.tensor(pe_s)
                pec = self.cnt["sPE"]

                def dv_s(v, ti=ti, r=r, pec=pec):
                    v.wait_ge(self.sem["sPE"], pec)
                    self.mark("vector", v.tensor_copy(scores[:, ti * NE:(ti + 1) * NE], ps_rot[r][:, 0:NE]))
                blk.vector(dv_s)

            def dv_g1(v):
                for ti in range(NT):
                    v.max(top8[:, ti * NE:(ti + 1) * NE], scores[:, ti * NE:(ti + 1) * NE])
                t8v = top8[:].rearrange("p (ti e) -> p ti e", ti=NT)
                inst = v.tensor_tensor(dif[:], t8v[:, :, 1], t8v[:, :, 0], ALU.subtract)
                self.mark("vector", inst)
            blk.vector(dv_g1)
            dvg1 = self.cnt["sDV"]

            def ac_sig(a):
                a.wait_ge(self.sem["sDV"], dvg1)
                self.mark("scalar", a.activation(w2c[:], dif[:], AF.Sigmoid))
            blk.scalar(ac_sig)
            acsig = self.cnt["sAC"]

            def dv_g1b(v):
                v.wait_ge(self.sem["sAC"], acsig)
                t8v = top8[:].rearrange("p (ti e) -> p ti e", ti=NT)
                self.mark("vector", v.tensor_tensor(dif[:], t8v[:, :, 0], t8v[:, :, 1], ALU.subtract))
            blk.vector(dv_g1b)
            dvg1b = self.cnt["sDV"]

            def ac_sig2(a):
                a.wait_ge(self.sem["sDV"], dvg1b)
                self.mark("scalar", a.activation(w1c[:], dif[:], AF.Sigmoid))
            blk.scalar(ac_sig2)
            acsig2 = self.cnt["sAC"]

            def dv_g2(v):
                v.wait_ge(self.sem["sAC"], acsig2)
                for ti in range(NT):
                    v.tensor_scalar(msb[:, ti * NE:(ti + 1) * NE], scores[:, ti * NE:(ti + 1) * NE],
                                    top8[:, ti * NE:ti * NE + 1], None, ALU.is_equal)
                for ti in range(NT):
                    v.tensor_scalar(eq2[:, ti * NE:(ti + 1) * NE], scores[:, ti * NE:(ti + 1) * NE],
                                    top8[:, ti * NE + 1:ti * NE + 2], None, ALU.is_equal)
                for ti in range(NT):
                    v.tensor_scalar(msb[:, ti * NE:(ti + 1) * NE], msb[:, ti * NE:(ti + 1) * NE],
                                    w1c[:, ti:ti + 1], None, ALU.mult)
                for ti in range(NT):
                    v.tensor_scalar(eq2[:, ti * NE:(ti + 1) * NE], eq2[:, ti * NE:(ti + 1) * NE],
                                    w2c[:, ti:ti + 1], None, ALU.mult)
                inst = v.tensor_tensor(msb[:], msb[:], eq2[:], ALU.add)
                self.mark("vector", inst)
            blk.vector(dv_g2)
            dvg2 = self.cnt["sDV"]

            for tq in range(4):
                def pe_mt(t, tq=tq):
                    t.wait_ge(self.sem["sDV"], dvg2 if tq == 0 else self.cnt["sDV"])
                    for j in range(4):
                        ti = tq * 4 + j
                        inst = t.transpose(ps_rot[0][0:NE, j * 128:(j + 1) * 128],
                                           msb[:, ti * NE:(ti + 1) * NE], self.identF[:])
                    self.mark("tensor", inst)
                blk.tensor(pe_mt)
                pemt = self.cnt["sPE"]

                def dv_mt(v, tq=tq, pemt=pemt):
                    v.wait_ge(self.sem["sPE"], pemt)
                    self.mark("vector", v.tensor_copy(mTb[0:NE, tq * 512:(tq + 1) * 512],
                                                      ps_rot[0][0:NE, 0:512]))
                blk.vector(dv_mt)

            def g_sh(g):
                g.wait_ge(self.sem["sDV"], self.cnt["sDV"])
                self.load_wchunks(g, wA, d["f1_shup"], DM, H1)
                self.load_wchunks(g, wB, d["f1_shdown"], H1, DM)
                self.loadW(g, upbr[:], d["f1_upb"][:])
            blk.gpsimd(g_sh)
            w_sh = self.cnt["sW"]

            ps_ub = ps_rot[0][:].bitcast(BF16)[:, 0:NE * ND]

            def pe_ub(t):
                t.wait_ge(self.sem["sW"], w_sh)
                self.mark("tensor", t.transpose(ps_ub, upbr[:],
                                                self.identB[0:NE * ND, 0:NE * ND]))
            blk.tensor(pe_ub)
            peub = self.cnt["sPE"]

            def dv_ub(v):
                v.wait_ge(self.sem["sPE"], peub)
                self.mark("vector", v.tensor_copy(upbc[:], ps_ub))
            blk.vector(dv_ub)

            pe0, ac0, dvP0 = self.cnt["sPE"], self.cnt["sAC"], self.cnt["sDV"]

            def pe_s1B(t, pe0=pe0, ac0=ac0, dvP0=dvP0):
                t.wait_ge(self.sem["sW"], w_sh)
                t.wait_ge(self.sem["sDV"], dvP0)
                for it in range(32):
                    mj, tc = divmod(it, 4)
                    r = it % 2
                    if it >= 2:
                        t.wait_ge(self.sem["sAC"], ac0 + it - 1)
                    for kj in range(ND):
                        t.matmul(ps_rot[r][:],
                                 wA[:, kj * H1 + mj * 128: kj * H1 + (mj + 1) * 128],
                                 xn[kj][:, tc * 512:(tc + 1) * 512],
                                 start=(kj == 0), stop=False)
                    inst = t.matmul(ps_rot[r][:],
                                    shupb[0:1, mj * 128:(mj + 1) * 128],
                                    self.ones[0:1, tc * 512:(tc + 1) * 512],
                                    start=False, stop=True)
                    self.mark("tensor", inst)
            blk.tensor(pe_s1B)

            def ac_s1B(a, pe0=pe0):
                for it in range(32):
                    mj, tc = divmod(it, 4)
                    r = it % 2
                    a.wait_ge(self.sem["sPE"], pe0 + it + 1)
                    self.mark("scalar", a.activation(hbuf[mj][:, tc * 512:(tc + 1) * 512],
                                                     ps_rot[r][:], AF.Silu))
            blk.scalar(ac_s1B)
            ac_s1_done = self.cnt["sAC"]

            peS0, dvS0 = self.cnt["sPE"], self.cnt["sDV"]

            def pe_s2B(t, peS0=peS0, dvS0=dvS0):
                t.wait_ge(self.sem["sAC"], ac_s1_done)
                t.wait_ge(self.sem["sDV"], dvS0)
                for it in range(32):
                    m, tc = divmod(it, 4)
                    r = 2 + it % 2
                    if it >= 2:
                        t.wait_ge(self.sem["sDV"], dvS0 + it - 1)
                    for kj in range(ND):
                        t.matmul(ps_rot[r][:],
                                 wB[:, kj * DM + m * 128: kj * DM + (m + 1) * 128],
                                 hbuf[kj][:, tc * 512:(tc + 1) * 512],
                                 start=(kj == 0), stop=False)
                    inst = t.matmul(ps_rot[r][:],
                                    shdnb[0:1, m * 128:(m + 1) * 128],
                                    self.ones[0:1, tc * 512:(tc + 1) * 512],
                                    start=False, stop=True)
                    self.mark("tensor", inst)
            blk.tensor(pe_s2B)

            def dv_s2B(v, peS0=peS0):
                for it in range(32):
                    m, tc = divmod(it, 4)
                    r = 2 + it % 2
                    v.wait_ge(self.sem["sPE"], peS0 + it + 1)
                    inst = v.tensor_tensor(self.resid[m][:, tc * 512:(tc + 1) * 512],
                                           self.resid[m][:, tc * 512:(tc + 1) * 512],
                                           ps_rot[r][:], ALU.add)
                    self.mark("vector", inst)
            blk.vector(dv_s2B)

            exp_pe_done = 0
            self.dv_psd_free = self.cnt["sDV"]
            h_dv_hist = []
            for e in range(NE):
                pb = e % 2
                dv_mbw = self.cnt["sDV"]
                for tc in range(4):
                    def pe_mb(t, e=e, tc=tc, dv_mbw=dv_mbw):
                        t.wait_ge(self.sem["sDV"], dv_mbw)
                        self.mark("tensor", t.matmul(ps_rot[tc][:], self.sel8[:, e * 128:(e + 1) * 128],
                                                     mTb[0:NE, tc * 512:(tc + 1) * 512],
                                                     start=True, stop=True))
                    blk.tensor(pe_mb)
                    pemb = self.cnt["sPE"]

                    def dv_mb(v, tc=tc, pemb=pemb, pb=pb):
                        v.wait_ge(self.sem["sPE"], pemb)
                        self.mark("vector", v.tensor_copy(mb[pb][:, tc * 512:(tc + 1) * 512], ps_rot[tc][:]))
                    blk.vector(dv_mb)
                dv_mb_done = self.cnt["sDV"]

                def g_e(g, e=e, exp_pe_done=exp_pe_done):
                    g.wait_ge(self.sem["sPE"], self.cnt["sPE"])
                    self.load_wchunks(g, wA, d["f1_up"][e], DM, H1)
                    self.load_wchunks(g, wB, d["f1_down"][e], H1, DM)
                blk.gpsimd(g_e)
                w_e = self.cnt["sW"]

                peU0, acU0, dvU0 = self.cnt["sPE"], self.cnt["sAC"], self.cnt["sDV"]

                def pe_upB(t, e=e, w_e=w_e, peU0=peU0, dvU0=dvU0):
                    t.wait_ge(self.sem["sW"], w_e)
                    for it in range(32):
                        mj, tc = divmod(it, 4)
                        r = it % 2
                        t.wait_ge(self.sem["sDV"], dvU0 + it - 1 if it >= 2 else dvU0)
                        for kj in range(ND):
                            inst = t.matmul(ps_rot[r][:],
                                            wA[:, kj * H1 + mj * 128: kj * H1 + (mj + 1) * 128],
                                            xn[kj][:, tc * 512:(tc + 1) * 512],
                                            start=(kj == 0), stop=(kj == ND - 1))
                        self.mark("tensor", inst)
                blk.tensor(pe_upB)

                def ac_upB(a, e=e, peU0=peU0, dvU0=dvU0):
                    for it in range(32):
                        mj, tc = divmod(it, 4)
                        r = it % 2
                        a.wait_ge(self.sem["sPE"], peU0 + it + 1)
                        if it >= 2:
                            a.wait_ge(self.sem["sDV"], dvU0 + it - 1)
                        self.mark("scalar", a.activation(hbuf[mj][:, tc * 512:(tc + 1) * 512],
                                                         ps_rot[r][:], AF.Silu,
                                                         bias=upbc[:, e * ND + mj: e * ND + mj + 1]))
                blk.scalar(ac_upB)

                def dv_hcB(v, pb=pb, acU0=acU0):
                    for it in range(32):
                        mj, tc = divmod(it, 4)
                        v.wait_ge(self.sem["sAC"], acU0 + it + 1)
                        hs = hbuf[mj][:, tc * 512:(tc + 1) * 512]
                        inst = v.tensor_tensor(hs, hs, mb[pb][:, tc * 512:(tc + 1) * 512], ALU.mult)
                        self.mark("vector", inst)
                blk.vector(dv_hcB)
                dv_h_done = self.cnt["sDV"]

                peD0, dvD0 = self.cnt["sPE"], self.cnt["sDV"]

                def pe_dnB(t, w_e=w_e, peD0=peD0, dvD0=dvD0):
                    t.wait_ge(self.sem["sW"], w_e)
                    for idx in range(16):
                        tc, mg = divmod(idx, 4)
                        pr = 2 * (idx % 2)
                        t.wait_ge(self.sem["sDV"], dvD0 + idx - 1 if idx >= 2 else dvD0)
                        for mi in range(2):
                            m = mg * 2 + mi
                            for kj in range(ND):
                                inst = t.matmul(ps_d[pr + mi][:],
                                                wB[:, kj * DM + m * 128: kj * DM + (m + 1) * 128],
                                                hbuf[kj][:, tc * 512:(tc + 1) * 512],
                                                start=(kj == 0), stop=(kj == ND - 1))
                        self.mark("tensor", inst)
                blk.tensor(pe_dnB)

                def dv_dnB(v, peD0=peD0):
                    for idx in range(16):
                        tc, mg = divmod(idx, 4)
                        pr = 2 * (idx % 2)
                        v.wait_ge(self.sem["sPE"], peD0 + idx + 1)
                        for mi in range(2):
                            m = mg * 2 + mi
                            inst = v.tensor_tensor(self.resid[m][:, tc * 512:(tc + 1) * 512],
                                                   self.resid[m][:, tc * 512:(tc + 1) * 512],
                                                   ps_d[pr + mi][:], ALU.add)
                        self.mark("vector", inst)
                blk.vector(dv_dnB)
                exp_pe_done = self.cnt["sPE"]

            for m in range(ND):
                for tc in range(4):
                    it = m * 4 + tc
                    r = it % 4
                    dvf = self.cnt["sDV"]

                    def pe_db(t, m=m, tc=tc, r=r, dvf=dvf):
                        t.wait_ge(self.sem["sDV"], dvf)
                        self.mark("tensor", t.matmul(ps_rot[r][:],
                                                     dnb[0:NE, m * 128:(m + 1) * 128],
                                                     mTb[0:NE, tc * 512:(tc + 1) * 512],
                                                     start=True, stop=True))
                    blk.tensor(pe_db)
                    pedb = self.cnt["sPE"]

                    def dv_db(v, m=m, tc=tc, r=r, pedb=pedb):
                        v.wait_ge(self.sem["sPE"], pedb)
                        inst = v.tensor_tensor(self.resid[m][:, tc * 512:(tc + 1) * 512],
                                               self.resid[m][:, tc * 512:(tc + 1) * 512],
                                               ps_rot[r][:], ALU.add)
                        self.mark("vector", inst)
                    blk.vector(dv_db)
        self.barrier()

    def phase_out(self):
        nc, blk, d = self.nc, self.block, self.d
        with ExitStack() as ph:
            ytok = [ph.enter_context(nc.sbuf_tensor(f"out_ytok{i}", [128, DM], F32)) for i in range(2)]
            pst = [ph.enter_context(nc.psum_tensor(f"out_ps{i}", [128, 512], F32)) for i in range(2)]
            pe0, dv0, x0 = self.cnt["sPE"], self.cnt["sDV"], self.cnt["sX"]

            def pe_body(t):
                for ti in range(NT):
                    for half in range(2):
                        if ti >= 1:
                            t.wait_ge(self.sem["sDV"], dv0 + 2 * (ti - 1) + half + 1)
                        for j in range(4):
                            dj = half * 4 + j
                            inst = t.transpose(pst[half][:, j * 128:(j + 1) * 128],
                                               self.resid[dj][:, ti * 128:(ti + 1) * 128],
                                               self.identF[:])
                        self.mark("tensor", inst)
            blk.tensor(pe_body)

            def dv_body(v):
                for ti in range(NT):
                    if ti >= 2:
                        v.wait_ge(self.sem["sX"], x0 + 16 * (ti - 1))
                    for half in range(2):
                        v.wait_ge(self.sem["sPE"], pe0 + 2 * ti + half + 1)
                        inst = v.tensor_copy(ytok[ti % 2][:, half * 512:(half + 1) * 512],
                                             pst[half][:, 0:512])
                        self.mark("vector", inst)
            blk.vector(dv_body)

            def s_body(s):
                for ti in range(NT):
                    s.wait_ge(self.sem["sDV"], dv0 + 2 * ti + 2)
                    self.loadX(s, d["y_out"][ti * 128:(ti + 1) * 128, :], ytok[ti % 2][:])
                s.wait_ge(self.sem["sX"], self.cnt["sX"])
            blk.sync(s_body)


def host_constants(blocks):
    c = {}
    cvals = [0.0, 1.0, -0.5, LN_EPS, RMS_EPS, SCALE, -SCALE, 1.0 / DM]
    c["cvals"] = np.broadcast_to(np.asarray(cvals, np.float32), (128, len(cvals))).copy()
    c["identF"] = np.eye(128, dtype=np.float32)
    c["onesrowF"] = np.ones((1, 128), np.float32)
    c["onesrow"] = np.ones((1, T), np.float32)
    c["identB"] = np.eye(128, dtype=BF16NP)
    if "l0a" in blocks:
        q = np.arange(128)
        m = np.where(q[None, :] <= q[:, None], 0.0, -1e9).astype(np.float32)
        c["maskT"] = m.T.copy().astype(BF16NP)
    if "l0f" in blocks or "l1f" in blocks:
        s8 = np.zeros((NE, NE * 128), BF16NP)
        for e in range(NE):
            s8[e, e * 128:(e + 1) * 128] = 1.0
        c["sel8"] = s8
    if "l1a" in blocks:
        s16 = np.zeros((DS, DS * 128), BF16NP)
        for s in range(DS):
            s16[s, s * 128:(s + 1) * 128] = 1.0
        c["sel16"] = s16
    return c


def host_weights(inputs, blocks):
    w = {}
    f32 = np.float32
    asc = np.ascontiguousarray
    if "l0a" in blocks:
        w["wq_lat"] = asc(inputs["L0_attn_wq_lat"], dtype=f32)
        w["wk_lat"] = asc(inputs["L0_attn_wk_lat"], dtype=f32)
        w["wv_lat"] = asc(inputs["L0_attn_wv_lat"], dtype=f32)
        w["wq_heads"] = asc(inputs["L0_attn_wq_heads"], dtype=f32)
        w["wk_heads"] = asc(inputs["L0_attn_wk_heads"], dtype=f32)
        w["wv_heads"] = asc(inputs["L0_attn_wv_heads"], dtype=f32)
        w["w_proj"] = asc(inputs["L0_attn_w_proj"], dtype=f32)
        w["b_proj"] = asc(inputs["L0_attn_b_proj"].reshape(1, DM), dtype=f32)
    if "l0f" in blocks:
        w["f0_scale"] = asc(inputs["L0_ffn_norm_scale"].reshape(ND, 128), dtype=f32)
        w["f0_router"] = asc(inputs["L0_ffn_w_router"].T, dtype=f32)
        w["f0_up"] = asc(inputs["L0_ffn_w_up"].transpose(0, 2, 1), dtype=f32)
        w["f0_down"] = asc(inputs["L0_ffn_w_down"].transpose(0, 2, 1), dtype=f32)
    if "l1a" in blocks:
        w["a1_Win"] = asc(inputs["L1_attn_W_in"], dtype=f32)
        w["a1_bin"] = asc(inputs["L1_attn_b_in"].reshape(1, -1), dtype=f32)
        w["a1_Wx"] = asc(inputs["L1_attn_W_x"], dtype=f32)
        w["a1_bx"] = asc(inputs["L1_attn_b_x"].reshape(1, -1), dtype=f32)
        w["a1_Wdt"] = asc(inputs["L1_attn_W_dt"], dtype=f32)
        w["a1_bdt"] = asc(inputs["L1_attn_b_dt"].reshape(1, -1), dtype=f32)
        w["a1_Wout"] = asc(inputs["L1_attn_W_out"], dtype=f32)
        w["a1_bout"] = asc(inputs["L1_attn_b_out"].reshape(1, -1), dtype=f32)
        w["a1_Alog"] = asc(inputs["L1_attn_A_log"], dtype=f32)
        w["a1_D"] = asc(inputs["L1_attn_D_param"].reshape(NDI, 128), dtype=f32)
    if "l1f" in blocks:
        w["f1_gate"] = asc(inputs["L1_ffn_gate_w"].T, dtype=f32)
        w["f1_up"] = asc(inputs["L1_ffn_up_w"], dtype=f32)
        w["f1_upb"] = asc(inputs["L1_ffn_up_b"].reshape(NE * ND, 128), dtype=f32)
        w["f1_down"] = asc(inputs["L1_ffn_down_w"], dtype=f32)
        w["f1_downb"] = asc(inputs["L1_ffn_down_b"], dtype=f32)
        w["f1_shup"] = asc(inputs["L1_ffn_sh_up_w"], dtype=f32)
        w["f1_shupb"] = asc(inputs["L1_ffn_sh_up_b"].reshape(1, -1), dtype=f32)
        w["f1_shdown"] = asc(inputs["L1_ffn_sh_down_w"], dtype=f32)
        w["f1_shdownb"] = asc(inputs["L1_ffn_sh_down_b"].reshape(1, -1), dtype=f32)
    return w


def run_blocks(x_full, inputs, blocks):
    nc = Builder(blocks).build()
    w = host_weights(inputs, blocks)
    c = host_constants(blocks)
    in_maps = []
    for b in range(8):
        m = {"x_in": np.ascontiguousarray(x_full[b], dtype=np.float32)}
        m.update(w)
        m.update(c)
        in_maps.append(m)
    res = run_bass_kernel_spmd(nc, in_maps, core_ids=list(range(8)))
    return np.stack([r["y_out"] for r in res.results], axis=0)


def kernel(**inputs):
    x = np.asarray(inputs["L0_attn_x"], dtype=np.float32)
    return run_blocks(x, inputs, list(ALL_BLOCKS))


# version 09
# speedup vs baseline: 1.2576x; 1.0322x over previous
import numpy as np
import ml_dtypes
from contextlib import ExitStack

import concourse.bass as bass
import concourse.mybir as mybir
from concourse.bass_utils import run_bass_kernel_spmd

F32 = mybir.dt.float32
BF16 = mybir.dt.bfloat16
AF = mybir.ActivationFunctionType
ALU = mybir.AluOpType
BF16NP = ml_dtypes.bfloat16

T = 2048
DM = 1024
NT = T // 128
ND = DM // 128
LAT = 256
NH = 16
HD = 64
SCALE = 0.125
DI = 2048
NDI = DI // 128
DS = 16
H1 = 1024
NE = 8
LN_EPS = 1e-5
RMS_EPS = 1e-6

ALL_BLOCKS = ("l0a", "l0f", "l1a", "l1f")


class Builder:
    def __init__(self, blocks, debug=False):
        self.debug = debug
        self.blocks = list(blocks)
        nc = bass.Bass("TRN2", target_bir_lowering=False, debug=False)
        self.nc = nc
        self.ctx = ExitStack()
        d = {}
        self.d = d

        def dram(name, shape, kind="ExternalInput", dtype=F32):
            d[name] = nc.dram_tensor(name, list(shape), dtype, kind=kind).ap()
        blocks = self.blocks
        dram("x_in", [T, DM])
        dram("y_out", [T, DM], kind="ExternalOutput")
        if "l0a" in blocks:
            for n, s in [("wq_lat", [DM, LAT]), ("wk_lat", [DM, LAT]), ("wv_lat", [DM, LAT]),
                         ("wq_heads", [LAT, DM]), ("wk_heads", [LAT, DM]), ("wv_heads", [LAT, DM]),
                         ("w_proj", [DM, DM]), ("b_proj", [1, DM])]:
                dram(n, s)
        if "l0f" in blocks:
            dram("f0_scale", [ND, 128])
            dram("f0_router", [DM, NE])
            dram("f0_up", [NE, DM, 2 * DI])
            dram("f0_down", [NE, DI, DM])
        if "l1a" in blocks:
            dram("a1_Win", [DM, 2 * DI])
            dram("a1_bin", [1, 2 * DI])
            dram("a1_Wx", [DI, 2 * DS])
            dram("a1_bx", [1, 2 * DS])
            dram("a1_Wdt", [DI, DI])
            dram("a1_bdt", [1, DI])
            dram("a1_Wout", [DI, DM])
            dram("a1_bout", [1, DM])
            dram("a1_Alog", [DI, DS])
            dram("a1_D", [NDI, 128])
            dram("sp_dt", [DI, T], kind="Internal", dtype=BF16)
            dram("sp_dtu", [DI, T], kind="Internal", dtype=BF16)
            dram("sp_u", [DI, T], kind="Internal", dtype=BF16)
            dram("sp_sz", [DI, T], kind="Internal", dtype=BF16)
        if "l1f" in blocks:
            dram("f1_gate", [DM, NE])
            dram("f1_up", [NE, DM, H1])
            dram("f1_upb", [NE * ND, 128])
            dram("f1_down", [NE, H1, DM])
            dram("f1_downb", [NE, DM])
            dram("f1_shup", [DM, H1])
            dram("f1_shupb", [1, H1])
            dram("f1_shdown", [H1, DM])
            dram("f1_shdownb", [1, DM])
        if debug:
            dram("dbg_a", [128, 2048], kind="ExternalOutput")
            dram("dbg_b", [128, 2048], kind="ExternalOutput")
            dram("dbg_xn", [DM, T], kind="ExternalOutput", dtype=BF16)
        dram("identF", [128, 128])
        dram("onesrow", [1, T])
        dram("onesrowF", [1, 128])
        dram("cvals", [128, 8])
        dram("identB", [128, 128], dtype=BF16)
        if "l0a" in blocks:
            dram("maskT", [128, 128], dtype=BF16)
        if "l0f" in blocks or "l1f" in blocks:
            dram("sel8", [NE, NE * 128], dtype=BF16)
        if "l1a" in blocks:
            dram("sel16", [DS, DS * 128], dtype=BF16)

        self.sem = {}
        self.cnt = {}
        for s in ("sW", "sX", "sPE", "sDV", "sAC", "sPO", "sBAR"):
            self.sem[s] = self.ctx.enter_context(nc.semaphore(s))
            self.cnt[s] = 0
        self.last_inst = {}
        self.eng_sem = {"tensor": "sPE", "vector": "sDV", "scalar": "sAC", "gpsimd": "sPO"}
        self.ac_evac_hist = []
        self.sz_spill_hist = []
        self.dt_spill_hist = []
        self.dt_dv_hist = []
        self.cnt_sDV_bb = []
        self.ssm_dv_hist = []
        self.wout_dv_hist = []

        sb = self.sb = lambda name, shape, dt: self.ctx.enter_context(
            nc.sbuf_tensor(name, list(shape), dt))
        self.cvals_list = [0.0, 1.0, -0.5, LN_EPS, RMS_EPS, SCALE, -SCALE, 1.0 / DM]
        self.cvals = sb("c_cvals", [128, len(self.cvals_list)], F32)
        for j, v in enumerate(self.cvals_list):
            nc.const_aps.register(F32, v, self.cvals[:, j:j + 1])
        self.resid = [sb(f"resid{i}", [128, T], F32) for i in range(ND)]
        self.identF = sb("c_identF", [128, 128], F32)
        self.ones = sb("c_ones", [1, T], BF16)
        self.onesrF = sb("c_onesrF", [1, 128], F32)
        self.identB = sb("c_identB", [128, 128], BF16)
        if "l0a" in blocks:
            self.maskT = sb("c_maskT", [128, 128], BF16)
        if "l0f" in blocks or "l1f" in blocks:
            self.sel8 = sb("c_sel8", [NE, NE * 128], BF16)
        if "l1a" in blocks:
            self.sel16 = sb("c_sel16", [DS, DS * 128], BF16)
        self.block = None

    def inc(self, inst, s, k=1):
        inst.then_inc(self.sem[s], k)
        self.cnt[s] += k
        return inst

    def note(self, eng, inst):
        self.last_inst[eng] = inst
        return inst

    def mark(self, eng, inst):
        self.note(eng, inst)
        s = self.eng_sem[eng]
        return self.inc(inst, s)

    def barrier(self):
        self.last_inst = {}
        tgts = {s: self.cnt[s] for s in ("sPE", "sDV", "sAC", "sPO", "sW", "sX")}
        for name in ("tensor", "vector", "scalar", "gpsimd", "sync"):
            def body(e, tgts=tgts):
                for s, v in tgts.items():
                    if v > 0:
                        e.wait_ge(self.sem[s], v)
            getattr(self.block, name)(body)

    def sb_local(self, stack, name, shape, dt):
        return stack.enter_context(self.nc.sbuf_tensor(name, list(shape), dt))

    def loadW(self, g, dst_ap, src_ap):
        return self.inc(g.dma_start(out=dst_ap, in_=src_ap), "sW", 16)

    def loadX(self, s, dst_ap, src_ap):
        return self.inc(s.dma_start(out=dst_ap, in_=src_ap), "sX", 16)

    def load_wchunks(self, g, dst_sb, src, K, M):
        nk = K // 128
        src_r = src.rearrange("(kj p) m -> p kj m", p=128)
        dst = dst_sb[:, 0:nk * M].rearrange("p (kj m) -> p kj m", kj=nk)
        return self.loadW(g, dst, src_r)

    def build(self):
        nc = self.nc
        self.block = self.ctx.enter_context(nc.Block())
        self.phase_consts()
        self.phase_in()
        for b in self.blocks:
            getattr(self, "phase_" + b)()
        self.phase_out()
        self.ctx.close()
        return nc

    def phase_consts(self):
        blk, d = self.block, self.d

        def s_body(s):
            self.loadX(s, self.cvals[:], d["cvals"][:])
            self.loadX(s, self.identF[:], d["identF"][:])
            self.loadX(s, self.onesrF[:], d["onesrowF"][:])
            self.loadX(s, self.identB[:], d["identB"][:])
            if "l0a" in self.blocks:
                self.loadX(s, self.maskT[:], d["maskT"][:])
            if "l0f" in self.blocks or "l1f" in self.blocks:
                self.loadX(s, self.sel8[:], d["sel8"][:])
            if "l1a" in self.blocks:
                self.loadX(s, self.sel16[:], d["sel16"][:])
        blk.sync(s_body)

        def g_body(g):
            self.loadW(g, self.ones[:], d["onesrow"][:])
        blk.gpsimd(g_body)
        self.barrier()

    def phase_in(self):
        nc, blk, d = self.nc, self.block, self.d
        with ExitStack() as ph:
            xtok = [ph.enter_context(nc.sbuf_tensor(f"in_xtok{i}", [128, DM], F32)) for i in range(2)]
            pst = [ph.enter_context(nc.psum_tensor(f"in_ps{i}", [128, 512], F32)) for i in range(2)]
            x0, pe0, dv0 = self.cnt["sX"], self.cnt["sPE"], self.cnt["sDV"]

            def s_body(s):
                for ti in range(NT):
                    if ti >= 2:
                        s.wait_ge(self.sem["sPE"], pe0 + 2 * ti - 2)
                    self.loadX(s, xtok[ti % 2][:], d["x_in"][ti * 128:(ti + 1) * 128, :])
            blk.sync(s_body)

            def pe_body(t):
                for ti in range(NT):
                    t.wait_ge(self.sem["sX"], x0 + 16 * (ti + 1))
                    for half in range(2):
                        if ti >= 1:
                            t.wait_ge(self.sem["sDV"], dv0 + 2 * (ti - 1) + half + 1)
                        for j in range(4):
                            dj = half * 4 + j
                            inst = t.transpose(pst[half][:, j * 128:(j + 1) * 128],
                                               xtok[ti % 2][:, dj * 128:(dj + 1) * 128],
                                               self.identF[:])
                        self.mark("tensor", inst)
            blk.tensor(pe_body)

            def dv_body(v):
                for ti in range(NT):
                    for half in range(2):
                        v.wait_ge(self.sem["sPE"], pe0 + 2 * ti + half + 1)
                        inst = None
                        for j in range(4):
                            dj = half * 4 + j
                            inst = v.tensor_copy(self.resid[dj][:, ti * 128:(ti + 1) * 128],
                                                 pst[half][:, j * 128:(j + 1) * 128])
                        self.mark("vector", inst)
            blk.vector(dv_body)
        self.barrier()

    def emit_ln(self, ph, xn, rms_scale_cols=None, skip_add=False, scratch=None):
        nc, blk = self.nc, self.block
        self.ln_id = getattr(self, "ln_id", 0) + 1
        u = self.ln_id
        with ExitStack() as ln:
            if scratch is None:
                sq = [ln.enter_context(nc.sbuf_tensor(f"ln{u}_sq{i}", [128, 512], F32))[:] for i in range(2)]
                tn = [ln.enter_context(nc.sbuf_tensor(f"ln{u}_tn{i}", [128, 512], F32))[:] for i in range(2)]
                mrow = ln.enter_context(nc.sbuf_tensor(f"ln{u}_m", [1, 512], F32))[:]
                vrow = ln.enter_context(nc.sbuf_tensor(f"ln{u}_v", [1, 512], F32))[:]
                rrow = ln.enter_context(nc.sbuf_tensor(f"ln{u}_r", [1, 512], F32))[:]
                mrrow = ln.enter_context(nc.sbuf_tensor(f"ln{u}_mr", [1, 512], F32))[:]
            else:
                sv = scratch
                sq = [sv[:, 0:512], sv[:, 512:1024]]
                tn = [sv[:, 1024:1536], sv[:, 1536:2048]]
                mrow = sv[0:1, 2048:2560]
                vrow = sv[0:1, 2560:3072]
                rrow = sv[0:1, 3072:3584]
                mrrow = sv[0:1, 3584:4096]
            onesc = self.cvals[:, 1:2]
            onesr = self.onesrF
            ps_s = ln.enter_context(nc.psum_tensor(f"ln{u}_ps_s", [128, 512], F32))
            ps_q = ln.enter_context(nc.psum_tensor(f"ln{u}_ps_q", [128, 512], F32))
            ps_b = [ln.enter_context(nc.psum_tensor(f"ln{u}_ps_b{i}", [128, 512], F32)) for i in range(2)]
            peLN0 = self.cnt["sPE"]

            for tc in range(4):
                sl = slice(tc * 512, (tc + 1) * 512)
                ac_base = self.cnt["sAC"]
                pe_tc0 = self.cnt["sPE"]

                def ac_sq(a, sl=sl, tc=tc, pe_tc0=pe_tc0):
                    for dj in range(ND):
                        if dj < 2:
                            if tc > 0:
                                a.wait_ge(self.sem["sPE"], pe_tc0)
                        else:
                            a.wait_ge(self.sem["sPE"], pe_tc0 + dj - 1)
                        self.mark("scalar", a.activation(sq[dj % 2], self.resid[dj][:, sl], AF.Square))
                blk.scalar(ac_sq)

                dv_prev_stats = self.cnt["sDV"]

                def pe_body(t, sl=sl, ac_base=ac_base, dv_prev_stats=dv_prev_stats):
                    t.wait_ge(self.sem["sDV"], dv_prev_stats)
                    for dj in range(ND):
                        t.matmul(ps_s[0:1, :], onesc, self.resid[dj][:, sl],
                                 start=(dj == 0), stop=(dj == ND - 1))
                    for dj in range(ND):
                        t.wait_ge(self.sem["sAC"], ac_base + dj + 1)
                        inst = t.matmul(ps_q[0:1, :], onesc, sq[dj % 2],
                                        start=(dj == 0), stop=(dj == ND - 1))
                        self.mark("tensor", inst)
                blk.tensor(pe_body)
                pe_done = self.cnt["sPE"]

                def dv_stats(v, sl=sl, pe_done=pe_done):
                    v.wait_ge(self.sem["sPE"], pe_done)
                    v.tensor_scalar_mul(mrow, ps_s[0:1, :], 1.0 / DM)
                    v.tensor_scalar_mul(vrow, ps_q[0:1, :], 1.0 / DM)
                    v.tensor_tensor(ps_s[0:1, :], mrow, mrow, ALU.mult)
                    self.mark("vector", v.tensor_tensor(vrow, vrow, ps_s[0:1, :], ALU.subtract))
                blk.vector(dv_stats)
                dv_done = self.cnt["sDV"]

                def ac_rstd(a, sl=sl, dv_done=dv_done):
                    a.wait_ge(self.sem["sDV"], dv_done)
                    a.activation(rrow, vrow, AF.Ln, bias=LN_EPS)
                    self.mark("scalar", a.activation(rrow, rrow, AF.Exp, scale=-0.5))
                blk.scalar(ac_rstd)
                ac_done = self.cnt["sAC"]

                def dv_mr(v, sl=sl, ac_done=ac_done):
                    v.wait_ge(self.sem["sAC"], ac_done)
                    self.mark("vector", v.tensor_tensor(mrrow, mrow, rrow, ALU.mult))
                blk.vector(dv_mr)
                dv_done2 = self.cnt["sDV"]

                def pe_bcast(t, sl=sl, dv_done2=dv_done2):
                    t.wait_ge(self.sem["sDV"], dv_done2)
                    t.matmul(ps_b[0][:], onesr[:], rrow, start=True, stop=True)
                    self.mark("tensor", t.matmul(ps_b[1][:], onesr[:], mrrow, start=True, stop=True))
                blk.tensor(pe_bcast)
                pe_done2 = self.cnt["sPE"]

                def dv_norm(v, sl=sl, pe_done2=pe_done2):
                    v.wait_ge(self.sem["sPE"], pe_done2)
                    for dj in range(ND):
                        v.tensor_tensor(tn[dj % 2], self.resid[dj][:, sl], ps_b[0][:], ALU.mult)
                        inst = v.tensor_tensor(xn[dj][:, sl], tn[dj % 2], ps_b[1][:], ALU.subtract)
                        self.mark("vector", inst)
                blk.vector(dv_norm)

            if skip_add:
                def dv_skip(v):
                    for dj in range(ND):
                        inst = v.tensor_tensor(self.resid[dj][:], self.resid[dj][:],
                                               xn[dj][:], ALU.add)
                    self.mark("vector", inst)
                blk.vector(dv_skip)

            if rms_scale_cols is not None:
                for tc in range(4):
                    sl = slice(tc * 512, (tc + 1) * 512)
                    dv_n = self.cnt["sDV"]
                    ac_base2 = self.cnt["sAC"]
                    pe_tc0 = self.cnt["sPE"]

                    def ac_sq2(a, sl=sl, dv_n=dv_n, tc=tc, pe_tc0=pe_tc0):
                        a.wait_ge(self.sem["sDV"], dv_n)
                        for dj in range(ND):
                            if dj < 2:
                                a.wait_ge(self.sem["sPE"], pe_tc0)
                            else:
                                a.wait_ge(self.sem["sPE"], pe_tc0 + dj - 1)
                            self.mark("scalar", a.activation(sq[dj % 2], xn[dj][:, sl], AF.Square))
                    blk.scalar(ac_sq2)

                    dv_cur = self.cnt["sDV"]

                    def pe_sum2(t, sl=sl, ac_base2=ac_base2, dv_cur=dv_cur):
                        t.wait_ge(self.sem["sDV"], dv_cur)
                        for dj in range(ND):
                            t.wait_ge(self.sem["sAC"], ac_base2 + dj + 1)
                            inst = t.matmul(ps_q[0:1, :], onesc, sq[dj % 2],
                                            start=(dj == 0), stop=(dj == ND - 1))
                            self.mark("tensor", inst)
                    blk.tensor(pe_sum2)
                    pe_d = self.cnt["sPE"]

                    def dv_ms(v, sl=sl, pe_d=pe_d):
                        v.wait_ge(self.sem["sPE"], pe_d)
                        self.mark("vector", v.tensor_scalar_mul(vrow, ps_q[0:1, :], 1.0 / DM))
                    blk.vector(dv_ms)
                    dv_d = self.cnt["sDV"]

                    def ac_r2(a, sl=sl, dv_d=dv_d):
                        a.wait_ge(self.sem["sDV"], dv_d)
                        a.activation(rrow, vrow, AF.Ln, bias=RMS_EPS)
                        self.mark("scalar", a.activation(rrow, rrow, AF.Exp, scale=-0.5))
                    blk.scalar(ac_r2)
                    ac_d = self.cnt["sAC"]

                    dv_cur2 = self.cnt["sDV"]

                    def pe_b2(t, sl=sl, ac_d=ac_d, dv_cur2=dv_cur2):
                        t.wait_ge(self.sem["sAC"], ac_d)
                        t.wait_ge(self.sem["sDV"], dv_cur2)
                        self.mark("tensor", t.matmul(ps_b[0][:], onesr[:], rrow, start=True, stop=True))
                    blk.tensor(pe_b2)
                    pe_d2 = self.cnt["sPE"]

                    def dv_apply(v, sl=sl, pe_d2=pe_d2):
                        v.wait_ge(self.sem["sPE"], pe_d2)
                        for dj in range(ND):
                            inst = v.scalar_tensor_tensor(
                                xn[dj][:, sl], xn[dj][:, sl], rms_scale_cols[:, dj:dj + 1],
                                ps_b[0][:], ALU.mult, ALU.mult)
                            self.mark("vector", inst)
                    blk.vector(dv_apply)
        self.barrier()

    def phase_l0a(self):
        nc, blk, d = self.nc, self.block, self.d
        NG = 4
        HPG = 4
        with ExitStack() as ph:
            xn = [ph.enter_context(nc.sbuf_tensor(f"a0_xn{i}", [128, T], BF16)) for i in range(ND)]
            wql = ph.enter_context(nc.sbuf_tensor("a0_wql", [128, ND * LAT], BF16))
            wkl = ph.enter_context(nc.sbuf_tensor("a0_wkl", [128, ND * LAT], BF16))
            wvl = ph.enter_context(nc.sbuf_tensor("a0_wvl", [128, ND * LAT], BF16))
            wqh = ph.enter_context(nc.sbuf_tensor("a0_wqh", [128, 2 * DM], BF16))
            wkh = ph.enter_context(nc.sbuf_tensor("a0_wkh", [128, 2 * DM], BF16))
            wvh = ph.enter_context(nc.sbuf_tensor("a0_wvh", [128, 2 * DM], BF16))
            wpr = ph.enter_context(nc.sbuf_tensor("a0_wpr", [128, 2 * DM], BF16))
            bpr = ph.enter_context(nc.sbuf_tensor("a0_bpr", [1, DM], BF16))

            def g_w(g):
                self.load_wchunks(g, wql, d["wq_lat"], DM, LAT)
                self.load_wchunks(g, wkl, d["wk_lat"], DM, LAT)
                self.load_wchunks(g, wvl, d["wv_lat"], DM, LAT)
                self.load_wchunks(g, wqh, d["wq_heads"], LAT, DM)
                self.load_wchunks(g, wkh, d["wk_heads"], LAT, DM)
                self.load_wchunks(g, wvh, d["wv_heads"], LAT, DM)
                self.loadW(g, bpr[:], d["b_proj"][:])
            blk.gpsimd(g_w)
            w_all = self.cnt["sW"]

            self.emit_ln(ph, xn)

            q1 = [ph.enter_context(nc.sbuf_tensor(f"a0_q1_{i}", [128, T], BF16)) for i in range(2)]
            k1 = [ph.enter_context(nc.sbuf_tensor(f"a0_k1_{i}", [128, T], BF16)) for i in range(2)]
            v1 = [ph.enter_context(nc.sbuf_tensor(f"a0_v1_{i}", [128, T], BF16)) for i in range(2)]
            qTg = [ph.enter_context(nc.sbuf_tensor(f"a0_qT{i}", [128, T], BF16)) for i in range(2)]
            kTg = [ph.enter_context(nc.sbuf_tensor(f"a0_kT{i}", [128, T], BF16)) for i in range(2)]
            vtokg = ph.enter_context(nc.sbuf_tensor("a0_vtok", [128, NT * 256], BF16))
            yTg = [ph.enter_context(nc.sbuf_tensor(f"a0_yT{i}", [128, T], BF16)) for i in range(2)]
            P = [ph.enter_context(nc.sbuf_tensor(f"a0_P{i}", [128, T], BF16)) for i in range(2)]
            PT0 = ph.enter_context(nc.sbuf_tensor("a0_PT0", [128, T], BF16))
            PT = [PT0, PT0]
            rmax = ph.enter_context(nc.sbuf_tensor("a0_rmax", [128, 1], F32))
            nbias = ph.enter_context(nc.sbuf_tensor("a0_nbias", [128, 1], F32))
            rsum = ph.enter_context(nc.sbuf_tensor("a0_rsum", [128, 1], F32))
            rrec = ph.enter_context(nc.sbuf_tensor("a0_rrec", [128, 2], F32))
            ps_rot = [ph.enter_context(nc.psum_tensor(f"a0_rot{i}", [128, 512], F32)) for i in range(2)]
            ps_S = ph.enter_context(nc.psum_tensor("a0_S", [128, T], F32))
            ps_pt = ph.enter_context(nc.psum_tensor("a0_pt", [128, 512], BF16))
            ps_y = ph.enter_context(nc.psum_tensor("a0_y", [64, 256], F32))

            def mm_evac_seq(jobs):
                pe0, dv0 = self.cnt["sPE"], self.cnt["sDV"]
                for i, (mm, ev) in enumerate(jobs):
                    r = i % 2

                    def pe_j(t, i=i, r=r, mm=mm):
                        t.wait_ge(self.sem["sDV"], dv0 + i - 1 if i >= 2 else dv0)
                        self.mark("tensor", mm(t, ps_rot[r]))
                    blk.tensor(pe_j)

                    def dv_j(v, i=i, r=r, ev=ev):
                        v.wait_ge(self.sem["sPE"], pe0 + i + 1)
                        self.mark("vector", ev(v, ps_rot[r]))
                    blk.vector(dv_j)

            jobs = []
            for (wsb, dst) in ((wql, q1), (wkl, k1), (wvl, v1)):
                for m2 in range(2):
                    for tc in range(4):
                        def mm(t, ps, wsb=wsb, m2=m2, tc=tc):
                            t.wait_ge(self.sem["sW"], w_all)
                            inst = None
                            for kj in range(ND):
                                inst = t.matmul(ps[:], wsb[:, kj * LAT + m2 * 128: kj * LAT + (m2 + 1) * 128],
                                                xn[kj][:, tc * 512:(tc + 1) * 512],
                                                start=(kj == 0), stop=(kj == ND - 1))
                            return inst

                        def ev(v, ps, dst=dst, m2=m2, tc=tc):
                            return v.tensor_copy(dst[m2][:, tc * 512:(tc + 1) * 512], ps[:])
                        jobs.append((mm, ev))
            mm_evac_seq(jobs)
            self.barrier()

            for g in range(NG):
                gofs = g * 256
                jobs = []
                for (wsb, dst) in ((wqh, qTg), (wkh, kTg)):
                    for mi in range(2):
                        for tc in range(4):
                            def mm(t, ps, wsb=wsb, mi=mi, tc=tc):
                                inst = None
                                for kj in range(2):
                                    inst = t.matmul(
                                        ps[:],
                                        wsb[:, kj * DM + gofs + mi * 128: kj * DM + gofs + (mi + 1) * 128],
                                        (q1 if wsb is wqh else k1)[kj][:, tc * 512:(tc + 1) * 512],
                                        start=(kj == 0), stop=(kj == 1))
                                return inst

                            def ev(v, ps, dst=dst, mi=mi, tc=tc):
                                return v.tensor_copy(dst[mi][:, tc * 512:(tc + 1) * 512], ps[:])
                            jobs.append((mm, ev))
                for ti2 in range(NT // 2):
                    def mm(t, ps, ti2=ti2):
                        inst = None
                        for half in range(2):
                            ti = ti2 * 2 + half
                            for kj in range(2):
                                inst = t.matmul(
                                    ps[:, half * 256:(half + 1) * 256],
                                    v1[kj][:, ti * 128:(ti + 1) * 128],
                                    wvh[:, kj * DM + gofs: kj * DM + gofs + 256],
                                    start=(kj == 0), stop=(kj == 1))
                        return inst

                    def ev(v, ps, ti2=ti2):
                        return v.tensor_copy(vtokg[:, ti2 * 512:(ti2 + 1) * 512], ps[:])
                    jobs.append((mm, ev))
                mm_evac_seq(jobs)

                for h in range(HPG):
                    qrow = (h % 2) * 64
                    mi = h // 2
                    acH0 = self.cnt["sAC"]
                    exp_jobs = []

                    def emit_ptpv(qc):
                        pq = qc % 2
                        acN = acH0 + 2 * qc + 2
                        nblk = qc + 1
                        for jb in range(0, nblk, 4):
                            nj = min(4, nblk - jb)
                            dv_at = self.cnt["sDV"]

                            def pe_pt(t, jb=jb, nj=nj, acN=acN, dv_at=dv_at, pq=pq, qc=qc):
                                if jb == 0:
                                    t.wait_ge(self.sem["sAC"], acN)
                                else:
                                    t.wait_ge(self.sem["sDV"], dv_at)
                                for j in range(nj):
                                    inst = t.transpose(ps_pt[:, j * 128:(j + 1) * 128],
                                                       P[pq][:, (jb + j) * 128:(jb + j + 1) * 128],
                                                       self.identB[:])
                                self.mark("tensor", inst)
                            blk.tensor(pe_pt)
                            pePT = self.cnt["sPE"]

                            def dv_pt(v, jb=jb, nj=nj, pePT=pePT, pq=pq):
                                v.wait_ge(self.sem["sPE"], pePT)
                                self.mark("vector", v.tensor_copy(PT[pq][:, jb * 128:(jb + nj) * 128],
                                                                  ps_pt[:, 0:nj * 128]))
                            blk.vector(dv_pt)

                        dvPT = self.cnt["sDV"]

                        def pe_pv(t, qc=qc, h=h, dvPT=dvPT, pq=pq):
                            t.wait_ge(self.sem["sDV"], dvPT)
                            for j in range(qc + 1):
                                inst = t.matmul(ps_y[:, pq * 128:(pq + 1) * 128],
                                                vtokg[:, j * 256 + h * 64: j * 256 + (h + 1) * 64],
                                                PT[pq][:, j * 128:(j + 1) * 128],
                                                start=(j == 0), stop=(j == qc))
                            self.mark("tensor", inst)
                        blk.tensor(pe_pv)
                        pePV = self.cnt["sPE"]

                        def dv_y(v, qc=qc, mi=mi, qrow=qrow, pePV=pePV, pq=pq):
                            v.wait_ge(self.sem["sPE"], pePV)
                            self.mark("vector", v.tensor_copy(
                                yTg[mi][qrow:qrow + 64, qc * 128:(qc + 1) * 128],
                                ps_y[:, pq * 128:(pq + 1) * 128]))
                        blk.vector(dv_y)

                    ac_jobs = []
                    for qc in range(NT):
                        def pe_S(t, qc=qc, mi=mi, qrow=qrow, acH0=acH0):
                            t.wait_ge(self.sem["sAC"], acH0 + 2 * qc - 1 if qc >= 1 else acH0)
                            wts = qTg[mi][qrow:qrow + 64, qc * 128:(qc + 1) * 128]
                            nch = (qc + 1) * 128
                            for c0 in range(0, nch, 512):
                                cw = min(512, nch - c0)
                                contains_diag = (c0 + cw == nch)
                                t.matmul(ps_S[:, c0:c0 + cw],
                                         wts, kTg[mi][qrow:qrow + 64, c0:c0 + cw],
                                         start=True, stop=not contains_diag)
                            inst = t.matmul(ps_S[:, qc * 128:(qc + 1) * 128],
                                            self.maskT[:], self.identB[:],
                                            start=False, stop=True)
                            self.mark("tensor", inst)
                        blk.tensor(pe_S)
                        peS = self.cnt["sPE"]

                        def dv_max(v, qc=qc, peS=peS):
                            v.wait_ge(self.sem["sPE"], peS)
                            v.reduce_max(rmax[:], ps_S[:, 0:(qc + 1) * 128], axis=mybir.AxisListType.X)
                            self.mark("vector", v.tensor_scalar_mul(nbias[:], rmax[:], -SCALE))
                        blk.vector(dv_max)
                        dvm = self.cnt["sDV"]
                        acE = acH0 + 2 * qc + 1
                        self.cnt["sAC"] += 2

                        def dv_recip(v, qc=qc, acE=acE):
                            v.wait_ge(self.sem["sAC"], acE)
                            self.mark("vector", v.reciprocal(rrec[:, qc % 2:qc % 2 + 1], rsum[:]))
                        blk.vector(dv_recip)
                        ac_jobs.append((qc, dvm, self.cnt["sDV"]))
                        if qc >= 1:
                            emit_ptpv(qc - 1)
                    emit_ptpv(NT - 1)

                    def ac_expB(a_, jobs=tuple(ac_jobs)):
                        for qc, dvm, dvr in jobs:
                            a_.wait_ge(self.sem["sDV"], dvm)
                            inst = a_.activation(P[qc % 2][:, 0:(qc + 1) * 128], ps_S[:, 0:(qc + 1) * 128],
                                                 AF.Exp, scale=SCALE, bias=nbias[:],
                                                 accum_out=rsum[:])
                            inst.then_inc(self.sem["sAC"], 1)
                            a_.wait_ge(self.sem["sDV"], dvr)
                            inst = a_.activation(P[qc % 2][:, 0:(qc + 1) * 128], P[qc % 2][:, 0:(qc + 1) * 128],
                                                 AF.Copy, scale=rrec[:, qc % 2:qc % 2 + 1])
                            inst.then_inc(self.sem["sAC"], 1)
                            self.note("scalar", inst)
                    blk.scalar(ac_expB)

                def g_pr(g_, g=g):
                    g_.wait_ge(self.sem["sPE"], self.cnt["sPE"])
                    self.load_wchunks(g_, wpr, d["w_proj"][g * 256:(g + 1) * 256, :], 256, DM)
                blk.gpsimd(g_pr)
                wpr_c = self.cnt["sW"]
                jobs = []
                for m in range(ND):
                    for tc in range(4):
                        def mm(t, ps, m=m, tc=tc, g=g, wpr_c=wpr_c):
                            t.wait_ge(self.sem["sW"], wpr_c)
                            for kj in range(2):
                                inst = t.matmul(ps[:],
                                                wpr[:, kj * DM + m * 128: kj * DM + (m + 1) * 128],
                                                yTg[kj][:, tc * 512:(tc + 1) * 512],
                                                start=(kj == 0), stop=(kj == 1 and g != 0))
                            if g == 0:
                                inst = t.matmul(ps[:], bpr[0:1, m * 128:(m + 1) * 128],
                                                self.ones[0:1, tc * 512:(tc + 1) * 512],
                                                start=False, stop=True)
                            return inst

                        def ev(v, ps, m=m, tc=tc):
                            return v.tensor_tensor(self.resid[m][:, tc * 512:(tc + 1) * 512],
                                                   self.resid[m][:, tc * 512:(tc + 1) * 512],
                                                   ps[:], ALU.add)
                        jobs.append((mm, ev))
                mm_evac_seq(jobs)
        self.barrier()

    def phase_l0f(self):
        nc, blk, d = self.nc, self.block, self.d
        with ExitStack() as ph:
            xn = [ph.enter_context(nc.sbuf_tensor(f"f0_xn{i}", [128, T], BF16)) for i in range(ND)]
            scale_rows = ph.enter_context(nc.sbuf_tensor("f0_sr", [ND, 128], F32))
            scale_cols = ph.enter_context(nc.sbuf_tensor("f0_sc", [128, ND], F32))
            wr = ph.enter_context(nc.sbuf_tensor("f0_wr", [128, ND * NE], BF16))
            scores = ph.enter_context(nc.sbuf_tensor("f0_scores", [128, NT * NE], F32))
            top8 = ph.enter_context(nc.sbuf_tensor("f0_top8", [128, NT * NE], F32))
            dif = ph.enter_context(nc.sbuf_tensor("f0_dif", [128, NT], F32))
            w2c = ph.enter_context(nc.sbuf_tensor("f0_w2", [128, NT], F32))
            w1c = ph.enter_context(nc.sbuf_tensor("f0_w1", [128, NT], F32))
            eq2 = ph.enter_context(nc.sbuf_tensor("f0_eq2", [128, NT * NE], F32))
            msb = ph.enter_context(nc.sbuf_tensor("f0_m", [128, NT * NE], F32))
            mTb = ph.enter_context(nc.sbuf_tensor("f0_mTb", [NE, T], BF16))
            mb = [ph.enter_context(nc.sbuf_tensor(f"f0_mb{i}", [128, T], BF16)) for i in range(2)]
            sg = [ph.enter_context(nc.sbuf_tensor(f"f0_sg{i}", [128, 512], BF16)) for i in range(2)]
            hsb = [ph.enter_context(nc.sbuf_tensor(f"f0_h{i}", [128, T], BF16)) for i in range(4)]
            wu = [ph.enter_context(nc.sbuf_tensor(f"f0_wu{i}", [128, ND * 512], BF16)) for i in range(2)]
            wg = [ph.enter_context(nc.sbuf_tensor(f"f0_wg{i}", [128, ND * 512], BF16)) for i in range(2)]
            wd = [ph.enter_context(nc.sbuf_tensor(f"f0_wd{i}", [128, 4 * DM], BF16)) for i in range(2)]

            def s_pre(s):
                self.loadX(s, scale_rows[:], d["f0_scale"][:])
            blk.sync(s_pre)
            x_sr = self.cnt["sX"]

            def g_pre(g):
                self.load_wchunks(g, wr, d["f0_router"], DM, NE)
            blk.gpsimd(g_pre)

            with nc.psum_tensor("f0_ps_sc", [128, 128], F32) as ps_sc:
                def pe_sc(t):
                    t.wait_ge(self.sem["sX"], x_sr)
                    self.mark("tensor", t.transpose(ps_sc[:, 0:ND], scale_rows[:],
                                                    self.identF[0:ND, 0:ND]))
                blk.tensor(pe_sc)
                pe_sc_c = self.cnt["sPE"]

                def dv_sc(v):
                    v.wait_ge(self.sem["sPE"], pe_sc_c)
                    self.mark("vector", v.tensor_copy(scale_cols[:], ps_sc[:, 0:ND]))
                blk.vector(dv_sc)
                self.barrier()

            self.emit_ln(ph, xn, rms_scale_cols=scale_cols, skip_add=True)
            ps_rot = [ph.enter_context(nc.psum_tensor(f"f0_rot{i}", [128, 512], F32)) for i in range(4)]
            ps_d = [ph.enter_context(nc.psum_tensor(f"f0_psd{i}", [128, 512], F32)) for i in range(4)]

            pe0, dv0 = self.cnt["sPE"], self.cnt["sDV"]
            for ti in range(NT):
                r = ti % 4

                def pe_s(t, ti=ti, r=r):
                    t.wait_ge(self.sem["sDV"], dv0 + ti - 3 if ti >= 4 else dv0)
                    for kj in range(ND):
                        inst = t.matmul(ps_rot[r][:, 0:NE], xn[kj][:, ti * 128:(ti + 1) * 128],
                                        wr[:, kj * NE:(kj + 1) * NE],
                                        start=(kj == 0), stop=(kj == ND - 1))
                    self.mark("tensor", inst)
                blk.tensor(pe_s)
                pec = self.cnt["sPE"]

                def dv_s(v, ti=ti, r=r, pec=pec):
                    v.wait_ge(self.sem["sPE"], pec)
                    self.mark("vector", v.tensor_copy(scores[:, ti * NE:(ti + 1) * NE], ps_rot[r][:, 0:NE]))
                blk.vector(dv_s)

            def dv_g1(v):
                for ti in range(NT):
                    v.max(top8[:, ti * NE:(ti + 1) * NE], scores[:, ti * NE:(ti + 1) * NE])
                t8v = top8[:].rearrange("p (ti e) -> p ti e", ti=NT)
                inst = v.tensor_tensor(dif[:], t8v[:, :, 1], t8v[:, :, 0], ALU.subtract)
                self.mark("vector", inst)
            blk.vector(dv_g1)
            dvg1 = self.cnt["sDV"]

            def ac_sig(a):
                a.wait_ge(self.sem["sDV"], dvg1)
                self.mark("scalar", a.activation(w2c[:], dif[:], AF.Sigmoid))
            blk.scalar(ac_sig)
            acsig = self.cnt["sAC"]

            def dv_g1b(v):
                v.wait_ge(self.sem["sAC"], acsig)
                t8v = top8[:].rearrange("p (ti e) -> p ti e", ti=NT)
                self.mark("vector", v.tensor_tensor(dif[:], t8v[:, :, 0], t8v[:, :, 1], ALU.subtract))
            blk.vector(dv_g1b)
            dvg1b = self.cnt["sDV"]

            def ac_sig2(a):
                a.wait_ge(self.sem["sDV"], dvg1b)
                self.mark("scalar", a.activation(w1c[:], dif[:], AF.Sigmoid))
            blk.scalar(ac_sig2)
            acsig2 = self.cnt["sAC"]

            def dv_g2(v):
                v.wait_ge(self.sem["sAC"], acsig2)
                for ti in range(NT):
                    v.tensor_scalar(msb[:, ti * NE:(ti + 1) * NE], scores[:, ti * NE:(ti + 1) * NE],
                                    top8[:, ti * NE:ti * NE + 1], None, ALU.is_equal)
                for ti in range(NT):
                    v.tensor_scalar(eq2[:, ti * NE:(ti + 1) * NE], scores[:, ti * NE:(ti + 1) * NE],
                                    top8[:, ti * NE + 1:ti * NE + 2], None, ALU.is_equal)
                for ti in range(NT):
                    v.tensor_scalar(msb[:, ti * NE:(ti + 1) * NE], msb[:, ti * NE:(ti + 1) * NE],
                                    w1c[:, ti:ti + 1], None, ALU.mult)
                for ti in range(NT):
                    v.tensor_scalar(eq2[:, ti * NE:(ti + 1) * NE], eq2[:, ti * NE:(ti + 1) * NE],
                                    w2c[:, ti:ti + 1], None, ALU.mult)
                inst = v.tensor_tensor(msb[:], msb[:], eq2[:], ALU.add)
                self.mark("vector", inst)
            blk.vector(dv_g2)
            dvg2 = self.cnt["sDV"]

            for tq in range(4):
                def pe_mt(t, tq=tq):
                    t.wait_ge(self.sem["sDV"], dvg2 if tq == 0 else self.cnt["sDV"])
                    for j in range(4):
                        ti = tq * 4 + j
                        inst = t.transpose(ps_rot[0][0:NE, j * 128:(j + 1) * 128],
                                           msb[:, ti * NE:(ti + 1) * NE], self.identF[:])
                    self.mark("tensor", inst)
                blk.tensor(pe_mt)
                pemt = self.cnt["sPE"]

                def dv_mt(v, tq=tq, pemt=pemt):
                    v.wait_ge(self.sem["sPE"], pemt)
                    self.mark("vector", v.tensor_copy(mTb[0:NE, tq * 512:(tq + 1) * 512],
                                                      ps_rot[0][0:NE, 0:512]))
                blk.vector(dv_mt)

            if self.debug:
                dvd = self.cnt["sDV"]

                def s_dbg(s):
                    s.wait_ge(self.sem["sDV"], dvd)
                    self.loadX(s, d["dbg_a"][:, 0:NT * NE], scores[:])
                    self.loadX(s, d["dbg_b"][:, 0:NT * NE], msb[:])
                    for dj in range(ND):
                        self.loadX(s, d["dbg_xn"][dj * 128:(dj + 1) * 128, :], xn[dj][:])
                blk.sync(s_dbg)

            slab_pe_done = {}
            slab_w = {}
            self.dv_psd_free = 0
            sg_hist = []

            def load_slab(i):
                e, j = divmod(i, 4)
                p = i % 2

                def g_l(g, e=e, j=j, p=p, i=i):
                    if i >= 2:
                        g.wait_ge(self.sem["sPE"], slab_pe_done[i - 2])
                    self.load_wchunks(g, wu[p], d["f0_up"][e][:, j * 512:(j + 1) * 512], DM, 512)
                    self.load_wchunks(g, wg[p], d["f0_up"][e][:, DI + j * 512:DI + (j + 1) * 512], DM, 512)
                    self.load_wchunks(g, wd[p], d["f0_down"][e][j * 512:(j + 1) * 512, :], 512, DM)
                blk.gpsimd(g_l)
                slab_w[i] = self.cnt["sW"]

            load_slab(0)
            load_slab(1)

            for e in range(NE):
                pb = e % 2
                dv_mbw = self.cnt["sDV"]
                for tc in range(4):
                    def pe_mb(t, e=e, tc=tc, dv_mbw=dv_mbw):
                        t.wait_ge(self.sem["sDV"], dv_mbw)
                        self.mark("tensor", t.matmul(ps_rot[tc][:], self.sel8[:, e * 128:(e + 1) * 128],
                                                     mTb[0:NE, tc * 512:(tc + 1) * 512],
                                                     start=True, stop=True))
                    blk.tensor(pe_mb)
                    pemb = self.cnt["sPE"]

                    def dv_mb(v, e=e, tc=tc, pemb=pemb, pb=pb):
                        v.wait_ge(self.sem["sPE"], pemb)
                        self.mark("vector", v.tensor_copy(mb[pb][:, tc * 512:(tc + 1) * 512], ps_rot[tc][:]))
                    blk.vector(dv_mb)

                dv_xre_c = self.cnt["sDV"]

                for j in range(4):
                    i = 4 * e + j
                    p = i % 2
                    pe0, ac0, dv0 = self.cnt["sPE"], self.cnt["sAC"], self.cnt["sDV"]

                    def pe_uB(t, p=p, i=i, pe0=pe0, dv0=dv0):
                        t.wait_ge(self.sem["sW"], slab_w[i])
                        for it in range(16):
                            mj, tc = divmod(it, 4)
                            rot = it % 2
                            t.wait_ge(self.sem["sDV"], dv0 + it - 1 if it >= 2 else dv0)
                            for kj in range(ND):
                                t.matmul(ps_rot[rot * 2][:],
                                         wu[p][:, kj * 512 + mj * 128: kj * 512 + (mj + 1) * 128],
                                         xn[kj][:, tc * 512:(tc + 1) * 512],
                                         start=(kj == 0), stop=(kj == ND - 1))
                            for kj in range(ND):
                                inst = t.matmul(ps_rot[rot * 2 + 1][:],
                                                wg[p][:, kj * 512 + mj * 128: kj * 512 + (mj + 1) * 128],
                                                xn[kj][:, tc * 512:(tc + 1) * 512],
                                                start=(kj == 0), stop=(kj == ND - 1))
                            self.mark("tensor", inst)
                    blk.tensor(pe_uB)

                    def ac_siluB(a, pe0=pe0, dv0=dv0):
                        for it in range(16):
                            rot = it % 2
                            a.wait_ge(self.sem["sPE"], pe0 + it + 1)
                            a.wait_ge(self.sem["sDV"], dv0 + it - 1 if it >= 2 else dv0)
                            self.mark("scalar", a.activation(sg[it % 2][:], ps_rot[rot * 2 + 1][:], AF.Silu))
                    blk.scalar(ac_siluB)

                    def dv_hB(v, ac0=ac0, pb=pb):
                        for it in range(16):
                            mj, tc = divmod(it, 4)
                            rot = it % 2
                            v.wait_ge(self.sem["sAC"], ac0 + it + 1)
                            hs = hsb[mj][:, tc * 512:(tc + 1) * 512]
                            v.tensor_tensor(hs, ps_rot[rot * 2][:], sg[it % 2][:], ALU.mult)
                            inst = v.tensor_tensor(hs, hs, mb[pb][:, tc * 512:(tc + 1) * 512], ALU.mult)
                            self.mark("vector", inst)
                    blk.vector(dv_hB)

                    peD0, dvD0 = self.cnt["sPE"], self.cnt["sDV"]

                    def pe_dB(t, p=p, peD0=peD0, dvD0=dvD0):
                        for idx in range(16):
                            tc, mg = divmod(idx, 4)
                            pr = 2 * (idx % 2)
                            t.wait_ge(self.sem["sDV"], dvD0 + idx - 1 if idx >= 2 else dvD0)
                            for mi in range(2):
                                m = mg * 2 + mi
                                for kj in range(4):
                                    inst = t.matmul(
                                        ps_d[pr + mi][:],
                                        wd[p][:, kj * DM + m * 128:kj * DM + (m + 1) * 128],
                                        hsb[kj][:, tc * 512:(tc + 1) * 512],
                                        start=(kj == 0), stop=(kj == 3))
                            self.mark("tensor", inst)
                    blk.tensor(pe_dB)

                    def dv_addB(v, peD0=peD0):
                        for idx in range(16):
                            tc, mg = divmod(idx, 4)
                            pr = 2 * (idx % 2)
                            v.wait_ge(self.sem["sPE"], peD0 + idx + 1)
                            for mi in range(2):
                                m = mg * 2 + mi
                                inst = v.tensor_tensor(
                                    self.resid[m][:, tc * 512:(tc + 1) * 512],
                                    self.resid[m][:, tc * 512:(tc + 1) * 512],
                                    ps_d[pr + mi][:], ALU.add)
                            self.mark("vector", inst)
                    blk.vector(dv_addB)

                    slab_pe_done[i] = self.cnt["sPE"]
                    if i + 2 < 4 * NE:
                        load_slab(i + 2)
        self.barrier()

    def phase_l1a(self):
        nc, blk, d = self.nc, self.block, self.d
        with ExitStack() as ph:
            wbig = ph.enter_context(nc.sbuf_tensor("a1_wbig", [128, 8 * 1024], BF16))
            BTb = ph.enter_context(nc.sbuf_tensor("a1_BTb", [DS, T], BF16))
            CTb = ph.enter_context(nc.sbuf_tensor("a1_CTb", [DS, T], BF16))
            bin_sb = ph.enter_context(nc.sbuf_tensor("a1_binb", [1, DI], BF16))
            bx_sb = ph.enter_context(nc.sbuf_tensor("a1_bxb", [1, 2 * DS], BF16))
            acol = ph.enter_context(nc.sbuf_tensor("a1_acol", [128, NDI * DS], F32))
            dcol = ph.enter_context(nc.sbuf_tensor("a1_dcol", [128, NDI], F32))
            carry = ph.enter_context(nc.sbuf_tensor("a1_carry", [128, NDI * DS], F32))
            ps_rot = [ph.enter_context(nc.psum_tensor(f"a1_rot{i}", [128, 512], F32)) for i in range(4)]

            def g_pre(g):
                self.loadW(g, bx_sb[:], d["a1_bx"][:])
            blk.gpsimd(g_pre)

            drows_cm = nc.sbuf_tensor("a1_drows", [NDI, 128], F32)
            drows = drows_cm.__enter__()

            def s_pre(s):
                self.loadX(s, drows[:], d["a1_D"][:])
                self.loadX(s, acol[:].rearrange("p (db s) -> p db s", db=NDI),
                           d["a1_Alog"][:].rearrange("(db p) s -> p db s", p=128))
            blk.sync(s_pre)
            x_pre = self.cnt["sX"]

            def ac_a(a):
                a.wait_ge(self.sem["sX"], x_pre)
                self.mark("scalar", a.activation(acol[:], acol[:], AF.Exp))
            blk.scalar(ac_a)
            ac_a_c = self.cnt["sAC"]

            def dv_a(v):
                v.wait_ge(self.sem["sAC"], ac_a_c)
                v.tensor_scalar_mul(acol[:], acol[:], -1.0)
                self.mark("vector", v.memset(carry[:], 0.0))
            blk.vector(dv_a)

            def pe_dc(t):
                t.wait_ge(self.sem["sX"], x_pre)
                self.mark("tensor", t.transpose(ps_rot[0][:, 0:NDI], drows[:],
                                                self.identF[0:NDI, 0:NDI]))
            blk.tensor(pe_dc)
            pedc = self.cnt["sPE"]

            def dv_dc(v):
                v.wait_ge(self.sem["sPE"], pedc)
                self.mark("vector", v.tensor_copy(dcol[:], ps_rot[0][:, 0:NDI]))
            blk.vector(dv_dc)
            self.barrier()
            drows_cm.__exit__(None, None, None)

            with ExitStack() as ph0:
                uT = [ph0.enter_context(nc.sbuf_tensor(f"a1_u{i}", [128, T], BF16)) for i in range(NDI)]
                szt = [ph0.enter_context(nc.sbuf_tensor(f"a1_szt{i}", [128, 512], BF16)) for i in range(2)]

                with ExitStack() as ph1:
                    xn = [ph1.enter_context(nc.sbuf_tensor(f"a1_xn{i}", [128, T], BF16)) for i in range(ND)]
                    self.emit_ln(ph1, xn, scratch=wbig[:].bitcast(F32))

                    for half, (cofs, spill) in enumerate([(DI, "sp_sz"), (0, "sp_u")]):
                        for q in range(4):
                            def g_w(g, cofs=cofs, q=q):
                                g.wait_ge(self.sem["sPE"], self.cnt["sPE"])
                                self.load_wchunks(g, wbig,
                                                  d["a1_Win"][:, cofs + q * 512: cofs + (q + 1) * 512],
                                                  DM, 512)
                                if q == 0:
                                    self.loadW(g, bin_sb[:], d["a1_bin"][:, cofs:cofs + DI])
                            blk.gpsimd(g_w)
                            w_c = self.cnt["sW"]
                            pe0, ac0, x0 = self.cnt["sPE"], self.cnt["sAC"], self.cnt["sX"]

                            def pe_bB(t, q=q, cofs=cofs, w_c=w_c, pe0=pe0, ac0=ac0):
                                t.wait_ge(self.sem["sW"], w_c)
                                for it in range(16):
                                    mjl, tc = divmod(it, 4)
                                    mj = q * 4 + mjl
                                    r = it % 2
                                    if it >= 2:
                                        t.wait_ge(self.sem["sAC"], ac0 + it - 1)
                                    for kj in range(ND):
                                        t.matmul(ps_rot[r][:],
                                                 wbig[:, kj * 512 + mjl * 128: kj * 512 + (mjl + 1) * 128],
                                                 xn[kj][:, tc * 512:(tc + 1) * 512],
                                                 start=(kj == 0), stop=False)
                                    inst = t.matmul(ps_rot[r][:],
                                                    bin_sb[0:1, mj * 128:(mj + 1) * 128],
                                                    self.ones[0:1, tc * 512:(tc + 1) * 512],
                                                    start=False, stop=True)
                                    self.mark("tensor", inst)
                            blk.tensor(pe_bB)

                            if half == 0:
                                def ac_evB(a_, q=q, pe0=pe0, x0=x0):
                                    for it in range(16):
                                        r = it % 2
                                        a_.wait_ge(self.sem["sPE"], pe0 + it + 1)
                                        a_.wait_ge(self.sem["sX"], x0 + 16 * (it - 1) if it >= 2 else x0)
                                        self.mark("scalar", a_.activation(szt[it % 2][:],
                                                                          ps_rot[r][:], AF.Silu))
                                blk.scalar(ac_evB)

                                def s_spB(s, q=q, ac0=ac0):
                                    for it in range(16):
                                        mjl, tc = divmod(it, 4)
                                        mj = q * 4 + mjl
                                        s.wait_ge(self.sem["sAC"], ac0 + it + 1)
                                        self.loadX(s, d["sp_sz"][mj * 128:(mj + 1) * 128,
                                                                 tc * 512:(tc + 1) * 512],
                                                   szt[it % 2][:])
                                blk.sync(s_spB)
                            else:
                                def ac_evB(a_, q=q, pe0=pe0):
                                    for it in range(16):
                                        mjl, tc = divmod(it, 4)
                                        mj = q * 4 + mjl
                                        r = it % 2
                                        a_.wait_ge(self.sem["sPE"], pe0 + it + 1)
                                        self.mark("scalar", a_.activation(
                                            uT[mj][:, tc * 512:(tc + 1) * 512],
                                            ps_rot[r][:], AF.Silu))
                                blk.scalar(ac_evB)
                                ac_q = self.cnt["sAC"]

                                def s_spB(s, q=q, ac_q=ac_q):
                                    for mjl in range(4):
                                        mj = q * 4 + mjl
                                        s.wait_ge(self.sem["sAC"], ac_q - (3 - mjl) * 4)
                                        self.loadX(s, d["sp_u"][mj * 128:(mj + 1) * 128, :], uT[mj][:])
                                blk.sync(s_spB)
                    self.barrier()

                wx = self.sb_local(ph0, "a1_wx", [128, NDI * 2 * DS], BF16)

                def g_wx(g):
                    self.load_wchunks(g, wx, d["a1_Wx"], DI, 2 * DS)
                blk.gpsimd(g_wx)
                wx_c = self.cnt["sW"]
                for tc in range(4):
                    dv_prev = self.cnt["sDV"]

                    def pe_bc(t, tc=tc, dv_prev=dv_prev):
                        t.wait_ge(self.sem["sW"], wx_c)
                        t.wait_ge(self.sem["sDV"], dv_prev)
                        for part, c0 in ((0, 0), (1, DS)):
                            for kj in range(NDI):
                                t.matmul(ps_rot[part][0:DS, :],
                                         wx[:, kj * 2 * DS + c0: kj * 2 * DS + c0 + DS],
                                         uT[kj][:, tc * 512:(tc + 1) * 512],
                                         start=(kj == 0), stop=False)
                            inst = t.matmul(ps_rot[part][0:DS, :],
                                            bx_sb[0:1, c0:c0 + DS],
                                            self.ones[0:1, tc * 512:(tc + 1) * 512],
                                            start=False, stop=True)
                        self.mark("tensor", inst)
                    blk.tensor(pe_bc)
                    pebc = self.cnt["sPE"]

                    def dv_bc(v, tc=tc, pebc=pebc):
                        v.wait_ge(self.sem["sPE"], pebc)
                        v.tensor_copy(BTb[0:DS, tc * 512:(tc + 1) * 512], ps_rot[0][0:DS, :])
                        inst = v.tensor_copy(CTb[0:DS, tc * 512:(tc + 1) * 512], ps_rot[1][0:DS, :])
                        self.mark("vector", inst)
                    blk.vector(dv_bc)

                bdt_sb = self.sb_local(ph0, "a1_bdtb", [1, DI], BF16)

                def g_bdt(g):
                    self.loadW(g, bdt_sb[:], d["a1_bdt"][:])
                blk.gpsimd(g_bdt)
                dtt = [self.sb_local(ph0, f"a1_dtt{i}", [128, T], BF16) for i in range(2)]
                dut = [self.sb_local(ph0, f"a1_dut{i}", [128, T], BF16) for i in range(2)]
                et = [self.sb_local(ph0, f"a1_et{i}", [128, 512], F32) for i in range(2)]
                for mh in range(4):
                    def g_wdt(g, mh=mh):
                        g.wait_ge(self.sem["sPE"], self.cnt["sPE"])
                        self.load_wchunks(g, wbig, d["a1_Wdt"][:, mh * 512:(mh + 1) * 512], DI, 512)
                    blk.gpsimd(g_wdt)
                    wdt_c = self.cnt["sW"]
                    peE0, acE0, dvE0, xE0 = (self.cnt["sPE"], self.cnt["sAC"],
                                             self.cnt["sDV"], self.cnt["sX"])

                    def pe_dtB(t, mh=mh, wdt_c=wdt_c, dvE0=dvE0):
                        t.wait_ge(self.sem["sW"], wdt_c)
                        for it in range(16):
                            mjh, tc = divmod(it, 4)
                            mj = mh * 4 + mjh
                            r = tc % 2
                            t.wait_ge(self.sem["sDV"], dvE0 + it - 1 if it >= 2 else dvE0)
                            for kj in range(NDI):
                                t.matmul(ps_rot[r][:],
                                         wbig[:, kj * 512 + mjh * 128: kj * 512 + (mjh + 1) * 128],
                                         uT[kj][:, tc * 512:(tc + 1) * 512],
                                         start=(kj == 0), stop=False)
                            inst = t.matmul(ps_rot[r][:],
                                            bdt_sb[0:1, mj * 128:(mj + 1) * 128],
                                            self.ones[0:1, tc * 512:(tc + 1) * 512],
                                            start=False, stop=True)
                            self.mark("tensor", inst)
                    blk.tensor(pe_dtB)

                    def ac_spB(a_, mh=mh, peE0=peE0, xE0=xE0, dvE0=dvE0):
                        for it in range(16):
                            mjh, tc = divmod(it, 4)
                            pp = (mh * 4 + mjh) % 2
                            r = tc % 2
                            a_.wait_ge(self.sem["sPE"], peE0 + it + 1)
                            if mjh >= 2 and tc == 0:
                                a_.wait_ge(self.sem["sX"], xE0 + 32 * (mjh - 1))
                            elif mjh < 2 and tc == 0:
                                a_.wait_ge(self.sem["sX"], xE0)
                            a_.activation(et[it % 2][:], ps_rot[r][:], AF.Exp)
                            self.mark("scalar", a_.activation(dtt[pp][:, tc * 512:(tc + 1) * 512],
                                                              et[it % 2][:], AF.Ln, bias=1.0))
                    blk.scalar(ac_spB)

                    def dv_duB(v, mh=mh, acE0=acE0):
                        for it in range(16):
                            mjh, tc = divmod(it, 4)
                            mj = mh * 4 + mjh
                            pp = mj % 2
                            v.wait_ge(self.sem["sAC"], acE0 + it + 1)
                            inst = v.tensor_tensor(dut[pp][:, tc * 512:(tc + 1) * 512],
                                                   dtt[pp][:, tc * 512:(tc + 1) * 512],
                                                   uT[mj][:, tc * 512:(tc + 1) * 512], ALU.mult)
                            self.mark("vector", inst)
                    blk.vector(dv_duB)

                    def s_spdtB(s, mh=mh, dvE0=dvE0):
                        for mjh in range(4):
                            mj = mh * 4 + mjh
                            pp = mj % 2
                            s.wait_ge(self.sem["sDV"], dvE0 + (mjh + 1) * 4)
                            self.loadX(s, d["sp_dt"][mj * 128:(mj + 1) * 128, :], dtt[pp][:])
                            self.loadX(s, d["sp_dtu"][mj * 128:(mj + 1) * 128, :], dut[pp][:])
                    blk.sync(s_spdtB)
                self.barrier()


            bout_sb = self.sb_local(ph, "a1_boutb", [1, DM], BF16)

            def g_bout(g):
                self.loadW(g, bout_sb[:], d["a1_bout"][:])
            blk.gpsimd(g_bout)
            Bball = self.sb_local(ph, "a1_Bball", [128, DS * 512], BF16)
            Cball = self.sb_local(ph, "a1_Cball", [128, DS * 512], BF16)
            Xall = self.sb_local(ph, "a1_Xall", [128, DS * 512], BF16)
            hall = self.sb_local(ph, "a1_hall", [128, DS * 512], BF16)
            dA = [self.sb_local(ph, f"a1_dA{i}", [128, 512], BF16) for i in range(2)]
            ygT = [self.sb_local(ph, f"a1_yg{i}", [128, 512], BF16) for i in range(NDI)]
            sl_dt = [self.sb_local(ph, f"a1_sldt{i}", [128, 512], BF16) for i in range(2)]
            sl_du = [self.sb_local(ph, f"a1_sldu{i}", [128, 512], BF16) for i in range(2)]
            sl_u = [self.sb_local(ph, f"a1_slu{i}", [128, 512], BF16) for i in range(2)]
            sl_sz = [self.sb_local(ph, f"a1_slsz{i}", [128, 512], BF16) for i in range(2)]

            dv_slice_done = {}
            for tc in range(4):
                dv_bc_prev = self.cnt["sDV"]
                for s16 in range(DS):
                    for which, (ball, src) in enumerate(((Bball, BTb), (Cball, CTb))):
                        r = (s16 * 2 + which) % 2

                        def pe_bb(t, s16=s16, r=r, src=src, dv_bc_prev=dv_bc_prev):
                            t.wait_ge(self.sem["sDV"], dv_bc_prev if s16 < 1 else self.cnt_sDV_bb[-2])
                            self.mark("tensor", t.matmul(
                                ps_rot[r][:], self.sel16[:, s16 * 128:(s16 + 1) * 128],
                                src[0:DS, tc * 512:(tc + 1) * 512], start=True, stop=True))
                        blk.tensor(pe_bb)
                        pebb = self.cnt["sPE"]

                        def dv_bb(v, s16=s16, r=r, ball=ball, pebb=pebb):
                            v.wait_ge(self.sem["sPE"], pebb)
                            self.mark("vector", v.tensor_copy(ball[:, s16 * 512:(s16 + 1) * 512],
                                                              ps_rot[r][:]))
                        blk.vector(dv_bb)
                        self.cnt_sDV_bb.append(self.cnt["sDV"])

                for db in range(NDI):
                    pp = db % 2
                    prev = dv_slice_done.get((tc, db - 2), dv_slice_done.get((tc - 1, NDI - 2 + db), 0))

                    def s_sl(s, tc=tc, db=db, pp=pp, prev=prev):
                        s.wait_ge(self.sem["sDV"], prev)
                        csl = slice(tc * 512, (tc + 1) * 512)
                        self.loadX(s, sl_dt[pp][:], d["sp_dt"][db * 128:(db + 1) * 128, csl])
                        self.loadX(s, sl_du[pp][:], d["sp_dtu"][db * 128:(db + 1) * 128, csl])
                        self.loadX(s, sl_u[pp][:], d["sp_u"][db * 128:(db + 1) * 128, csl])
                        self.loadX(s, sl_sz[pp][:], d["sp_sz"][db * 128:(db + 1) * 128, csl])
                    blk.sync(s_sl)
                    x_sl = self.cnt["sX"]

                    dvDB0, acDB0 = self.cnt["sDV"], self.cnt["sAC"]

                    def ac_daB(a_, tc=tc, db=db, pp=pp, x_sl=x_sl, dvDB0=dvDB0):
                        a_.wait_ge(self.sem["sX"], x_sl)
                        for s16 in range(DS):
                            a_.wait_ge(self.sem["sDV"], dvDB0 + s16 if s16 >= 2 else dvDB0)
                            inst = a_.activation(dA[s16 % 2][:], sl_dt[pp][:], AF.Exp,
                                                 scale=acol[:, db * DS + s16: db * DS + s16 + 1])
                            self.mark("scalar", inst)
                    blk.scalar(ac_daB)

                    def dv_dbB(v, tc=tc, db=db, pp=pp, x_sl=x_sl, acDB0=acDB0):
                        v.wait_ge(self.sem["sX"], x_sl)
                        rep = sl_du[pp][:].unsqueeze(1).broadcast_to([128, DS, 512])
                        inst = v.tensor_tensor(
                            Xall[:].rearrange("p (s t) -> p s t", s=DS), rep,
                            Bball[:].rearrange("p (s t) -> p s t", s=DS), ALU.mult)
                        self.mark("vector", inst)
                        for s16 in range(DS):
                            v.wait_ge(self.sem["sAC"], acDB0 + s16 + 1)
                            inst = v.tensor_tensor_scan(
                                hall[:, s16 * 512:(s16 + 1) * 512], dA[s16 % 2][:],
                                Xall[:, s16 * 512:(s16 + 1) * 512],
                                carry[:, db * DS + s16: db * DS + s16 + 1],
                                ALU.mult, ALU.add)
                            self.mark("vector", inst)
                        v.tensor_copy(carry[:, db * DS:(db + 1) * DS],
                                      hall[:].rearrange("p (s t) -> p s t", s=DS)[:, :, 511])
                        v.tensor_tensor(Xall[:].rearrange("p (s t) -> p s t", s=DS),
                                        hall[:].rearrange("p (s t) -> p s t", s=DS),
                                        Cball[:].rearrange("p (s t) -> p s t", s=DS), ALU.mult)
                        v.tensor_tensor(Xall[:, 0:8 * 512], Xall[:, 0:8 * 512], Xall[:, 8 * 512:16 * 512], ALU.add)
                        v.tensor_tensor(Xall[:, 0:4 * 512], Xall[:, 0:4 * 512], Xall[:, 4 * 512:8 * 512], ALU.add)
                        v.tensor_tensor(Xall[:, 0:2 * 512], Xall[:, 0:2 * 512], Xall[:, 2 * 512:4 * 512], ALU.add)
                        v.tensor_tensor(Xall[:, 0:512], Xall[:, 0:512], Xall[:, 512:1024], ALU.add)
                        v.scalar_tensor_tensor(Xall[:, 0:512], sl_u[pp][:], dcol[:, db:db + 1],
                                               Xall[:, 0:512], ALU.mult, ALU.add)
                        inst = v.tensor_tensor(ygT[db][:], Xall[:, 0:512], sl_sz[pp][:], ALU.mult)
                        self.mark("vector", inst)
                    blk.vector(dv_dbB)
                    dv_slice_done[(tc, db)] = self.cnt["sDV"]

                dv_yg = self.cnt["sDV"]
                for mh in range(2):
                    def g_wo(g, mh=mh):
                        g.wait_ge(self.sem["sPE"], self.cnt["sPE"])
                        self.load_wchunks(g, wbig, d["a1_Wout"][:, mh * 512:(mh + 1) * 512], DI, 512)
                    blk.gpsimd(g_wo)
                    wo_c = self.cnt["sW"]
                    for ml in range(4):
                        m = mh * 4 + ml
                        r = m % 2
                        dvw = self.wout_dv_hist[-2] if len(self.wout_dv_hist) >= 2 else 0

                        def pe_wo(t, m=m, ml=ml, r=r, tc=tc, dv_yg=dv_yg, dvw=dvw, wo_c=wo_c):
                            t.wait_ge(self.sem["sW"], wo_c)
                            t.wait_ge(self.sem["sDV"], max(dv_yg, dvw))
                            for kj in range(NDI):
                                t.matmul(ps_rot[2 + r][:],
                                         wbig[:, kj * 512 + ml * 128: kj * 512 + (ml + 1) * 128],
                                         ygT[kj][:], start=(kj == 0), stop=False)
                            inst = t.matmul(ps_rot[2 + r][:],
                                            bout_sb[0:1, m * 128:(m + 1) * 128],
                                            self.ones[0:1, tc * 512:(tc + 1) * 512],
                                            start=False, stop=True)
                            self.mark("tensor", inst)
                        blk.tensor(pe_wo)
                        pewo = self.cnt["sPE"]

                        def dv_wo(v, m=m, r=r, tc=tc, pewo=pewo):
                            v.wait_ge(self.sem["sPE"], pewo)
                            inst = v.tensor_tensor(self.resid[m][:, tc * 512:(tc + 1) * 512],
                                                   self.resid[m][:, tc * 512:(tc + 1) * 512],
                                                   ps_rot[2 + r][:], ALU.add)
                            self.mark("vector", inst)
                        blk.vector(dv_wo)
                        self.wout_dv_hist.append(self.cnt["sDV"])
        self.barrier()

    def phase_l1f(self):
        nc, blk, d = self.nc, self.block, self.d
        with ExitStack() as ph:
            xn = [ph.enter_context(nc.sbuf_tensor(f"f1_xn{i}", [128, T], BF16)) for i in range(ND)]
            wr = ph.enter_context(nc.sbuf_tensor("f1_wr", [128, ND * NE], BF16))
            scores = ph.enter_context(nc.sbuf_tensor("f1_scores", [128, NT * NE], F32))
            top8 = ph.enter_context(nc.sbuf_tensor("f1_top8", [128, NT * NE], F32))
            dif = ph.enter_context(nc.sbuf_tensor("f1_dif", [128, NT], F32))
            w2c = ph.enter_context(nc.sbuf_tensor("f1_w2", [128, NT], F32))
            w1c = ph.enter_context(nc.sbuf_tensor("f1_w1", [128, NT], F32))
            eq2 = ph.enter_context(nc.sbuf_tensor("f1_eq2", [128, NT * NE], F32))
            msb = ph.enter_context(nc.sbuf_tensor("f1_m", [128, NT * NE], F32))
            mTb = ph.enter_context(nc.sbuf_tensor("f1_mTb", [NE, T], BF16))
            mb = [ph.enter_context(nc.sbuf_tensor(f"f1_mb{i}", [128, T], BF16)) for i in range(2)]
            upbr = ph.enter_context(nc.sbuf_tensor("f1_upbr", [NE * ND, 128], BF16))
            upbc = ph.enter_context(nc.sbuf_tensor("f1_upbc", [128, NE * ND], F32))
            dnb = ph.enter_context(nc.sbuf_tensor("f1_dnb", [NE, DM], BF16))
            shupb = ph.enter_context(nc.sbuf_tensor("f1_shupbb", [1, H1], BF16))
            shdnb = ph.enter_context(nc.sbuf_tensor("f1_shdnbb", [1, DM], BF16))
            wA = ph.enter_context(nc.sbuf_tensor("f1_wA", [128, ND * H1], BF16))
            wB = ph.enter_context(nc.sbuf_tensor("f1_wB", [128, ND * DM], BF16))
            hbuf = [ph.enter_context(nc.sbuf_tensor(f"f1_h{i}", [128, T], BF16)) for i in range(ND)]

            def g_pre(g):
                self.load_wchunks(g, wr, d["f1_gate"], DM, NE)
                self.loadW(g, dnb[:], d["f1_downb"][:])
                self.loadW(g, shupb[:], d["f1_shupb"][:])
                self.loadW(g, shdnb[:], d["f1_shdownb"][:])
                self.load_wchunks(g, wA, d["f1_shup"], DM, H1)
                self.load_wchunks(g, wB, d["f1_shdown"], H1, DM)
            blk.gpsimd(g_pre)
            w_pre = self.cnt["sW"]

            self.emit_ln(ph, xn, scratch=wA[:].bitcast(F32))
            ps_rot = [ph.enter_context(nc.psum_tensor(f"f1_rot{i}", [128, 512], F32)) for i in range(4)]
            ps_d = [ph.enter_context(nc.psum_tensor(f"f1_psd{i}", [128, 512], F32)) for i in range(4)]

            pe0, dv0 = self.cnt["sPE"], self.cnt["sDV"]
            for ti in range(NT):
                r = ti % 4

                def pe_s(t, ti=ti, r=r):
                    t.wait_ge(self.sem["sW"], w_pre)
                    t.wait_ge(self.sem["sDV"], dv0 + ti - 3 if ti >= 4 else dv0)
                    for kj in range(ND):
                        inst = t.matmul(ps_rot[r][:, 0:NE], xn[kj][:, ti * 128:(ti + 1) * 128],
                                        wr[:, kj * NE:(kj + 1) * NE],
                                        start=(kj == 0), stop=(kj == ND - 1))
                    self.mark("tensor", inst)
                blk.tensor(pe_s)
                pec = self.cnt["sPE"]

                def dv_s(v, ti=ti, r=r, pec=pec):
                    v.wait_ge(self.sem["sPE"], pec)
                    self.mark("vector", v.tensor_copy(scores[:, ti * NE:(ti + 1) * NE], ps_rot[r][:, 0:NE]))
                blk.vector(dv_s)

            def dv_g1(v):
                for ti in range(NT):
                    v.max(top8[:, ti * NE:(ti + 1) * NE], scores[:, ti * NE:(ti + 1) * NE])
                t8v = top8[:].rearrange("p (ti e) -> p ti e", ti=NT)
                inst = v.tensor_tensor(dif[:], t8v[:, :, 1], t8v[:, :, 0], ALU.subtract)
                self.mark("vector", inst)
            blk.vector(dv_g1)
            dvg1 = self.cnt["sDV"]

            def ac_sig(a):
                a.wait_ge(self.sem["sDV"], dvg1)
                self.mark("scalar", a.activation(w2c[:], dif[:], AF.Sigmoid))
            blk.scalar(ac_sig)
            acsig = self.cnt["sAC"]

            def dv_g1b(v):
                v.wait_ge(self.sem["sAC"], acsig)
                t8v = top8[:].rearrange("p (ti e) -> p ti e", ti=NT)
                self.mark("vector", v.tensor_tensor(dif[:], t8v[:, :, 0], t8v[:, :, 1], ALU.subtract))
            blk.vector(dv_g1b)
            dvg1b = self.cnt["sDV"]

            def ac_sig2(a):
                a.wait_ge(self.sem["sDV"], dvg1b)
                self.mark("scalar", a.activation(w1c[:], dif[:], AF.Sigmoid))
            blk.scalar(ac_sig2)
            acsig2 = self.cnt["sAC"]

            def dv_g2(v):
                v.wait_ge(self.sem["sAC"], acsig2)
                for ti in range(NT):
                    v.tensor_scalar(msb[:, ti * NE:(ti + 1) * NE], scores[:, ti * NE:(ti + 1) * NE],
                                    top8[:, ti * NE:ti * NE + 1], None, ALU.is_equal)
                for ti in range(NT):
                    v.tensor_scalar(eq2[:, ti * NE:(ti + 1) * NE], scores[:, ti * NE:(ti + 1) * NE],
                                    top8[:, ti * NE + 1:ti * NE + 2], None, ALU.is_equal)
                for ti in range(NT):
                    v.tensor_scalar(msb[:, ti * NE:(ti + 1) * NE], msb[:, ti * NE:(ti + 1) * NE],
                                    w1c[:, ti:ti + 1], None, ALU.mult)
                for ti in range(NT):
                    v.tensor_scalar(eq2[:, ti * NE:(ti + 1) * NE], eq2[:, ti * NE:(ti + 1) * NE],
                                    w2c[:, ti:ti + 1], None, ALU.mult)
                inst = v.tensor_tensor(msb[:], msb[:], eq2[:], ALU.add)
                self.mark("vector", inst)
            blk.vector(dv_g2)
            dvg2 = self.cnt["sDV"]

            for tq in range(4):
                def pe_mt(t, tq=tq):
                    t.wait_ge(self.sem["sDV"], dvg2 if tq == 0 else self.cnt["sDV"])
                    for j in range(4):
                        ti = tq * 4 + j
                        inst = t.transpose(ps_rot[0][0:NE, j * 128:(j + 1) * 128],
                                           msb[:, ti * NE:(ti + 1) * NE], self.identF[:])
                    self.mark("tensor", inst)
                blk.tensor(pe_mt)
                pemt = self.cnt["sPE"]

                def dv_mt(v, tq=tq, pemt=pemt):
                    v.wait_ge(self.sem["sPE"], pemt)
                    self.mark("vector", v.tensor_copy(mTb[0:NE, tq * 512:(tq + 1) * 512],
                                                      ps_rot[0][0:NE, 0:512]))
                blk.vector(dv_mt)

            def g_sh(g):
                g.wait_ge(self.sem["sDV"], self.cnt["sDV"])
                self.load_wchunks(g, wA, d["f1_shup"], DM, H1)
                self.load_wchunks(g, wB, d["f1_shdown"], H1, DM)
                self.loadW(g, upbr[:], d["f1_upb"][:])
            blk.gpsimd(g_sh)
            w_sh = self.cnt["sW"]

            ps_ub = ps_rot[0][:].bitcast(BF16)[:, 0:NE * ND]

            def pe_ub(t):
                t.wait_ge(self.sem["sW"], w_sh)
                self.mark("tensor", t.transpose(ps_ub, upbr[:],
                                                self.identB[0:NE * ND, 0:NE * ND]))
            blk.tensor(pe_ub)
            peub = self.cnt["sPE"]

            def dv_ub(v):
                v.wait_ge(self.sem["sPE"], peub)
                self.mark("vector", v.tensor_copy(upbc[:], ps_ub))
            blk.vector(dv_ub)

            pe0, ac0, dvP0 = self.cnt["sPE"], self.cnt["sAC"], self.cnt["sDV"]

            def pe_s1B(t, pe0=pe0, ac0=ac0, dvP0=dvP0):
                t.wait_ge(self.sem["sW"], w_sh)
                t.wait_ge(self.sem["sDV"], dvP0)
                for it in range(32):
                    mj, tc = divmod(it, 4)
                    r = it % 2
                    if it >= 2:
                        t.wait_ge(self.sem["sAC"], ac0 + it - 1)
                    for kj in range(ND):
                        t.matmul(ps_rot[r][:],
                                 wA[:, kj * H1 + mj * 128: kj * H1 + (mj + 1) * 128],
                                 xn[kj][:, tc * 512:(tc + 1) * 512],
                                 start=(kj == 0), stop=False)
                    inst = t.matmul(ps_rot[r][:],
                                    shupb[0:1, mj * 128:(mj + 1) * 128],
                                    self.ones[0:1, tc * 512:(tc + 1) * 512],
                                    start=False, stop=True)
                    self.mark("tensor", inst)
            blk.tensor(pe_s1B)

            def ac_s1B(a, pe0=pe0):
                for it in range(32):
                    mj, tc = divmod(it, 4)
                    r = it % 2
                    a.wait_ge(self.sem["sPE"], pe0 + it + 1)
                    self.mark("scalar", a.activation(hbuf[mj][:, tc * 512:(tc + 1) * 512],
                                                     ps_rot[r][:], AF.Silu))
            blk.scalar(ac_s1B)
            ac_s1_done = self.cnt["sAC"]

            peS0, dvS0 = self.cnt["sPE"], self.cnt["sDV"]

            def pe_s2B(t, peS0=peS0, dvS0=dvS0):
                t.wait_ge(self.sem["sAC"], ac_s1_done)
                t.wait_ge(self.sem["sDV"], dvS0)
                for it in range(32):
                    m, tc = divmod(it, 4)
                    r = 2 + it % 2
                    if it >= 2:
                        t.wait_ge(self.sem["sDV"], dvS0 + it - 1)
                    for kj in range(ND):
                        t.matmul(ps_rot[r][:],
                                 wB[:, kj * DM + m * 128: kj * DM + (m + 1) * 128],
                                 hbuf[kj][:, tc * 512:(tc + 1) * 512],
                                 start=(kj == 0), stop=False)
                    inst = t.matmul(ps_rot[r][:],
                                    shdnb[0:1, m * 128:(m + 1) * 128],
                                    self.ones[0:1, tc * 512:(tc + 1) * 512],
                                    start=False, stop=True)
                    self.mark("tensor", inst)
            blk.tensor(pe_s2B)

            def dv_s2B(v, peS0=peS0):
                for it in range(32):
                    m, tc = divmod(it, 4)
                    r = 2 + it % 2
                    v.wait_ge(self.sem["sPE"], peS0 + it + 1)
                    inst = v.tensor_tensor(self.resid[m][:, tc * 512:(tc + 1) * 512],
                                           self.resid[m][:, tc * 512:(tc + 1) * 512],
                                           ps_rot[r][:], ALU.add)
                    self.mark("vector", inst)
            blk.vector(dv_s2B)

            exp_pe_done = 0
            self.dv_psd_free = self.cnt["sDV"]
            h_dv_hist = []
            for e in range(NE):
                pb = e % 2
                dv_mbw = self.cnt["sDV"]
                for tc in range(4):
                    def pe_mb(t, e=e, tc=tc, dv_mbw=dv_mbw):
                        t.wait_ge(self.sem["sDV"], dv_mbw)
                        self.mark("tensor", t.matmul(ps_rot[tc][:], self.sel8[:, e * 128:(e + 1) * 128],
                                                     mTb[0:NE, tc * 512:(tc + 1) * 512],
                                                     start=True, stop=True))
                    blk.tensor(pe_mb)
                    pemb = self.cnt["sPE"]

                    def dv_mb(v, tc=tc, pemb=pemb, pb=pb):
                        v.wait_ge(self.sem["sPE"], pemb)
                        self.mark("vector", v.tensor_copy(mb[pb][:, tc * 512:(tc + 1) * 512], ps_rot[tc][:]))
                    blk.vector(dv_mb)
                dv_mb_done = self.cnt["sDV"]

                def g_e(g, e=e, exp_pe_done=exp_pe_done):
                    g.wait_ge(self.sem["sPE"], self.cnt["sPE"])
                    self.load_wchunks(g, wA, d["f1_up"][e], DM, H1)
                    self.load_wchunks(g, wB, d["f1_down"][e], H1, DM)
                blk.gpsimd(g_e)
                w_e = self.cnt["sW"]

                peU0, acU0, dvU0 = self.cnt["sPE"], self.cnt["sAC"], self.cnt["sDV"]

                def pe_upB(t, e=e, w_e=w_e, peU0=peU0, dvU0=dvU0):
                    t.wait_ge(self.sem["sW"], w_e)
                    for it in range(32):
                        mj, tc = divmod(it, 4)
                        r = it % 2
                        t.wait_ge(self.sem["sDV"], dvU0 + it - 1 if it >= 2 else dvU0)
                        for kj in range(ND):
                            inst = t.matmul(ps_rot[r][:],
                                            wA[:, kj * H1 + mj * 128: kj * H1 + (mj + 1) * 128],
                                            xn[kj][:, tc * 512:(tc + 1) * 512],
                                            start=(kj == 0), stop=(kj == ND - 1))
                        self.mark("tensor", inst)
                blk.tensor(pe_upB)

                def ac_upB(a, e=e, peU0=peU0, dvU0=dvU0):
                    for it in range(32):
                        mj, tc = divmod(it, 4)
                        r = it % 2
                        a.wait_ge(self.sem["sPE"], peU0 + it + 1)
                        if it >= 2:
                            a.wait_ge(self.sem["sDV"], dvU0 + it - 1)
                        self.mark("scalar", a.activation(hbuf[mj][:, tc * 512:(tc + 1) * 512],
                                                         ps_rot[r][:], AF.Silu,
                                                         bias=upbc[:, e * ND + mj: e * ND + mj + 1]))
                blk.scalar(ac_upB)

                def dv_hcB(v, pb=pb, acU0=acU0):
                    for it in range(32):
                        mj, tc = divmod(it, 4)
                        v.wait_ge(self.sem["sAC"], acU0 + it + 1)
                        hs = hbuf[mj][:, tc * 512:(tc + 1) * 512]
                        inst = v.tensor_tensor(hs, hs, mb[pb][:, tc * 512:(tc + 1) * 512], ALU.mult)
                        self.mark("vector", inst)
                blk.vector(dv_hcB)
                dv_h_done = self.cnt["sDV"]

                peD0, dvD0 = self.cnt["sPE"], self.cnt["sDV"]

                def pe_dnB(t, w_e=w_e, peD0=peD0, dvD0=dvD0):
                    t.wait_ge(self.sem["sW"], w_e)
                    for idx in range(16):
                        tc, mg = divmod(idx, 4)
                        pr = 2 * (idx % 2)
                        t.wait_ge(self.sem["sDV"], dvD0 + idx - 1 if idx >= 2 else dvD0)
                        for mi in range(2):
                            m = mg * 2 + mi
                            for kj in range(ND):
                                inst = t.matmul(ps_d[pr + mi][:],
                                                wB[:, kj * DM + m * 128: kj * DM + (m + 1) * 128],
                                                hbuf[kj][:, tc * 512:(tc + 1) * 512],
                                                start=(kj == 0), stop=(kj == ND - 1))
                        self.mark("tensor", inst)
                blk.tensor(pe_dnB)

                def dv_dnB(v, peD0=peD0):
                    for idx in range(16):
                        tc, mg = divmod(idx, 4)
                        pr = 2 * (idx % 2)
                        v.wait_ge(self.sem["sPE"], peD0 + idx + 1)
                        for mi in range(2):
                            m = mg * 2 + mi
                            inst = v.tensor_tensor(self.resid[m][:, tc * 512:(tc + 1) * 512],
                                                   self.resid[m][:, tc * 512:(tc + 1) * 512],
                                                   ps_d[pr + mi][:], ALU.add)
                        self.mark("vector", inst)
                blk.vector(dv_dnB)
                exp_pe_done = self.cnt["sPE"]

            for m in range(ND):
                for tc in range(4):
                    it = m * 4 + tc
                    r = it % 4
                    dvf = self.cnt["sDV"]

                    def pe_db(t, m=m, tc=tc, r=r, dvf=dvf):
                        t.wait_ge(self.sem["sDV"], dvf)
                        self.mark("tensor", t.matmul(ps_rot[r][:],
                                                     dnb[0:NE, m * 128:(m + 1) * 128],
                                                     mTb[0:NE, tc * 512:(tc + 1) * 512],
                                                     start=True, stop=True))
                    blk.tensor(pe_db)
                    pedb = self.cnt["sPE"]

                    def dv_db(v, m=m, tc=tc, r=r, pedb=pedb):
                        v.wait_ge(self.sem["sPE"], pedb)
                        inst = v.tensor_tensor(self.resid[m][:, tc * 512:(tc + 1) * 512],
                                               self.resid[m][:, tc * 512:(tc + 1) * 512],
                                               ps_rot[r][:], ALU.add)
                        self.mark("vector", inst)
                    blk.vector(dv_db)
        self.barrier()

    def phase_out(self):
        nc, blk, d = self.nc, self.block, self.d
        with ExitStack() as ph:
            ytok = [ph.enter_context(nc.sbuf_tensor(f"out_ytok{i}", [128, DM], F32)) for i in range(2)]
            pst = [ph.enter_context(nc.psum_tensor(f"out_ps{i}", [128, 512], F32)) for i in range(2)]
            pe0, dv0, x0 = self.cnt["sPE"], self.cnt["sDV"], self.cnt["sX"]

            def pe_body(t):
                for ti in range(NT):
                    for half in range(2):
                        if ti >= 1:
                            t.wait_ge(self.sem["sDV"], dv0 + 2 * (ti - 1) + half + 1)
                        for j in range(4):
                            dj = half * 4 + j
                            inst = t.transpose(pst[half][:, j * 128:(j + 1) * 128],
                                               self.resid[dj][:, ti * 128:(ti + 1) * 128],
                                               self.identF[:])
                        self.mark("tensor", inst)
            blk.tensor(pe_body)

            def dv_body(v):
                for ti in range(NT):
                    if ti >= 2:
                        v.wait_ge(self.sem["sX"], x0 + 16 * (ti - 1))
                    for half in range(2):
                        v.wait_ge(self.sem["sPE"], pe0 + 2 * ti + half + 1)
                        inst = v.tensor_copy(ytok[ti % 2][:, half * 512:(half + 1) * 512],
                                             pst[half][:, 0:512])
                        self.mark("vector", inst)
            blk.vector(dv_body)

            def s_body(s):
                for ti in range(NT):
                    s.wait_ge(self.sem["sDV"], dv0 + 2 * ti + 2)
                    self.loadX(s, d["y_out"][ti * 128:(ti + 1) * 128, :], ytok[ti % 2][:])
                s.wait_ge(self.sem["sX"], self.cnt["sX"])
            blk.sync(s_body)


def host_constants(blocks):
    c = {}
    cvals = [0.0, 1.0, -0.5, LN_EPS, RMS_EPS, SCALE, -SCALE, 1.0 / DM]
    c["cvals"] = np.broadcast_to(np.asarray(cvals, np.float32), (128, len(cvals))).copy()
    c["identF"] = np.eye(128, dtype=np.float32)
    c["onesrowF"] = np.ones((1, 128), np.float32)
    c["onesrow"] = np.ones((1, T), np.float32)
    c["identB"] = np.eye(128, dtype=BF16NP)
    if "l0a" in blocks:
        q = np.arange(128)
        m = np.where(q[None, :] <= q[:, None], 0.0, -1e9).astype(np.float32)
        c["maskT"] = m.T.copy().astype(BF16NP)
    if "l0f" in blocks or "l1f" in blocks:
        s8 = np.zeros((NE, NE * 128), BF16NP)
        for e in range(NE):
            s8[e, e * 128:(e + 1) * 128] = 1.0
        c["sel8"] = s8
    if "l1a" in blocks:
        s16 = np.zeros((DS, DS * 128), BF16NP)
        for s in range(DS):
            s16[s, s * 128:(s + 1) * 128] = 1.0
        c["sel16"] = s16
    return c


def host_weights(inputs, blocks):
    w = {}
    f32 = np.float32
    asc = np.ascontiguousarray
    if "l0a" in blocks:
        w["wq_lat"] = asc(inputs["L0_attn_wq_lat"], dtype=f32)
        w["wk_lat"] = asc(inputs["L0_attn_wk_lat"], dtype=f32)
        w["wv_lat"] = asc(inputs["L0_attn_wv_lat"], dtype=f32)
        w["wq_heads"] = asc(inputs["L0_attn_wq_heads"], dtype=f32)
        w["wk_heads"] = asc(inputs["L0_attn_wk_heads"], dtype=f32)
        w["wv_heads"] = asc(inputs["L0_attn_wv_heads"], dtype=f32)
        w["w_proj"] = asc(inputs["L0_attn_w_proj"], dtype=f32)
        w["b_proj"] = asc(inputs["L0_attn_b_proj"].reshape(1, DM), dtype=f32)
    if "l0f" in blocks:
        w["f0_scale"] = asc(inputs["L0_ffn_norm_scale"].reshape(ND, 128), dtype=f32)
        w["f0_router"] = asc(inputs["L0_ffn_w_router"].T, dtype=f32)
        w["f0_up"] = asc(inputs["L0_ffn_w_up"].transpose(0, 2, 1), dtype=f32)
        w["f0_down"] = asc(inputs["L0_ffn_w_down"].transpose(0, 2, 1), dtype=f32)
    if "l1a" in blocks:
        w["a1_Win"] = asc(inputs["L1_attn_W_in"], dtype=f32)
        w["a1_bin"] = asc(inputs["L1_attn_b_in"].reshape(1, -1), dtype=f32)
        w["a1_Wx"] = asc(inputs["L1_attn_W_x"], dtype=f32)
        w["a1_bx"] = asc(inputs["L1_attn_b_x"].reshape(1, -1), dtype=f32)
        w["a1_Wdt"] = asc(inputs["L1_attn_W_dt"], dtype=f32)
        w["a1_bdt"] = asc(inputs["L1_attn_b_dt"].reshape(1, -1), dtype=f32)
        w["a1_Wout"] = asc(inputs["L1_attn_W_out"], dtype=f32)
        w["a1_bout"] = asc(inputs["L1_attn_b_out"].reshape(1, -1), dtype=f32)
        w["a1_Alog"] = asc(inputs["L1_attn_A_log"], dtype=f32)
        w["a1_D"] = asc(inputs["L1_attn_D_param"].reshape(NDI, 128), dtype=f32)
    if "l1f" in blocks:
        w["f1_gate"] = asc(inputs["L1_ffn_gate_w"].T, dtype=f32)
        w["f1_up"] = asc(inputs["L1_ffn_up_w"], dtype=f32)
        w["f1_upb"] = asc(inputs["L1_ffn_up_b"].reshape(NE * ND, 128), dtype=f32)
        w["f1_down"] = asc(inputs["L1_ffn_down_w"], dtype=f32)
        w["f1_downb"] = asc(inputs["L1_ffn_down_b"], dtype=f32)
        w["f1_shup"] = asc(inputs["L1_ffn_sh_up_w"], dtype=f32)
        w["f1_shupb"] = asc(inputs["L1_ffn_sh_up_b"].reshape(1, -1), dtype=f32)
        w["f1_shdown"] = asc(inputs["L1_ffn_sh_down_w"], dtype=f32)
        w["f1_shdownb"] = asc(inputs["L1_ffn_sh_down_b"].reshape(1, -1), dtype=f32)
    return w


def run_blocks(x_full, inputs, blocks):
    nc = Builder(blocks).build()
    w = host_weights(inputs, blocks)
    c = host_constants(blocks)
    in_maps = []
    for b in range(8):
        m = {"x_in": np.ascontiguousarray(x_full[b], dtype=np.float32)}
        m.update(w)
        m.update(c)
        in_maps.append(m)
    res = run_bass_kernel_spmd(nc, in_maps, core_ids=list(range(8)))
    return np.stack([r["y_out"] for r in res.results], axis=0)


def kernel(**inputs):
    x = np.asarray(inputs["L0_attn_x"], dtype=np.float32)
    return run_blocks(x, inputs, list(ALL_BLOCKS))


# version 12
# speedup vs baseline: 1.2874x; 1.0237x over previous
import numpy as np
import ml_dtypes
from contextlib import ExitStack

import concourse.bass as bass
import concourse.mybir as mybir
from concourse.bass_utils import run_bass_kernel_spmd

F32 = mybir.dt.float32
BF16 = mybir.dt.bfloat16
AF = mybir.ActivationFunctionType
ALU = mybir.AluOpType
BF16NP = ml_dtypes.bfloat16

T = 2048
DM = 1024
NT = T // 128
ND = DM // 128
LAT = 256
NH = 16
HD = 64
SCALE = 0.125
DI = 2048
NDI = DI // 128
DS = 16
H1 = 1024
NE = 8
LN_EPS = 1e-5
RMS_EPS = 1e-6

ALL_BLOCKS = ("l0a", "l0f", "l1a", "l1f")


class Builder:
    def __init__(self, blocks, debug=False):
        self.debug = debug
        self.blocks = list(blocks)
        nc = bass.Bass("TRN2", target_bir_lowering=False, debug=False)
        self.nc = nc
        self.ctx = ExitStack()
        d = {}
        self.d = d

        def dram(name, shape, kind="ExternalInput", dtype=F32):
            d[name] = nc.dram_tensor(name, list(shape), dtype, kind=kind).ap()
        blocks = self.blocks
        dram("x_in", [T, DM])
        dram("y_out", [T, DM], kind="ExternalOutput")
        if "l0a" in blocks:
            for n, s in [("wq_lat", [DM, LAT]), ("wk_lat", [DM, LAT]), ("wv_lat", [DM, LAT]),
                         ("wq_heads", [LAT, DM]), ("wk_heads", [LAT, DM]), ("wv_heads", [LAT, DM]),
                         ("w_proj", [DM, DM]), ("b_proj", [1, DM])]:
                dram(n, s)
        if "l0f" in blocks:
            dram("f0_scale", [ND, 128])
            dram("f0_router", [DM, NE])
            dram("f0_up", [NE, DM, 2 * DI])
            dram("f0_down", [NE, DI, DM])
        if "l1a" in blocks:
            dram("a1_Win", [DM, 2 * DI])
            dram("a1_bin", [1, 2 * DI])
            dram("a1_Wx", [DI, 2 * DS])
            dram("a1_bx", [1, 2 * DS])
            dram("a1_Wdt", [DI, DI])
            dram("a1_bdt", [1, DI])
            dram("a1_Wout", [DI, DM])
            dram("a1_bout", [1, DM])
            dram("a1_Alog", [DI, DS])
            dram("a1_D", [NDI, 128])
            dram("sp_dt", [DI, T], kind="Internal", dtype=BF16)
            dram("sp_dtu", [DI, T], kind="Internal", dtype=BF16)
            dram("sp_u", [DI, T], kind="Internal", dtype=BF16)
            dram("sp_sz", [DI, T], kind="Internal", dtype=BF16)
        if "l1f" in blocks:
            dram("f1_gate", [DM, NE])
            dram("f1_up", [NE, DM, H1])
            dram("f1_upb", [NE * ND, 128])
            dram("f1_down", [NE, H1, DM])
            dram("f1_downb", [NE, DM])
            dram("f1_shup", [DM, H1])
            dram("f1_shupb", [1, H1])
            dram("f1_shdown", [H1, DM])
            dram("f1_shdownb", [1, DM])
        if debug:
            dram("dbg_a", [128, 2048], kind="ExternalOutput")
            dram("dbg_b", [128, 2048], kind="ExternalOutput")
            dram("dbg_xn", [DM, T], kind="ExternalOutput", dtype=BF16)
        dram("identF", [128, 128])
        dram("onesrow", [1, T])
        dram("onesrowF", [1, 128])
        dram("cvals", [128, 8])
        dram("identB", [128, 128], dtype=BF16)
        if "l0a" in blocks:
            dram("maskT", [128, 128], dtype=BF16)
        if "l0f" in blocks or "l1f" in blocks:
            dram("sel8", [NE, NE * 128], dtype=BF16)
        if "l1a" in blocks:
            dram("sel16", [DS, DS * 128], dtype=BF16)

        self.sem = {}
        self.cnt = {}
        for s in ("sW", "sX", "sPE", "sDV", "sAC", "sPO", "sBAR"):
            self.sem[s] = self.ctx.enter_context(nc.semaphore(s))
            self.cnt[s] = 0
        self.last_inst = {}
        self.eng_sem = {"tensor": "sPE", "vector": "sDV", "scalar": "sAC", "gpsimd": "sPO"}
        self.ac_evac_hist = []
        self.sz_spill_hist = []
        self.dt_spill_hist = []
        self.dt_dv_hist = []
        self.cnt_sDV_bb = []
        self.ssm_dv_hist = []
        self.wout_dv_hist = []

        sb = self.sb = lambda name, shape, dt: self.ctx.enter_context(
            nc.sbuf_tensor(name, list(shape), dt))
        self.cvals_list = [0.0, 1.0, -0.5, LN_EPS, RMS_EPS, SCALE, -SCALE, 1.0 / DM]
        self.cvals = sb("c_cvals", [128, len(self.cvals_list)], F32)
        for j, v in enumerate(self.cvals_list):
            nc.const_aps.register(F32, v, self.cvals[:, j:j + 1])
        self.resid = [sb(f"resid{i}", [128, T], F32) for i in range(ND)]
        self.identF = sb("c_identF", [128, 128], F32)
        self.ones = sb("c_ones", [1, T], BF16)
        self.onesrF = sb("c_onesrF", [1, 128], F32)
        self.identB = sb("c_identB", [128, 128], BF16)
        if "l0a" in blocks:
            self.maskT = sb("c_maskT", [128, 128], BF16)
        if "l0f" in blocks or "l1f" in blocks:
            self.sel8 = sb("c_sel8", [NE, NE * 128], BF16)
        if "l1a" in blocks:
            self.sel16 = sb("c_sel16", [DS, DS * 128], BF16)
        self.block = None

    def inc(self, inst, s, k=1):
        inst.then_inc(self.sem[s], k)
        self.cnt[s] += k
        return inst

    def note(self, eng, inst):
        self.last_inst[eng] = inst
        return inst

    def mark(self, eng, inst):
        self.note(eng, inst)
        s = self.eng_sem[eng]
        return self.inc(inst, s)

    def barrier(self):
        self.last_inst = {}
        tgts = {s: self.cnt[s] for s in ("sPE", "sDV", "sAC", "sPO", "sW", "sX")}
        for name in ("tensor", "vector", "scalar", "gpsimd", "sync"):
            def body(e, tgts=tgts):
                for s, v in tgts.items():
                    if v > 0:
                        e.wait_ge(self.sem[s], v)
            getattr(self.block, name)(body)

    def sb_local(self, stack, name, shape, dt):
        return stack.enter_context(self.nc.sbuf_tensor(name, list(shape), dt))

    def loadW(self, g, dst_ap, src_ap):
        return self.inc(g.dma_start(out=dst_ap, in_=src_ap), "sW", 16)

    def loadX(self, s, dst_ap, src_ap):
        return self.inc(s.dma_start(out=dst_ap, in_=src_ap), "sX", 16)

    def load_wchunks(self, g, dst_sb, src, K, M):
        nk = K // 128
        src_r = src.rearrange("(kj p) m -> p kj m", p=128)
        dst = dst_sb[:, 0:nk * M].rearrange("p (kj m) -> p kj m", kj=nk)
        return self.loadW(g, dst, src_r)

    def build(self):
        nc = self.nc
        self.block = self.ctx.enter_context(nc.Block())
        self.phase_consts()
        self.phase_in()
        for b in self.blocks:
            getattr(self, "phase_" + b)()
        self.phase_out()
        self.ctx.close()
        return nc

    def phase_consts(self):
        blk, d = self.block, self.d

        def s_body(s):
            self.loadX(s, self.cvals[:], d["cvals"][:])
            self.loadX(s, self.identF[:], d["identF"][:])
            self.loadX(s, self.onesrF[:], d["onesrowF"][:])
            self.loadX(s, self.identB[:], d["identB"][:])
            if "l0a" in self.blocks:
                self.loadX(s, self.maskT[:], d["maskT"][:])
            if "l0f" in self.blocks or "l1f" in self.blocks:
                self.loadX(s, self.sel8[:], d["sel8"][:])
            if "l1a" in self.blocks:
                self.loadX(s, self.sel16[:], d["sel16"][:])
        blk.sync(s_body)

        def g_body(g):
            self.loadW(g, self.ones[:], d["onesrow"][:])
        blk.gpsimd(g_body)
        self.barrier()

    def phase_in(self):
        nc, blk, d = self.nc, self.block, self.d
        with ExitStack() as ph:
            xtok = [ph.enter_context(nc.sbuf_tensor(f"in_xtok{i}", [128, DM], F32)) for i in range(2)]
            pst = [ph.enter_context(nc.psum_tensor(f"in_ps{i}", [128, 512], F32)) for i in range(2)]
            x0, pe0, dv0 = self.cnt["sX"], self.cnt["sPE"], self.cnt["sDV"]

            def s_body(s):
                for ti in range(NT):
                    if ti >= 2:
                        s.wait_ge(self.sem["sPE"], pe0 + 2 * ti - 2)
                    self.loadX(s, xtok[ti % 2][:], d["x_in"][ti * 128:(ti + 1) * 128, :])
            blk.sync(s_body)

            def pe_body(t):
                for ti in range(NT):
                    t.wait_ge(self.sem["sX"], x0 + 16 * (ti + 1))
                    for half in range(2):
                        if ti >= 1:
                            t.wait_ge(self.sem["sDV"], dv0 + 2 * (ti - 1) + half + 1)
                        for j in range(4):
                            dj = half * 4 + j
                            inst = t.transpose(pst[half][:, j * 128:(j + 1) * 128],
                                               xtok[ti % 2][:, dj * 128:(dj + 1) * 128],
                                               self.identF[:])
                        self.mark("tensor", inst)
            blk.tensor(pe_body)

            def dv_body(v):
                for ti in range(NT):
                    for half in range(2):
                        v.wait_ge(self.sem["sPE"], pe0 + 2 * ti + half + 1)
                        inst = None
                        for j in range(4):
                            dj = half * 4 + j
                            inst = v.tensor_copy(self.resid[dj][:, ti * 128:(ti + 1) * 128],
                                                 pst[half][:, j * 128:(j + 1) * 128])
                        self.mark("vector", inst)
            blk.vector(dv_body)
        self.barrier()

    def emit_ln(self, ph, xn, rms_scale_cols=None, skip_add=False, scratch=None):
        nc, blk = self.nc, self.block
        self.ln_id = getattr(self, "ln_id", 0) + 1
        u = self.ln_id
        with ExitStack() as ln:
            if scratch is None:
                sq = [ln.enter_context(nc.sbuf_tensor(f"ln{u}_sq{i}", [128, 512], F32))[:] for i in range(2)]
                tn = [ln.enter_context(nc.sbuf_tensor(f"ln{u}_tn{i}", [128, 512], F32))[:] for i in range(2)]
                mrow = ln.enter_context(nc.sbuf_tensor(f"ln{u}_m", [1, 512], F32))[:]
                vrow = ln.enter_context(nc.sbuf_tensor(f"ln{u}_v", [1, 512], F32))[:]
                rrow = ln.enter_context(nc.sbuf_tensor(f"ln{u}_r", [1, 512], F32))[:]
                mrrow = ln.enter_context(nc.sbuf_tensor(f"ln{u}_mr", [1, 512], F32))[:]
            else:
                sv = scratch
                sq = [sv[:, 0:512], sv[:, 512:1024]]
                tn = [sv[:, 1024:1536], sv[:, 1536:2048]]
                mrow = sv[0:1, 2048:2560]
                vrow = sv[0:1, 2560:3072]
                rrow = sv[0:1, 3072:3584]
                mrrow = sv[0:1, 3584:4096]
            onesc = self.cvals[:, 1:2]
            onesr = self.onesrF
            ps_s = ln.enter_context(nc.psum_tensor(f"ln{u}_ps_s", [128, 512], F32))
            ps_q = ln.enter_context(nc.psum_tensor(f"ln{u}_ps_q", [128, 512], F32))
            ps_b = [ln.enter_context(nc.psum_tensor(f"ln{u}_ps_b{i}", [128, 512], F32)) for i in range(2)]
            peLN0 = self.cnt["sPE"]

            for tc in range(4):
                sl = slice(tc * 512, (tc + 1) * 512)
                ac_base = self.cnt["sAC"]
                pe_tc0 = self.cnt["sPE"]

                def ac_sq(a, sl=sl, tc=tc, pe_tc0=pe_tc0):
                    for dj in range(ND):
                        if dj < 2:
                            if tc > 0:
                                a.wait_ge(self.sem["sPE"], pe_tc0)
                        else:
                            a.wait_ge(self.sem["sPE"], pe_tc0 + dj - 1)
                        self.mark("scalar", a.activation(sq[dj % 2], self.resid[dj][:, sl], AF.Square))
                blk.scalar(ac_sq)

                dv_prev_stats = self.cnt["sDV"]

                def pe_body(t, sl=sl, ac_base=ac_base, dv_prev_stats=dv_prev_stats):
                    t.wait_ge(self.sem["sDV"], dv_prev_stats)
                    for dj in range(ND):
                        t.matmul(ps_s[0:1, :], onesc, self.resid[dj][:, sl],
                                 start=(dj == 0), stop=(dj == ND - 1))
                    for dj in range(ND):
                        t.wait_ge(self.sem["sAC"], ac_base + dj + 1)
                        inst = t.matmul(ps_q[0:1, :], onesc, sq[dj % 2],
                                        start=(dj == 0), stop=(dj == ND - 1))
                        self.mark("tensor", inst)
                blk.tensor(pe_body)
                pe_done = self.cnt["sPE"]

                def dv_stats(v, sl=sl, pe_done=pe_done):
                    v.wait_ge(self.sem["sPE"], pe_done)
                    v.tensor_scalar_mul(mrow, ps_s[0:1, :], 1.0 / DM)
                    v.tensor_scalar_mul(vrow, ps_q[0:1, :], 1.0 / DM)
                    v.tensor_tensor(ps_s[0:1, :], mrow, mrow, ALU.mult)
                    self.mark("vector", v.tensor_tensor(vrow, vrow, ps_s[0:1, :], ALU.subtract))
                blk.vector(dv_stats)
                dv_done = self.cnt["sDV"]

                def ac_rstd(a, sl=sl, dv_done=dv_done):
                    a.wait_ge(self.sem["sDV"], dv_done)
                    a.activation(rrow, vrow, AF.Ln, bias=LN_EPS)
                    self.mark("scalar", a.activation(rrow, rrow, AF.Exp, scale=-0.5))
                blk.scalar(ac_rstd)
                ac_done = self.cnt["sAC"]

                def dv_mr(v, sl=sl, ac_done=ac_done):
                    v.wait_ge(self.sem["sAC"], ac_done)
                    self.mark("vector", v.tensor_tensor(mrrow, mrow, rrow, ALU.mult))
                blk.vector(dv_mr)
                dv_done2 = self.cnt["sDV"]

                def pe_bcast(t, sl=sl, dv_done2=dv_done2):
                    t.wait_ge(self.sem["sDV"], dv_done2)
                    t.matmul(ps_b[0][:], onesr[:], rrow, start=True, stop=True)
                    self.mark("tensor", t.matmul(ps_b[1][:], onesr[:], mrrow, start=True, stop=True))
                blk.tensor(pe_bcast)
                pe_done2 = self.cnt["sPE"]

                def dv_norm(v, sl=sl, pe_done2=pe_done2):
                    v.wait_ge(self.sem["sPE"], pe_done2)
                    for dj in range(ND):
                        v.tensor_tensor(tn[dj % 2], self.resid[dj][:, sl], ps_b[0][:], ALU.mult)
                        inst = v.tensor_tensor(xn[dj][:, sl], tn[dj % 2], ps_b[1][:], ALU.subtract)
                        self.mark("vector", inst)
                blk.vector(dv_norm)

            if skip_add:
                def dv_skip(v):
                    for dj in range(ND):
                        inst = v.tensor_tensor(self.resid[dj][:], self.resid[dj][:],
                                               xn[dj][:], ALU.add)
                    self.mark("vector", inst)
                blk.vector(dv_skip)

            if rms_scale_cols is not None:
                for tc in range(4):
                    sl = slice(tc * 512, (tc + 1) * 512)
                    dv_n = self.cnt["sDV"]
                    ac_base2 = self.cnt["sAC"]
                    pe_tc0 = self.cnt["sPE"]

                    def ac_sq2(a, sl=sl, dv_n=dv_n, tc=tc, pe_tc0=pe_tc0):
                        a.wait_ge(self.sem["sDV"], dv_n)
                        for dj in range(ND):
                            if dj < 2:
                                a.wait_ge(self.sem["sPE"], pe_tc0)
                            else:
                                a.wait_ge(self.sem["sPE"], pe_tc0 + dj - 1)
                            self.mark("scalar", a.activation(sq[dj % 2], xn[dj][:, sl], AF.Square))
                    blk.scalar(ac_sq2)

                    dv_cur = self.cnt["sDV"]

                    def pe_sum2(t, sl=sl, ac_base2=ac_base2, dv_cur=dv_cur):
                        t.wait_ge(self.sem["sDV"], dv_cur)
                        for dj in range(ND):
                            t.wait_ge(self.sem["sAC"], ac_base2 + dj + 1)
                            inst = t.matmul(ps_q[0:1, :], onesc, sq[dj % 2],
                                            start=(dj == 0), stop=(dj == ND - 1))
                            self.mark("tensor", inst)
                    blk.tensor(pe_sum2)
                    pe_d = self.cnt["sPE"]

                    def dv_ms(v, sl=sl, pe_d=pe_d):
                        v.wait_ge(self.sem["sPE"], pe_d)
                        self.mark("vector", v.tensor_scalar_mul(vrow, ps_q[0:1, :], 1.0 / DM))
                    blk.vector(dv_ms)
                    dv_d = self.cnt["sDV"]

                    def ac_r2(a, sl=sl, dv_d=dv_d):
                        a.wait_ge(self.sem["sDV"], dv_d)
                        a.activation(rrow, vrow, AF.Ln, bias=RMS_EPS)
                        self.mark("scalar", a.activation(rrow, rrow, AF.Exp, scale=-0.5))
                    blk.scalar(ac_r2)
                    ac_d = self.cnt["sAC"]

                    dv_cur2 = self.cnt["sDV"]

                    def pe_b2(t, sl=sl, ac_d=ac_d, dv_cur2=dv_cur2):
                        t.wait_ge(self.sem["sAC"], ac_d)
                        t.wait_ge(self.sem["sDV"], dv_cur2)
                        self.mark("tensor", t.matmul(ps_b[0][:], onesr[:], rrow, start=True, stop=True))
                    blk.tensor(pe_b2)
                    pe_d2 = self.cnt["sPE"]

                    def dv_apply(v, sl=sl, pe_d2=pe_d2):
                        v.wait_ge(self.sem["sPE"], pe_d2)
                        for dj in range(ND):
                            inst = v.scalar_tensor_tensor(
                                xn[dj][:, sl], xn[dj][:, sl], rms_scale_cols[:, dj:dj + 1],
                                ps_b[0][:], ALU.mult, ALU.mult)
                            self.mark("vector", inst)
                    blk.vector(dv_apply)
        self.barrier()

    def phase_l0a(self):
        nc, blk, d = self.nc, self.block, self.d
        NG = 4
        HPG = 4
        with ExitStack() as ph:
            xn = [ph.enter_context(nc.sbuf_tensor(f"a0_xn{i}", [128, T], BF16)) for i in range(ND)]
            wql = ph.enter_context(nc.sbuf_tensor("a0_wql", [128, ND * LAT], BF16))
            wkl = ph.enter_context(nc.sbuf_tensor("a0_wkl", [128, ND * LAT], BF16))
            wvl = ph.enter_context(nc.sbuf_tensor("a0_wvl", [128, ND * LAT], BF16))
            wqh = ph.enter_context(nc.sbuf_tensor("a0_wqh", [128, 2 * DM], BF16))
            wkh = ph.enter_context(nc.sbuf_tensor("a0_wkh", [128, 2 * DM], BF16))
            wvh = ph.enter_context(nc.sbuf_tensor("a0_wvh", [128, 2 * DM], BF16))
            wpr = ph.enter_context(nc.sbuf_tensor("a0_wpr", [128, 2 * DM], BF16))
            bpr = ph.enter_context(nc.sbuf_tensor("a0_bpr", [1, DM], BF16))

            def g_w(g):
                self.load_wchunks(g, wql, d["wq_lat"], DM, LAT)
                self.load_wchunks(g, wkl, d["wk_lat"], DM, LAT)
                self.load_wchunks(g, wvl, d["wv_lat"], DM, LAT)
                self.load_wchunks(g, wqh, d["wq_heads"], LAT, DM)
                self.load_wchunks(g, wkh, d["wk_heads"], LAT, DM)
                self.load_wchunks(g, wvh, d["wv_heads"], LAT, DM)
                self.loadW(g, bpr[:], d["b_proj"][:])
            blk.gpsimd(g_w)
            w_all = self.cnt["sW"]

            self.emit_ln(ph, xn)

            q1 = [ph.enter_context(nc.sbuf_tensor(f"a0_q1_{i}", [128, T], BF16)) for i in range(2)]
            k1 = [ph.enter_context(nc.sbuf_tensor(f"a0_k1_{i}", [128, T], BF16)) for i in range(2)]
            v1 = [ph.enter_context(nc.sbuf_tensor(f"a0_v1_{i}", [128, T], BF16)) for i in range(2)]
            qTg = [ph.enter_context(nc.sbuf_tensor(f"a0_qT{i}", [128, T], BF16)) for i in range(2)]
            kTg = [ph.enter_context(nc.sbuf_tensor(f"a0_kT{i}", [128, T], BF16)) for i in range(2)]
            vtokg = ph.enter_context(nc.sbuf_tensor("a0_vtok", [128, NT * 256], BF16))
            yTg = [ph.enter_context(nc.sbuf_tensor(f"a0_yT{i}", [128, T], BF16)) for i in range(2)]
            P = [ph.enter_context(nc.sbuf_tensor(f"a0_P{i}", [128, T], BF16)) for i in range(2)]
            PT0 = ph.enter_context(nc.sbuf_tensor("a0_PT0", [128, T], BF16))
            PT = [PT0, PT0]
            rmax = ph.enter_context(nc.sbuf_tensor("a0_rmax", [128, 1], F32))
            nbias = ph.enter_context(nc.sbuf_tensor("a0_nbias", [128, 1], F32))
            rsum = ph.enter_context(nc.sbuf_tensor("a0_rsum", [128, 1], F32))
            rrec = ph.enter_context(nc.sbuf_tensor("a0_rrec", [128, 2], F32))
            ps_rot = [ph.enter_context(nc.psum_tensor(f"a0_rot{i}", [128, 512], F32)) for i in range(2)]
            ps_S = ph.enter_context(nc.psum_tensor("a0_S", [128, T], F32))
            ps_pt = ph.enter_context(nc.psum_tensor("a0_pt", [128, 512], BF16))
            ps_y = ph.enter_context(nc.psum_tensor("a0_y", [64, 256], F32))

            def mm_evac_seq(jobs):
                pe0, dv0 = self.cnt["sPE"], self.cnt["sDV"]
                for i, (mm, ev) in enumerate(jobs):
                    r = i % 2

                    def pe_j(t, i=i, r=r, mm=mm):
                        t.wait_ge(self.sem["sDV"], dv0 + i - 1 if i >= 2 else dv0)
                        self.mark("tensor", mm(t, ps_rot[r]))
                    blk.tensor(pe_j)

                    def dv_j(v, i=i, r=r, ev=ev):
                        v.wait_ge(self.sem["sPE"], pe0 + i + 1)
                        self.mark("vector", ev(v, ps_rot[r]))
                    blk.vector(dv_j)

            jobs = []
            for (wsb, dst) in ((wql, q1), (wkl, k1), (wvl, v1)):
                for m2 in range(2):
                    for tc in range(4):
                        def mm(t, ps, wsb=wsb, m2=m2, tc=tc):
                            t.wait_ge(self.sem["sW"], w_all)
                            inst = None
                            for kj in range(ND):
                                inst = t.matmul(ps[:], wsb[:, kj * LAT + m2 * 128: kj * LAT + (m2 + 1) * 128],
                                                xn[kj][:, tc * 512:(tc + 1) * 512],
                                                start=(kj == 0), stop=(kj == ND - 1))
                            return inst

                        def ev(v, ps, dst=dst, m2=m2, tc=tc):
                            return v.tensor_copy(dst[m2][:, tc * 512:(tc + 1) * 512], ps[:])
                        jobs.append((mm, ev))
            mm_evac_seq(jobs)
            self.barrier()

            for g in range(NG):
                gofs = g * 256
                jobs = []
                for (wsb, dst) in ((wqh, qTg), (wkh, kTg)):
                    for mi in range(2):
                        for tc in range(4):
                            def mm(t, ps, wsb=wsb, mi=mi, tc=tc):
                                inst = None
                                for kj in range(2):
                                    inst = t.matmul(
                                        ps[:],
                                        wsb[:, kj * DM + gofs + mi * 128: kj * DM + gofs + (mi + 1) * 128],
                                        (q1 if wsb is wqh else k1)[kj][:, tc * 512:(tc + 1) * 512],
                                        start=(kj == 0), stop=(kj == 1))
                                return inst

                            def ev(v, ps, dst=dst, mi=mi, tc=tc):
                                return v.tensor_copy(dst[mi][:, tc * 512:(tc + 1) * 512], ps[:])
                            jobs.append((mm, ev))
                for ti2 in range(NT // 2):
                    def mm(t, ps, ti2=ti2):
                        inst = None
                        for half in range(2):
                            ti = ti2 * 2 + half
                            for kj in range(2):
                                inst = t.matmul(
                                    ps[:, half * 256:(half + 1) * 256],
                                    v1[kj][:, ti * 128:(ti + 1) * 128],
                                    wvh[:, kj * DM + gofs: kj * DM + gofs + 256],
                                    start=(kj == 0), stop=(kj == 1))
                        return inst

                    def ev(v, ps, ti2=ti2):
                        return v.tensor_copy(vtokg[:, ti2 * 512:(ti2 + 1) * 512], ps[:])
                    jobs.append((mm, ev))
                mm_evac_seq(jobs)

                for h in range(HPG):
                    qrow = (h % 2) * 64
                    mi = h // 2
                    acH0 = self.cnt["sAC"]
                    exp_jobs = []

                    def emit_ptpv(qc):
                        pq = qc % 2
                        acN = acH0 + 2 * qc + 2
                        nblk = qc + 1
                        pt_banks = [ps_pt[:], ps_rot[0][:].bitcast(BF16)]
                        for jbidx, jb in enumerate(range(0, nblk, 4)):
                            nj = min(4, nblk - jb)
                            bank = pt_banks[jbidx % 2]
                            dv_at = self.cnt["sDV"]

                            def pe_pt(t, jb=jb, nj=nj, acN=acN, dv_at=dv_at, pq=pq, bank=bank, jbidx=jbidx):
                                if jbidx < 2:
                                    t.wait_ge(self.sem["sAC"], acN)
                                else:
                                    t.wait_ge(self.sem["sDV"], dv_at - 1)
                                for j in range(nj):
                                    inst = t.transpose(bank[:, j * 128:(j + 1) * 128],
                                                       P[pq][:, (jb + j) * 128:(jb + j + 1) * 128],
                                                       self.identB[:])
                                self.mark("tensor", inst)
                            blk.tensor(pe_pt)
                            pePT = self.cnt["sPE"]

                            def dv_pt(v, jb=jb, nj=nj, pePT=pePT, pq=pq, bank=bank):
                                v.wait_ge(self.sem["sPE"], pePT)
                                self.mark("vector", v.tensor_copy(PT[pq][:, jb * 128:(jb + nj) * 128],
                                                                  bank[:, 0:nj * 128]))
                            blk.vector(dv_pt)

                        dvPT = self.cnt["sDV"]

                        def pe_pv(t, qc=qc, h=h, dvPT=dvPT, pq=pq):
                            t.wait_ge(self.sem["sDV"], dvPT)
                            for j in range(qc + 1):
                                inst = t.matmul(ps_y[:, pq * 128:(pq + 1) * 128],
                                                vtokg[:, j * 256 + h * 64: j * 256 + (h + 1) * 64],
                                                PT[pq][:, j * 128:(j + 1) * 128],
                                                start=(j == 0), stop=(j == qc))
                            self.mark("tensor", inst)
                        blk.tensor(pe_pv)
                        pePV = self.cnt["sPE"]

                        def dv_y(v, qc=qc, mi=mi, qrow=qrow, pePV=pePV, pq=pq):
                            v.wait_ge(self.sem["sPE"], pePV)
                            self.mark("vector", v.tensor_copy(
                                yTg[mi][qrow:qrow + 64, qc * 128:(qc + 1) * 128],
                                ps_y[:, pq * 128:(pq + 1) * 128]))
                        blk.vector(dv_y)

                    ac_jobs = []
                    for qc in range(NT):
                        def pe_S(t, qc=qc, mi=mi, qrow=qrow, acH0=acH0):
                            t.wait_ge(self.sem["sAC"], acH0 + 2 * qc - 1 if qc >= 1 else acH0)
                            wts = qTg[mi][qrow:qrow + 64, qc * 128:(qc + 1) * 128]
                            nch = (qc + 1) * 128
                            for c0 in range(0, nch, 512):
                                cw = min(512, nch - c0)
                                contains_diag = (c0 + cw == nch)
                                t.matmul(ps_S[:, c0:c0 + cw],
                                         wts, kTg[mi][qrow:qrow + 64, c0:c0 + cw],
                                         start=True, stop=not contains_diag)
                            inst = t.matmul(ps_S[:, qc * 128:(qc + 1) * 128],
                                            self.maskT[:], self.identB[:],
                                            start=False, stop=True)
                            self.mark("tensor", inst)
                        blk.tensor(pe_S)
                        peS = self.cnt["sPE"]

                        def dv_max(v, qc=qc, peS=peS):
                            v.wait_ge(self.sem["sPE"], peS)
                            v.reduce_max(rmax[:], ps_S[:, 0:(qc + 1) * 128], axis=mybir.AxisListType.X)
                            self.mark("vector", v.tensor_scalar_mul(nbias[:], rmax[:], -SCALE))
                        blk.vector(dv_max)
                        dvm = self.cnt["sDV"]
                        acE = acH0 + 2 * qc + 1
                        self.cnt["sAC"] += 2

                        def dv_recip(v, qc=qc, acE=acE):
                            v.wait_ge(self.sem["sAC"], acE)
                            self.mark("vector", v.reciprocal(rrec[:, qc % 2:qc % 2 + 1], rsum[:]))
                        blk.vector(dv_recip)
                        ac_jobs.append((qc, dvm, self.cnt["sDV"]))
                        if qc >= 1:
                            emit_ptpv(qc - 1)
                    emit_ptpv(NT - 1)

                    def ac_expB(a_, jobs=tuple(ac_jobs)):
                        for qc, dvm, dvr in jobs:
                            a_.wait_ge(self.sem["sDV"], dvm)
                            inst = a_.activation(P[qc % 2][:, 0:(qc + 1) * 128], ps_S[:, 0:(qc + 1) * 128],
                                                 AF.Exp, scale=SCALE, bias=nbias[:],
                                                 accum_out=rsum[:])
                            inst.then_inc(self.sem["sAC"], 1)
                            a_.wait_ge(self.sem["sDV"], dvr)
                            inst = a_.activation(P[qc % 2][:, 0:(qc + 1) * 128], P[qc % 2][:, 0:(qc + 1) * 128],
                                                 AF.Copy, scale=rrec[:, qc % 2:qc % 2 + 1])
                            inst.then_inc(self.sem["sAC"], 1)
                            self.note("scalar", inst)
                    blk.scalar(ac_expB)

                def g_pr(g_, g=g):
                    g_.wait_ge(self.sem["sPE"], self.cnt["sPE"])
                    self.load_wchunks(g_, wpr, d["w_proj"][g * 256:(g + 1) * 256, :], 256, DM)
                blk.gpsimd(g_pr)
                wpr_c = self.cnt["sW"]
                jobs = []
                for m in range(ND):
                    for tc in range(4):
                        def mm(t, ps, m=m, tc=tc, g=g, wpr_c=wpr_c):
                            t.wait_ge(self.sem["sW"], wpr_c)
                            for kj in range(2):
                                inst = t.matmul(ps[:],
                                                wpr[:, kj * DM + m * 128: kj * DM + (m + 1) * 128],
                                                yTg[kj][:, tc * 512:(tc + 1) * 512],
                                                start=(kj == 0), stop=(kj == 1 and g != 0))
                            if g == 0:
                                inst = t.matmul(ps[:], bpr[0:1, m * 128:(m + 1) * 128],
                                                self.ones[0:1, tc * 512:(tc + 1) * 512],
                                                start=False, stop=True)
                            return inst

                        def ev(v, ps, m=m, tc=tc):
                            return v.tensor_tensor(self.resid[m][:, tc * 512:(tc + 1) * 512],
                                                   self.resid[m][:, tc * 512:(tc + 1) * 512],
                                                   ps[:], ALU.add)
                        jobs.append((mm, ev))
                mm_evac_seq(jobs)
        self.barrier()

    def phase_l0f(self):
        nc, blk, d = self.nc, self.block, self.d
        with ExitStack() as ph:
            xn = [ph.enter_context(nc.sbuf_tensor(f"f0_xn{i}", [128, T], BF16)) for i in range(ND)]
            scale_rows = ph.enter_context(nc.sbuf_tensor("f0_sr", [ND, 128], F32))
            scale_cols = ph.enter_context(nc.sbuf_tensor("f0_sc", [128, ND], F32))
            wr = ph.enter_context(nc.sbuf_tensor("f0_wr", [128, ND * NE], BF16))
            scores = ph.enter_context(nc.sbuf_tensor("f0_scores", [128, NT * NE], F32))
            top8 = ph.enter_context(nc.sbuf_tensor("f0_top8", [128, NT * NE], F32))
            dif = ph.enter_context(nc.sbuf_tensor("f0_dif", [128, NT], F32))
            w2c = ph.enter_context(nc.sbuf_tensor("f0_w2", [128, NT], F32))
            w1c = ph.enter_context(nc.sbuf_tensor("f0_w1", [128, NT], F32))
            eq2 = ph.enter_context(nc.sbuf_tensor("f0_eq2", [128, NT * NE], F32))
            msb = ph.enter_context(nc.sbuf_tensor("f0_m", [128, NT * NE], F32))
            mTb = ph.enter_context(nc.sbuf_tensor("f0_mTb", [NE, T], BF16))
            mb = [ph.enter_context(nc.sbuf_tensor(f"f0_mb{i}", [128, T], BF16)) for i in range(2)]
            sg = [ph.enter_context(nc.sbuf_tensor(f"f0_sg{i}", [128, 512], BF16)) for i in range(2)]
            hsb = [ph.enter_context(nc.sbuf_tensor(f"f0_h{i}", [128, T], BF16)) for i in range(4)]
            wu = [ph.enter_context(nc.sbuf_tensor(f"f0_wu{i}", [128, ND * 512], BF16)) for i in range(2)]
            wg = [ph.enter_context(nc.sbuf_tensor(f"f0_wg{i}", [128, ND * 512], BF16)) for i in range(2)]
            wd = [ph.enter_context(nc.sbuf_tensor(f"f0_wd{i}", [128, 4 * DM], BF16)) for i in range(2)]

            def s_pre(s):
                self.loadX(s, scale_rows[:], d["f0_scale"][:])
            blk.sync(s_pre)
            x_sr = self.cnt["sX"]

            def g_pre(g):
                self.load_wchunks(g, wr, d["f0_router"], DM, NE)
            blk.gpsimd(g_pre)

            with nc.psum_tensor("f0_ps_sc", [128, 128], F32) as ps_sc:
                def pe_sc(t):
                    t.wait_ge(self.sem["sX"], x_sr)
                    self.mark("tensor", t.transpose(ps_sc[:, 0:ND], scale_rows[:],
                                                    self.identF[0:ND, 0:ND]))
                blk.tensor(pe_sc)
                pe_sc_c = self.cnt["sPE"]

                def dv_sc(v):
                    v.wait_ge(self.sem["sPE"], pe_sc_c)
                    self.mark("vector", v.tensor_copy(scale_cols[:], ps_sc[:, 0:ND]))
                blk.vector(dv_sc)
                self.barrier()

            self.emit_ln(ph, xn, rms_scale_cols=scale_cols, skip_add=True)
            ps_rot = [ph.enter_context(nc.psum_tensor(f"f0_rot{i}", [128, 512], F32)) for i in range(4)]
            ps_d = [ph.enter_context(nc.psum_tensor(f"f0_psd{i}", [128, 512], F32)) for i in range(4)]

            pe0, dv0 = self.cnt["sPE"], self.cnt["sDV"]
            for ti in range(NT):
                r = ti % 4

                def pe_s(t, ti=ti, r=r):
                    t.wait_ge(self.sem["sDV"], dv0 + ti - 3 if ti >= 4 else dv0)
                    for kj in range(ND):
                        inst = t.matmul(ps_rot[r][:, 0:NE], xn[kj][:, ti * 128:(ti + 1) * 128],
                                        wr[:, kj * NE:(kj + 1) * NE],
                                        start=(kj == 0), stop=(kj == ND - 1))
                    self.mark("tensor", inst)
                blk.tensor(pe_s)
                pec = self.cnt["sPE"]

                def dv_s(v, ti=ti, r=r, pec=pec):
                    v.wait_ge(self.sem["sPE"], pec)
                    self.mark("vector", v.tensor_copy(scores[:, ti * NE:(ti + 1) * NE], ps_rot[r][:, 0:NE]))
                blk.vector(dv_s)

            def dv_g1(v):
                for ti in range(NT):
                    v.max(top8[:, ti * NE:(ti + 1) * NE], scores[:, ti * NE:(ti + 1) * NE])
                t8v = top8[:].rearrange("p (ti e) -> p ti e", ti=NT)
                inst = v.tensor_tensor(dif[:], t8v[:, :, 1], t8v[:, :, 0], ALU.subtract)
                self.mark("vector", inst)
            blk.vector(dv_g1)
            dvg1 = self.cnt["sDV"]

            def ac_sig(a):
                a.wait_ge(self.sem["sDV"], dvg1)
                self.mark("scalar", a.activation(w2c[:], dif[:], AF.Sigmoid))
            blk.scalar(ac_sig)
            acsig = self.cnt["sAC"]

            def dv_g1b(v):
                v.wait_ge(self.sem["sAC"], acsig)
                t8v = top8[:].rearrange("p (ti e) -> p ti e", ti=NT)
                self.mark("vector", v.tensor_tensor(dif[:], t8v[:, :, 0], t8v[:, :, 1], ALU.subtract))
            blk.vector(dv_g1b)
            dvg1b = self.cnt["sDV"]

            def ac_sig2(a):
                a.wait_ge(self.sem["sDV"], dvg1b)
                self.mark("scalar", a.activation(w1c[:], dif[:], AF.Sigmoid))
            blk.scalar(ac_sig2)
            acsig2 = self.cnt["sAC"]

            def dv_g2(v):
                v.wait_ge(self.sem["sAC"], acsig2)
                for ti in range(NT):
                    v.tensor_scalar(msb[:, ti * NE:(ti + 1) * NE], scores[:, ti * NE:(ti + 1) * NE],
                                    top8[:, ti * NE:ti * NE + 1], None, ALU.is_equal)
                for ti in range(NT):
                    v.tensor_scalar(eq2[:, ti * NE:(ti + 1) * NE], scores[:, ti * NE:(ti + 1) * NE],
                                    top8[:, ti * NE + 1:ti * NE + 2], None, ALU.is_equal)
                for ti in range(NT):
                    v.tensor_scalar(msb[:, ti * NE:(ti + 1) * NE], msb[:, ti * NE:(ti + 1) * NE],
                                    w1c[:, ti:ti + 1], None, ALU.mult)
                for ti in range(NT):
                    v.tensor_scalar(eq2[:, ti * NE:(ti + 1) * NE], eq2[:, ti * NE:(ti + 1) * NE],
                                    w2c[:, ti:ti + 1], None, ALU.mult)
                inst = v.tensor_tensor(msb[:], msb[:], eq2[:], ALU.add)
                self.mark("vector", inst)
            blk.vector(dv_g2)
            dvg2 = self.cnt["sDV"]

            for tq in range(4):
                def pe_mt(t, tq=tq):
                    t.wait_ge(self.sem["sDV"], dvg2 if tq == 0 else self.cnt["sDV"])
                    for j in range(4):
                        ti = tq * 4 + j
                        inst = t.transpose(ps_rot[0][0:NE, j * 128:(j + 1) * 128],
                                           msb[:, ti * NE:(ti + 1) * NE], self.identF[:])
                    self.mark("tensor", inst)
                blk.tensor(pe_mt)
                pemt = self.cnt["sPE"]

                def dv_mt(v, tq=tq, pemt=pemt):
                    v.wait_ge(self.sem["sPE"], pemt)
                    self.mark("vector", v.tensor_copy(mTb[0:NE, tq * 512:(tq + 1) * 512],
                                                      ps_rot[0][0:NE, 0:512]))
                blk.vector(dv_mt)

            if self.debug:
                dvd = self.cnt["sDV"]

                def s_dbg(s):
                    s.wait_ge(self.sem["sDV"], dvd)
                    self.loadX(s, d["dbg_a"][:, 0:NT * NE], scores[:])
                    self.loadX(s, d["dbg_b"][:, 0:NT * NE], msb[:])
                    for dj in range(ND):
                        self.loadX(s, d["dbg_xn"][dj * 128:(dj + 1) * 128, :], xn[dj][:])
                blk.sync(s_dbg)

            slab_pe_done = {}
            slab_w = {}
            self.dv_psd_free = 0
            sg_hist = []

            def load_slab(i):
                e, j = divmod(i, 4)
                p = i % 2

                def g_l(g, e=e, j=j, p=p, i=i):
                    if i >= 2:
                        g.wait_ge(self.sem["sPE"], slab_pe_done[i - 2])
                    self.load_wchunks(g, wu[p], d["f0_up"][e][:, j * 512:(j + 1) * 512], DM, 512)
                    self.load_wchunks(g, wg[p], d["f0_up"][e][:, DI + j * 512:DI + (j + 1) * 512], DM, 512)
                    self.load_wchunks(g, wd[p], d["f0_down"][e][j * 512:(j + 1) * 512, :], 512, DM)
                blk.gpsimd(g_l)
                slab_w[i] = self.cnt["sW"]

            load_slab(0)
            load_slab(1)

            for e in range(NE):
                pb = e % 2
                dv_mbw = self.cnt["sDV"]
                for tc in range(4):
                    def pe_mb(t, e=e, tc=tc, dv_mbw=dv_mbw):
                        t.wait_ge(self.sem["sDV"], dv_mbw)
                        self.mark("tensor", t.matmul(ps_rot[tc][:], self.sel8[:, e * 128:(e + 1) * 128],
                                                     mTb[0:NE, tc * 512:(tc + 1) * 512],
                                                     start=True, stop=True))
                    blk.tensor(pe_mb)
                    pemb = self.cnt["sPE"]

                    def dv_mb(v, e=e, tc=tc, pemb=pemb, pb=pb):
                        v.wait_ge(self.sem["sPE"], pemb)
                        self.mark("vector", v.tensor_copy(mb[pb][:, tc * 512:(tc + 1) * 512], ps_rot[tc][:]))
                    blk.vector(dv_mb)

                dv_xre_c = self.cnt["sDV"]

                for j in range(4):
                    i = 4 * e + j
                    p = i % 2
                    pe0, ac0, dv0 = self.cnt["sPE"], self.cnt["sAC"], self.cnt["sDV"]

                    def pe_uB(t, p=p, i=i, pe0=pe0, dv0=dv0):
                        t.wait_ge(self.sem["sW"], slab_w[i])
                        for it in range(16):
                            mj, tc = divmod(it, 4)
                            rot = it % 2
                            t.wait_ge(self.sem["sDV"], dv0 + it - 1 if it >= 2 else dv0)
                            for kj in range(ND):
                                t.matmul(ps_rot[rot * 2][:],
                                         wu[p][:, kj * 512 + mj * 128: kj * 512 + (mj + 1) * 128],
                                         xn[kj][:, tc * 512:(tc + 1) * 512],
                                         start=(kj == 0), stop=(kj == ND - 1))
                            for kj in range(ND):
                                inst = t.matmul(ps_rot[rot * 2 + 1][:],
                                                wg[p][:, kj * 512 + mj * 128: kj * 512 + (mj + 1) * 128],
                                                xn[kj][:, tc * 512:(tc + 1) * 512],
                                                start=(kj == 0), stop=(kj == ND - 1))
                            self.mark("tensor", inst)
                    blk.tensor(pe_uB)

                    def ac_siluB(a, pe0=pe0, dv0=dv0):
                        for it in range(16):
                            rot = it % 2
                            a.wait_ge(self.sem["sPE"], pe0 + it + 1)
                            a.wait_ge(self.sem["sDV"], dv0 + it - 1 if it >= 2 else dv0)
                            self.mark("scalar", a.activation(sg[it % 2][:], ps_rot[rot * 2 + 1][:], AF.Silu))
                    blk.scalar(ac_siluB)

                    def dv_hB(v, ac0=ac0, pb=pb):
                        for it in range(16):
                            mj, tc = divmod(it, 4)
                            rot = it % 2
                            v.wait_ge(self.sem["sAC"], ac0 + it + 1)
                            hs = hsb[mj][:, tc * 512:(tc + 1) * 512]
                            v.tensor_tensor(hs, ps_rot[rot * 2][:], sg[it % 2][:], ALU.mult)
                            inst = v.tensor_tensor(hs, hs, mb[pb][:, tc * 512:(tc + 1) * 512], ALU.mult)
                            self.mark("vector", inst)
                    blk.vector(dv_hB)

                    peD0, dvD0 = self.cnt["sPE"], self.cnt["sDV"]

                    def pe_dB(t, p=p, peD0=peD0, dvD0=dvD0):
                        for idx in range(16):
                            tc, mg = divmod(idx, 4)
                            pr = 2 * (idx % 2)
                            t.wait_ge(self.sem["sDV"], dvD0 + idx - 1 if idx >= 2 else dvD0)
                            for mi in range(2):
                                m = mg * 2 + mi
                                for kj in range(4):
                                    inst = t.matmul(
                                        ps_d[pr + mi][:],
                                        wd[p][:, kj * DM + m * 128:kj * DM + (m + 1) * 128],
                                        hsb[kj][:, tc * 512:(tc + 1) * 512],
                                        start=(kj == 0), stop=(kj == 3))
                            self.mark("tensor", inst)
                    blk.tensor(pe_dB)

                    def dv_addB(v, peD0=peD0):
                        for idx in range(16):
                            tc, mg = divmod(idx, 4)
                            pr = 2 * (idx % 2)
                            v.wait_ge(self.sem["sPE"], peD0 + idx + 1)
                            for mi in range(2):
                                m = mg * 2 + mi
                                inst = v.tensor_tensor(
                                    self.resid[m][:, tc * 512:(tc + 1) * 512],
                                    self.resid[m][:, tc * 512:(tc + 1) * 512],
                                    ps_d[pr + mi][:], ALU.add)
                            self.mark("vector", inst)
                    blk.vector(dv_addB)

                    slab_pe_done[i] = self.cnt["sPE"]
                    if i + 2 < 4 * NE:
                        load_slab(i + 2)
        self.barrier()

    def phase_l1a(self):
        nc, blk, d = self.nc, self.block, self.d
        with ExitStack() as ph:
            wbig = ph.enter_context(nc.sbuf_tensor("a1_wbig", [128, 8 * 1024], BF16))
            BTb = ph.enter_context(nc.sbuf_tensor("a1_BTb", [DS, T], BF16))
            CTb = ph.enter_context(nc.sbuf_tensor("a1_CTb", [DS, T], BF16))
            bin_sb = ph.enter_context(nc.sbuf_tensor("a1_binb", [1, DI], BF16))
            bx_sb = ph.enter_context(nc.sbuf_tensor("a1_bxb", [1, 2 * DS], BF16))
            acol = ph.enter_context(nc.sbuf_tensor("a1_acol", [128, NDI * DS], F32))
            dcol = ph.enter_context(nc.sbuf_tensor("a1_dcol", [128, NDI], F32))
            carry = ph.enter_context(nc.sbuf_tensor("a1_carry", [128, NDI * DS], F32))
            ps_rot = [ph.enter_context(nc.psum_tensor(f"a1_rot{i}", [128, 512], F32)) for i in range(4)]

            def g_pre(g):
                self.loadW(g, bx_sb[:], d["a1_bx"][:])
            blk.gpsimd(g_pre)

            drows_cm = nc.sbuf_tensor("a1_drows", [NDI, 128], F32)
            drows = drows_cm.__enter__()

            def s_pre(s):
                self.loadX(s, drows[:], d["a1_D"][:])
                self.loadX(s, acol[:].rearrange("p (db s) -> p db s", db=NDI),
                           d["a1_Alog"][:].rearrange("(db p) s -> p db s", p=128))
            blk.sync(s_pre)
            x_pre = self.cnt["sX"]

            def ac_a(a):
                a.wait_ge(self.sem["sX"], x_pre)
                self.mark("scalar", a.activation(acol[:], acol[:], AF.Exp))
            blk.scalar(ac_a)
            ac_a_c = self.cnt["sAC"]

            def dv_a(v):
                v.wait_ge(self.sem["sAC"], ac_a_c)
                v.tensor_scalar_mul(acol[:], acol[:], -1.0)
                self.mark("vector", v.memset(carry[:], 0.0))
            blk.vector(dv_a)

            def pe_dc(t):
                t.wait_ge(self.sem["sX"], x_pre)
                self.mark("tensor", t.transpose(ps_rot[0][:, 0:NDI], drows[:],
                                                self.identF[0:NDI, 0:NDI]))
            blk.tensor(pe_dc)
            pedc = self.cnt["sPE"]

            def dv_dc(v):
                v.wait_ge(self.sem["sPE"], pedc)
                self.mark("vector", v.tensor_copy(dcol[:], ps_rot[0][:, 0:NDI]))
            blk.vector(dv_dc)
            self.barrier()
            drows_cm.__exit__(None, None, None)

            with ExitStack() as ph0:
                uT = [ph0.enter_context(nc.sbuf_tensor(f"a1_u{i}", [128, T], BF16)) for i in range(NDI)]
                szt = [ph0.enter_context(nc.sbuf_tensor(f"a1_szt{i}", [128, 512], BF16)) for i in range(2)]

                with ExitStack() as ph1:
                    xn = [ph1.enter_context(nc.sbuf_tensor(f"a1_xn{i}", [128, T], BF16)) for i in range(ND)]
                    self.emit_ln(ph1, xn, scratch=wbig[:].bitcast(F32))

                    for half, (cofs, spill) in enumerate([(DI, "sp_sz"), (0, "sp_u")]):
                        for q in range(4):
                            def g_w(g, cofs=cofs, q=q):
                                g.wait_ge(self.sem["sPE"], self.cnt["sPE"])
                                self.load_wchunks(g, wbig,
                                                  d["a1_Win"][:, cofs + q * 512: cofs + (q + 1) * 512],
                                                  DM, 512)
                                if q == 0:
                                    self.loadW(g, bin_sb[:], d["a1_bin"][:, cofs:cofs + DI])
                            blk.gpsimd(g_w)
                            w_c = self.cnt["sW"]
                            pe0, ac0, x0 = self.cnt["sPE"], self.cnt["sAC"], self.cnt["sX"]

                            def pe_bB(t, q=q, cofs=cofs, w_c=w_c, pe0=pe0, ac0=ac0):
                                t.wait_ge(self.sem["sW"], w_c)
                                for it in range(16):
                                    mjl, tc = divmod(it, 4)
                                    mj = q * 4 + mjl
                                    r = it % 2
                                    if it >= 2:
                                        t.wait_ge(self.sem["sAC"], ac0 + it - 1)
                                    for kj in range(ND):
                                        t.matmul(ps_rot[r][:],
                                                 wbig[:, kj * 512 + mjl * 128: kj * 512 + (mjl + 1) * 128],
                                                 xn[kj][:, tc * 512:(tc + 1) * 512],
                                                 start=(kj == 0), stop=False)
                                    inst = t.matmul(ps_rot[r][:],
                                                    bin_sb[0:1, mj * 128:(mj + 1) * 128],
                                                    self.ones[0:1, tc * 512:(tc + 1) * 512],
                                                    start=False, stop=True)
                                    self.mark("tensor", inst)
                            blk.tensor(pe_bB)

                            if half == 0:
                                def ac_evB(a_, q=q, pe0=pe0, x0=x0):
                                    for it in range(16):
                                        r = it % 2
                                        a_.wait_ge(self.sem["sPE"], pe0 + it + 1)
                                        a_.wait_ge(self.sem["sX"], x0 + 16 * (it - 1) if it >= 2 else x0)
                                        self.mark("scalar", a_.activation(szt[it % 2][:],
                                                                          ps_rot[r][:], AF.Silu))
                                blk.scalar(ac_evB)

                                def s_spB(s, q=q, ac0=ac0):
                                    for it in range(16):
                                        mjl, tc = divmod(it, 4)
                                        mj = q * 4 + mjl
                                        s.wait_ge(self.sem["sAC"], ac0 + it + 1)
                                        self.loadX(s, d["sp_sz"][mj * 128:(mj + 1) * 128,
                                                                 tc * 512:(tc + 1) * 512],
                                                   szt[it % 2][:])
                                blk.sync(s_spB)
                            else:
                                def ac_evB(a_, q=q, pe0=pe0):
                                    for it in range(16):
                                        mjl, tc = divmod(it, 4)
                                        mj = q * 4 + mjl
                                        r = it % 2
                                        a_.wait_ge(self.sem["sPE"], pe0 + it + 1)
                                        self.mark("scalar", a_.activation(
                                            uT[mj][:, tc * 512:(tc + 1) * 512],
                                            ps_rot[r][:], AF.Silu))
                                blk.scalar(ac_evB)
                                ac_q = self.cnt["sAC"]

                                def s_spB(s, q=q, ac_q=ac_q):
                                    for mjl in range(4):
                                        mj = q * 4 + mjl
                                        s.wait_ge(self.sem["sAC"], ac_q - (3 - mjl) * 4)
                                        self.loadX(s, d["sp_u"][mj * 128:(mj + 1) * 128, :], uT[mj][:])
                                blk.sync(s_spB)
                    self.barrier()

                wx = self.sb_local(ph0, "a1_wx", [128, NDI * 2 * DS], BF16)

                def g_wx(g):
                    self.load_wchunks(g, wx, d["a1_Wx"], DI, 2 * DS)
                blk.gpsimd(g_wx)
                wx_c = self.cnt["sW"]
                for tc in range(4):
                    dv_prev = self.cnt["sDV"]

                    def pe_bc(t, tc=tc, dv_prev=dv_prev):
                        t.wait_ge(self.sem["sW"], wx_c)
                        t.wait_ge(self.sem["sDV"], dv_prev)
                        for part, c0 in ((0, 0), (1, DS)):
                            for kj in range(NDI):
                                t.matmul(ps_rot[part][0:DS, :],
                                         wx[:, kj * 2 * DS + c0: kj * 2 * DS + c0 + DS],
                                         uT[kj][:, tc * 512:(tc + 1) * 512],
                                         start=(kj == 0), stop=False)
                            inst = t.matmul(ps_rot[part][0:DS, :],
                                            bx_sb[0:1, c0:c0 + DS],
                                            self.ones[0:1, tc * 512:(tc + 1) * 512],
                                            start=False, stop=True)
                        self.mark("tensor", inst)
                    blk.tensor(pe_bc)
                    pebc = self.cnt["sPE"]

                    def dv_bc(v, tc=tc, pebc=pebc):
                        v.wait_ge(self.sem["sPE"], pebc)
                        v.tensor_copy(BTb[0:DS, tc * 512:(tc + 1) * 512], ps_rot[0][0:DS, :])
                        inst = v.tensor_copy(CTb[0:DS, tc * 512:(tc + 1) * 512], ps_rot[1][0:DS, :])
                        self.mark("vector", inst)
                    blk.vector(dv_bc)

                bdt_sb = self.sb_local(ph0, "a1_bdtb", [1, DI], BF16)

                def g_bdt(g):
                    self.loadW(g, bdt_sb[:], d["a1_bdt"][:])
                blk.gpsimd(g_bdt)
                dtt = [self.sb_local(ph0, f"a1_dtt{i}", [128, T], BF16) for i in range(2)]
                dut = [self.sb_local(ph0, f"a1_dut{i}", [128, T], BF16) for i in range(2)]
                et = [self.sb_local(ph0, f"a1_et{i}", [128, 512], F32) for i in range(2)]
                for mh in range(4):
                    def g_wdt(g, mh=mh):
                        g.wait_ge(self.sem["sPE"], self.cnt["sPE"])
                        self.load_wchunks(g, wbig, d["a1_Wdt"][:, mh * 512:(mh + 1) * 512], DI, 512)
                    blk.gpsimd(g_wdt)
                    wdt_c = self.cnt["sW"]
                    peE0, acE0, dvE0, xE0 = (self.cnt["sPE"], self.cnt["sAC"],
                                             self.cnt["sDV"], self.cnt["sX"])

                    def pe_dtB(t, mh=mh, wdt_c=wdt_c, dvE0=dvE0):
                        t.wait_ge(self.sem["sW"], wdt_c)
                        for it in range(16):
                            mjh, tc = divmod(it, 4)
                            mj = mh * 4 + mjh
                            r = tc % 2
                            t.wait_ge(self.sem["sDV"], dvE0 + it - 1 if it >= 2 else dvE0)
                            for kj in range(NDI):
                                t.matmul(ps_rot[r][:],
                                         wbig[:, kj * 512 + mjh * 128: kj * 512 + (mjh + 1) * 128],
                                         uT[kj][:, tc * 512:(tc + 1) * 512],
                                         start=(kj == 0), stop=False)
                            inst = t.matmul(ps_rot[r][:],
                                            bdt_sb[0:1, mj * 128:(mj + 1) * 128],
                                            self.ones[0:1, tc * 512:(tc + 1) * 512],
                                            start=False, stop=True)
                            self.mark("tensor", inst)
                    blk.tensor(pe_dtB)

                    def ac_spB(a_, mh=mh, peE0=peE0, xE0=xE0, dvE0=dvE0):
                        for it in range(16):
                            mjh, tc = divmod(it, 4)
                            pp = (mh * 4 + mjh) % 2
                            r = tc % 2
                            a_.wait_ge(self.sem["sPE"], peE0 + it + 1)
                            if mjh >= 2 and tc == 0:
                                a_.wait_ge(self.sem["sX"], xE0 + 32 * (mjh - 1))
                            elif mjh < 2 and tc == 0:
                                a_.wait_ge(self.sem["sX"], xE0)
                            a_.activation(et[it % 2][:], ps_rot[r][:], AF.Exp)
                            self.mark("scalar", a_.activation(dtt[pp][:, tc * 512:(tc + 1) * 512],
                                                              et[it % 2][:], AF.Ln, bias=1.0))
                    blk.scalar(ac_spB)

                    def dv_duB(v, mh=mh, acE0=acE0):
                        for it in range(16):
                            mjh, tc = divmod(it, 4)
                            mj = mh * 4 + mjh
                            pp = mj % 2
                            v.wait_ge(self.sem["sAC"], acE0 + it + 1)
                            inst = v.tensor_tensor(dut[pp][:, tc * 512:(tc + 1) * 512],
                                                   dtt[pp][:, tc * 512:(tc + 1) * 512],
                                                   uT[mj][:, tc * 512:(tc + 1) * 512], ALU.mult)
                            self.mark("vector", inst)
                    blk.vector(dv_duB)

                    def s_spdtB(s, mh=mh, dvE0=dvE0):
                        for mjh in range(4):
                            mj = mh * 4 + mjh
                            pp = mj % 2
                            s.wait_ge(self.sem["sDV"], dvE0 + (mjh + 1) * 4)
                            self.loadX(s, d["sp_dt"][mj * 128:(mj + 1) * 128, :], dtt[pp][:])
                            self.loadX(s, d["sp_dtu"][mj * 128:(mj + 1) * 128, :], dut[pp][:])
                    blk.sync(s_spdtB)
                self.barrier()


            bout_sb = self.sb_local(ph, "a1_boutb", [1, DM], BF16)

            def g_bout(g):
                self.loadW(g, bout_sb[:], d["a1_bout"][:])
            blk.gpsimd(g_bout)
            Bball = self.sb_local(ph, "a1_Bball", [128, DS * 512], BF16)
            Cball = self.sb_local(ph, "a1_Cball", [128, DS * 512], BF16)
            Xall = self.sb_local(ph, "a1_Xall", [128, DS * 512], BF16)
            hall = self.sb_local(ph, "a1_hall", [128, DS * 512], BF16)
            dA = [self.sb_local(ph, f"a1_dA{i}", [128, 512], BF16) for i in range(2)]
            ygT = [self.sb_local(ph, f"a1_yg{i}", [128, 512], BF16) for i in range(NDI)]
            sl_dt = [self.sb_local(ph, f"a1_sldt{i}", [128, 512], BF16) for i in range(2)]
            sl_du = [self.sb_local(ph, f"a1_sldu{i}", [128, 512], BF16) for i in range(2)]
            sl_u = [self.sb_local(ph, f"a1_slu{i}", [128, 512], BF16) for i in range(2)]
            sl_sz = [self.sb_local(ph, f"a1_slsz{i}", [128, 512], BF16) for i in range(2)]

            dv_slice_done = {}
            for tc in range(4):
                dv_bc_prev = self.cnt["sDV"]
                for s16 in range(DS):
                    for which, (ball, src) in enumerate(((Bball, BTb), (Cball, CTb))):
                        r = (s16 * 2 + which) % 2

                        def pe_bb(t, s16=s16, r=r, src=src, dv_bc_prev=dv_bc_prev):
                            t.wait_ge(self.sem["sDV"], dv_bc_prev if s16 < 1 else self.cnt_sDV_bb[-2])
                            self.mark("tensor", t.matmul(
                                ps_rot[r][:], self.sel16[:, s16 * 128:(s16 + 1) * 128],
                                src[0:DS, tc * 512:(tc + 1) * 512], start=True, stop=True))
                        blk.tensor(pe_bb)
                        pebb = self.cnt["sPE"]

                        def dv_bb(v, s16=s16, r=r, ball=ball, pebb=pebb):
                            v.wait_ge(self.sem["sPE"], pebb)
                            self.mark("vector", v.tensor_copy(ball[:, s16 * 512:(s16 + 1) * 512],
                                                              ps_rot[r][:]))
                        blk.vector(dv_bb)
                        self.cnt_sDV_bb.append(self.cnt["sDV"])

                for db in range(NDI):
                    pp = db % 2
                    prev = dv_slice_done.get((tc, db - 2), dv_slice_done.get((tc - 1, NDI - 2 + db), 0))

                    def s_sl(s, tc=tc, db=db, pp=pp, prev=prev):
                        s.wait_ge(self.sem["sDV"], prev)
                        csl = slice(tc * 512, (tc + 1) * 512)
                        self.loadX(s, sl_dt[pp][:], d["sp_dt"][db * 128:(db + 1) * 128, csl])
                        self.loadX(s, sl_du[pp][:], d["sp_dtu"][db * 128:(db + 1) * 128, csl])
                        self.loadX(s, sl_u[pp][:], d["sp_u"][db * 128:(db + 1) * 128, csl])
                        self.loadX(s, sl_sz[pp][:], d["sp_sz"][db * 128:(db + 1) * 128, csl])
                    blk.sync(s_sl)
                    x_sl = self.cnt["sX"]

                    dvDB0, acDB0 = self.cnt["sDV"], self.cnt["sAC"]

                    def ac_daB(a_, tc=tc, db=db, pp=pp, x_sl=x_sl, dvDB0=dvDB0):
                        a_.wait_ge(self.sem["sX"], x_sl)
                        for s16 in range(DS):
                            a_.wait_ge(self.sem["sDV"], dvDB0 + s16 if s16 >= 2 else dvDB0)
                            inst = a_.activation(dA[s16 % 2][:], sl_dt[pp][:], AF.Exp,
                                                 scale=acol[:, db * DS + s16: db * DS + s16 + 1])
                            self.mark("scalar", inst)
                    blk.scalar(ac_daB)

                    def dv_dbB(v, tc=tc, db=db, pp=pp, x_sl=x_sl, acDB0=acDB0):
                        v.wait_ge(self.sem["sX"], x_sl)
                        rep = sl_du[pp][:].unsqueeze(1).broadcast_to([128, DS, 512])
                        inst = v.tensor_tensor(
                            Xall[:].rearrange("p (s t) -> p s t", s=DS), rep,
                            Bball[:].rearrange("p (s t) -> p s t", s=DS), ALU.mult)
                        self.mark("vector", inst)
                        for s16 in range(DS):
                            v.wait_ge(self.sem["sAC"], acDB0 + s16 + 1)
                            inst = v.tensor_tensor_scan(
                                hall[:, s16 * 512:(s16 + 1) * 512], dA[s16 % 2][:],
                                Xall[:, s16 * 512:(s16 + 1) * 512],
                                carry[:, db * DS + s16: db * DS + s16 + 1],
                                ALU.mult, ALU.add)
                            self.mark("vector", inst)
                        v.tensor_copy(carry[:, db * DS:(db + 1) * DS],
                                      hall[:].rearrange("p (s t) -> p s t", s=DS)[:, :, 511])
                        v.tensor_tensor(Xall[:].rearrange("p (s t) -> p s t", s=DS),
                                        hall[:].rearrange("p (s t) -> p s t", s=DS),
                                        Cball[:].rearrange("p (s t) -> p s t", s=DS), ALU.mult)
                        v.tensor_tensor(Xall[:, 0:8 * 512], Xall[:, 0:8 * 512], Xall[:, 8 * 512:16 * 512], ALU.add)
                        v.tensor_tensor(Xall[:, 0:4 * 512], Xall[:, 0:4 * 512], Xall[:, 4 * 512:8 * 512], ALU.add)
                        v.tensor_tensor(Xall[:, 0:2 * 512], Xall[:, 0:2 * 512], Xall[:, 2 * 512:4 * 512], ALU.add)
                        v.tensor_tensor(Xall[:, 0:512], Xall[:, 0:512], Xall[:, 512:1024], ALU.add)
                        v.scalar_tensor_tensor(Xall[:, 0:512], sl_u[pp][:], dcol[:, db:db + 1],
                                               Xall[:, 0:512], ALU.mult, ALU.add)
                        inst = v.tensor_tensor(ygT[db][:], Xall[:, 0:512], sl_sz[pp][:], ALU.mult)
                        self.mark("vector", inst)
                    blk.vector(dv_dbB)
                    dv_slice_done[(tc, db)] = self.cnt["sDV"]

                dv_yg = self.cnt["sDV"]
                for mh in range(2):
                    def g_wo(g, mh=mh):
                        g.wait_ge(self.sem["sPE"], self.cnt["sPE"])
                        self.load_wchunks(g, wbig, d["a1_Wout"][:, mh * 512:(mh + 1) * 512], DI, 512)
                    blk.gpsimd(g_wo)
                    wo_c = self.cnt["sW"]
                    for ml in range(4):
                        m = mh * 4 + ml
                        r = m % 2
                        dvw = self.wout_dv_hist[-2] if len(self.wout_dv_hist) >= 2 else 0

                        def pe_wo(t, m=m, ml=ml, r=r, tc=tc, dv_yg=dv_yg, dvw=dvw, wo_c=wo_c):
                            t.wait_ge(self.sem["sW"], wo_c)
                            t.wait_ge(self.sem["sDV"], max(dv_yg, dvw))
                            for kj in range(NDI):
                                t.matmul(ps_rot[2 + r][:],
                                         wbig[:, kj * 512 + ml * 128: kj * 512 + (ml + 1) * 128],
                                         ygT[kj][:], start=(kj == 0), stop=False)
                            inst = t.matmul(ps_rot[2 + r][:],
                                            bout_sb[0:1, m * 128:(m + 1) * 128],
                                            self.ones[0:1, tc * 512:(tc + 1) * 512],
                                            start=False, stop=True)
                            self.mark("tensor", inst)
                        blk.tensor(pe_wo)
                        pewo = self.cnt["sPE"]

                        def dv_wo(v, m=m, r=r, tc=tc, pewo=pewo):
                            v.wait_ge(self.sem["sPE"], pewo)
                            inst = v.tensor_tensor(self.resid[m][:, tc * 512:(tc + 1) * 512],
                                                   self.resid[m][:, tc * 512:(tc + 1) * 512],
                                                   ps_rot[2 + r][:], ALU.add)
                            self.mark("vector", inst)
                        blk.vector(dv_wo)
                        self.wout_dv_hist.append(self.cnt["sDV"])
        self.barrier()

    def phase_l1f(self):
        nc, blk, d = self.nc, self.block, self.d
        with ExitStack() as ph:
            xn = [ph.enter_context(nc.sbuf_tensor(f"f1_xn{i}", [128, T], BF16)) for i in range(ND)]
            wr = ph.enter_context(nc.sbuf_tensor("f1_wr", [128, ND * NE], BF16))
            scores = ph.enter_context(nc.sbuf_tensor("f1_scores", [128, NT * NE], F32))
            top8 = ph.enter_context(nc.sbuf_tensor("f1_top8", [128, NT * NE], F32))
            dif = ph.enter_context(nc.sbuf_tensor("f1_dif", [128, NT], F32))
            w2c = ph.enter_context(nc.sbuf_tensor("f1_w2", [128, NT], F32))
            w1c = ph.enter_context(nc.sbuf_tensor("f1_w1", [128, NT], F32))
            eq2 = ph.enter_context(nc.sbuf_tensor("f1_eq2", [128, NT * NE], F32))
            msb = ph.enter_context(nc.sbuf_tensor("f1_m", [128, NT * NE], F32))
            mTb = ph.enter_context(nc.sbuf_tensor("f1_mTb", [NE, T], BF16))
            mb = [ph.enter_context(nc.sbuf_tensor(f"f1_mb{i}", [128, T], BF16)) for i in range(2)]
            upbr = ph.enter_context(nc.sbuf_tensor("f1_upbr", [NE * ND, 128], BF16))
            upbc = ph.enter_context(nc.sbuf_tensor("f1_upbc", [128, NE * ND], F32))
            dnb = ph.enter_context(nc.sbuf_tensor("f1_dnb", [NE, DM], BF16))
            shupb = ph.enter_context(nc.sbuf_tensor("f1_shupbb", [1, H1], BF16))
            shdnb = ph.enter_context(nc.sbuf_tensor("f1_shdnbb", [1, DM], BF16))
            wA = ph.enter_context(nc.sbuf_tensor("f1_wA", [128, ND * H1], BF16))
            wB = ph.enter_context(nc.sbuf_tensor("f1_wB", [128, ND * DM], BF16))
            hbuf = [ph.enter_context(nc.sbuf_tensor(f"f1_h{i}", [128, T], BF16)) for i in range(ND)]

            def g_pre(g):
                self.load_wchunks(g, wr, d["f1_gate"], DM, NE)
                self.loadW(g, dnb[:], d["f1_downb"][:])
                self.loadW(g, shupb[:], d["f1_shupb"][:])
                self.loadW(g, shdnb[:], d["f1_shdownb"][:])
                self.load_wchunks(g, wA, d["f1_shup"], DM, H1)
                self.load_wchunks(g, wB, d["f1_shdown"], H1, DM)
            blk.gpsimd(g_pre)
            w_pre = self.cnt["sW"]

            self.emit_ln(ph, xn, scratch=wA[:].bitcast(F32))
            ps_rot = [ph.enter_context(nc.psum_tensor(f"f1_rot{i}", [128, 512], F32)) for i in range(4)]
            ps_d = [ph.enter_context(nc.psum_tensor(f"f1_psd{i}", [128, 512], F32)) for i in range(4)]

            pe0, dv0 = self.cnt["sPE"], self.cnt["sDV"]
            for ti in range(NT):
                r = ti % 4

                def pe_s(t, ti=ti, r=r):
                    t.wait_ge(self.sem["sW"], w_pre)
                    t.wait_ge(self.sem["sDV"], dv0 + ti - 3 if ti >= 4 else dv0)
                    for kj in range(ND):
                        inst = t.matmul(ps_rot[r][:, 0:NE], xn[kj][:, ti * 128:(ti + 1) * 128],
                                        wr[:, kj * NE:(kj + 1) * NE],
                                        start=(kj == 0), stop=(kj == ND - 1))
                    self.mark("tensor", inst)
                blk.tensor(pe_s)
                pec = self.cnt["sPE"]

                def dv_s(v, ti=ti, r=r, pec=pec):
                    v.wait_ge(self.sem["sPE"], pec)
                    self.mark("vector", v.tensor_copy(scores[:, ti * NE:(ti + 1) * NE], ps_rot[r][:, 0:NE]))
                blk.vector(dv_s)

            def dv_g1(v):
                for ti in range(NT):
                    v.max(top8[:, ti * NE:(ti + 1) * NE], scores[:, ti * NE:(ti + 1) * NE])
                t8v = top8[:].rearrange("p (ti e) -> p ti e", ti=NT)
                inst = v.tensor_tensor(dif[:], t8v[:, :, 1], t8v[:, :, 0], ALU.subtract)
                self.mark("vector", inst)
            blk.vector(dv_g1)
            dvg1 = self.cnt["sDV"]

            def ac_sig(a):
                a.wait_ge(self.sem["sDV"], dvg1)
                self.mark("scalar", a.activation(w2c[:], dif[:], AF.Sigmoid))
            blk.scalar(ac_sig)
            acsig = self.cnt["sAC"]

            def dv_g1b(v):
                v.wait_ge(self.sem["sAC"], acsig)
                t8v = top8[:].rearrange("p (ti e) -> p ti e", ti=NT)
                self.mark("vector", v.tensor_tensor(dif[:], t8v[:, :, 0], t8v[:, :, 1], ALU.subtract))
            blk.vector(dv_g1b)
            dvg1b = self.cnt["sDV"]

            def ac_sig2(a):
                a.wait_ge(self.sem["sDV"], dvg1b)
                self.mark("scalar", a.activation(w1c[:], dif[:], AF.Sigmoid))
            blk.scalar(ac_sig2)
            acsig2 = self.cnt["sAC"]

            def dv_g2(v):
                v.wait_ge(self.sem["sAC"], acsig2)
                for ti in range(NT):
                    v.tensor_scalar(msb[:, ti * NE:(ti + 1) * NE], scores[:, ti * NE:(ti + 1) * NE],
                                    top8[:, ti * NE:ti * NE + 1], None, ALU.is_equal)
                for ti in range(NT):
                    v.tensor_scalar(eq2[:, ti * NE:(ti + 1) * NE], scores[:, ti * NE:(ti + 1) * NE],
                                    top8[:, ti * NE + 1:ti * NE + 2], None, ALU.is_equal)
                for ti in range(NT):
                    v.tensor_scalar(msb[:, ti * NE:(ti + 1) * NE], msb[:, ti * NE:(ti + 1) * NE],
                                    w1c[:, ti:ti + 1], None, ALU.mult)
                for ti in range(NT):
                    v.tensor_scalar(eq2[:, ti * NE:(ti + 1) * NE], eq2[:, ti * NE:(ti + 1) * NE],
                                    w2c[:, ti:ti + 1], None, ALU.mult)
                inst = v.tensor_tensor(msb[:], msb[:], eq2[:], ALU.add)
                self.mark("vector", inst)
            blk.vector(dv_g2)
            dvg2 = self.cnt["sDV"]

            for tq in range(4):
                def pe_mt(t, tq=tq):
                    t.wait_ge(self.sem["sDV"], dvg2 if tq == 0 else self.cnt["sDV"])
                    for j in range(4):
                        ti = tq * 4 + j
                        inst = t.transpose(ps_rot[0][0:NE, j * 128:(j + 1) * 128],
                                           msb[:, ti * NE:(ti + 1) * NE], self.identF[:])
                    self.mark("tensor", inst)
                blk.tensor(pe_mt)
                pemt = self.cnt["sPE"]

                def dv_mt(v, tq=tq, pemt=pemt):
                    v.wait_ge(self.sem["sPE"], pemt)
                    self.mark("vector", v.tensor_copy(mTb[0:NE, tq * 512:(tq + 1) * 512],
                                                      ps_rot[0][0:NE, 0:512]))
                blk.vector(dv_mt)

            def g_sh(g):
                g.wait_ge(self.sem["sDV"], self.cnt["sDV"])
                self.load_wchunks(g, wA, d["f1_shup"], DM, H1)
                self.load_wchunks(g, wB, d["f1_shdown"], H1, DM)
                self.loadW(g, upbr[:], d["f1_upb"][:])
            blk.gpsimd(g_sh)
            w_sh = self.cnt["sW"]

            ps_ub = ps_rot[0][:].bitcast(BF16)[:, 0:NE * ND]

            def pe_ub(t):
                t.wait_ge(self.sem["sW"], w_sh)
                self.mark("tensor", t.transpose(ps_ub, upbr[:],
                                                self.identB[0:NE * ND, 0:NE * ND]))
            blk.tensor(pe_ub)
            peub = self.cnt["sPE"]

            def dv_ub(v):
                v.wait_ge(self.sem["sPE"], peub)
                self.mark("vector", v.tensor_copy(upbc[:], ps_ub))
            blk.vector(dv_ub)

            pe0, ac0, dvP0 = self.cnt["sPE"], self.cnt["sAC"], self.cnt["sDV"]

            def pe_s1B(t, pe0=pe0, ac0=ac0, dvP0=dvP0):
                t.wait_ge(self.sem["sW"], w_sh)
                t.wait_ge(self.sem["sDV"], dvP0)
                for it in range(32):
                    mj, tc = divmod(it, 4)
                    r = it % 2
                    if it >= 2:
                        t.wait_ge(self.sem["sAC"], ac0 + it - 1)
                    for kj in range(ND):
                        t.matmul(ps_rot[r][:],
                                 wA[:, kj * H1 + mj * 128: kj * H1 + (mj + 1) * 128],
                                 xn[kj][:, tc * 512:(tc + 1) * 512],
                                 start=(kj == 0), stop=False)
                    inst = t.matmul(ps_rot[r][:],
                                    shupb[0:1, mj * 128:(mj + 1) * 128],
                                    self.ones[0:1, tc * 512:(tc + 1) * 512],
                                    start=False, stop=True)
                    self.mark("tensor", inst)
            blk.tensor(pe_s1B)

            def ac_s1B(a, pe0=pe0):
                for it in range(32):
                    mj, tc = divmod(it, 4)
                    r = it % 2
                    a.wait_ge(self.sem["sPE"], pe0 + it + 1)
                    self.mark("scalar", a.activation(hbuf[mj][:, tc * 512:(tc + 1) * 512],
                                                     ps_rot[r][:], AF.Silu))
            blk.scalar(ac_s1B)
            ac_s1_done = self.cnt["sAC"]

            peS0, dvS0 = self.cnt["sPE"], self.cnt["sDV"]

            def pe_s2B(t, peS0=peS0, dvS0=dvS0):
                t.wait_ge(self.sem["sAC"], ac_s1_done)
                t.wait_ge(self.sem["sDV"], dvS0)
                for it in range(32):
                    m, tc = divmod(it, 4)
                    r = 2 + it % 2
                    if it >= 2:
                        t.wait_ge(self.sem["sDV"], dvS0 + it - 1)
                    for kj in range(ND):
                        t.matmul(ps_rot[r][:],
                                 wB[:, kj * DM + m * 128: kj * DM + (m + 1) * 128],
                                 hbuf[kj][:, tc * 512:(tc + 1) * 512],
                                 start=(kj == 0), stop=False)
                    inst = t.matmul(ps_rot[r][:],
                                    shdnb[0:1, m * 128:(m + 1) * 128],
                                    self.ones[0:1, tc * 512:(tc + 1) * 512],
                                    start=False, stop=True)
                    self.mark("tensor", inst)
            blk.tensor(pe_s2B)

            def dv_s2B(v, peS0=peS0):
                for it in range(32):
                    m, tc = divmod(it, 4)
                    r = 2 + it % 2
                    v.wait_ge(self.sem["sPE"], peS0 + it + 1)
                    inst = v.tensor_tensor(self.resid[m][:, tc * 512:(tc + 1) * 512],
                                           self.resid[m][:, tc * 512:(tc + 1) * 512],
                                           ps_rot[r][:], ALU.add)
                    self.mark("vector", inst)
            blk.vector(dv_s2B)

            exp_pe_done = 0
            self.dv_psd_free = self.cnt["sDV"]
            h_dv_hist = []
            for e in range(NE):
                pb = e % 2
                dv_mbw = self.cnt["sDV"]
                for tc in range(4):
                    def pe_mb(t, e=e, tc=tc, dv_mbw=dv_mbw):
                        t.wait_ge(self.sem["sDV"], dv_mbw)
                        self.mark("tensor", t.matmul(ps_rot[tc][:], self.sel8[:, e * 128:(e + 1) * 128],
                                                     mTb[0:NE, tc * 512:(tc + 1) * 512],
                                                     start=True, stop=True))
                    blk.tensor(pe_mb)
                    pemb = self.cnt["sPE"]

                    def dv_mb(v, tc=tc, pemb=pemb, pb=pb):
                        v.wait_ge(self.sem["sPE"], pemb)
                        self.mark("vector", v.tensor_copy(mb[pb][:, tc * 512:(tc + 1) * 512], ps_rot[tc][:]))
                    blk.vector(dv_mb)
                dv_mb_done = self.cnt["sDV"]

                def g_e(g, e=e, exp_pe_done=exp_pe_done):
                    g.wait_ge(self.sem["sPE"], self.cnt["sPE"])
                    self.load_wchunks(g, wA, d["f1_up"][e], DM, H1)
                    self.load_wchunks(g, wB, d["f1_down"][e], H1, DM)
                blk.gpsimd(g_e)
                w_e = self.cnt["sW"]

                peU0, acU0, dvU0 = self.cnt["sPE"], self.cnt["sAC"], self.cnt["sDV"]

                def pe_upB(t, e=e, w_e=w_e, peU0=peU0, dvU0=dvU0):
                    t.wait_ge(self.sem["sW"], w_e)
                    for it in range(32):
                        mj, tc = divmod(it, 4)
                        r = it % 2
                        t.wait_ge(self.sem["sDV"], dvU0 + it - 1 if it >= 2 else dvU0)
                        for kj in range(ND):
                            inst = t.matmul(ps_rot[r][:],
                                            wA[:, kj * H1 + mj * 128: kj * H1 + (mj + 1) * 128],
                                            xn[kj][:, tc * 512:(tc + 1) * 512],
                                            start=(kj == 0), stop=(kj == ND - 1))
                        self.mark("tensor", inst)
                blk.tensor(pe_upB)

                def ac_upB(a, e=e, peU0=peU0, dvU0=dvU0):
                    for it in range(32):
                        mj, tc = divmod(it, 4)
                        r = it % 2
                        a.wait_ge(self.sem["sPE"], peU0 + it + 1)
                        if it >= 2:
                            a.wait_ge(self.sem["sDV"], dvU0 + it - 1)
                        self.mark("scalar", a.activation(hbuf[mj][:, tc * 512:(tc + 1) * 512],
                                                         ps_rot[r][:], AF.Silu,
                                                         bias=upbc[:, e * ND + mj: e * ND + mj + 1]))
                blk.scalar(ac_upB)

                def dv_hcB(v, pb=pb, acU0=acU0):
                    for it in range(32):
                        mj, tc = divmod(it, 4)
                        v.wait_ge(self.sem["sAC"], acU0 + it + 1)
                        hs = hbuf[mj][:, tc * 512:(tc + 1) * 512]
                        inst = v.tensor_tensor(hs, hs, mb[pb][:, tc * 512:(tc + 1) * 512], ALU.mult)
                        self.mark("vector", inst)
                blk.vector(dv_hcB)
                dv_h_done = self.cnt["sDV"]

                peD0, dvD0 = self.cnt["sPE"], self.cnt["sDV"]

                def pe_dnB(t, w_e=w_e, peD0=peD0, dvD0=dvD0):
                    t.wait_ge(self.sem["sW"], w_e)
                    for idx in range(16):
                        tc, mg = divmod(idx, 4)
                        pr = 2 * (idx % 2)
                        t.wait_ge(self.sem["sDV"], dvD0 + idx - 1 if idx >= 2 else dvD0)
                        for mi in range(2):
                            m = mg * 2 + mi
                            for kj in range(ND):
                                inst = t.matmul(ps_d[pr + mi][:],
                                                wB[:, kj * DM + m * 128: kj * DM + (m + 1) * 128],
                                                hbuf[kj][:, tc * 512:(tc + 1) * 512],
                                                start=(kj == 0), stop=(kj == ND - 1))
                        self.mark("tensor", inst)
                blk.tensor(pe_dnB)

                def dv_dnB(v, peD0=peD0):
                    for idx in range(16):
                        tc, mg = divmod(idx, 4)
                        pr = 2 * (idx % 2)
                        v.wait_ge(self.sem["sPE"], peD0 + idx + 1)
                        for mi in range(2):
                            m = mg * 2 + mi
                            inst = v.tensor_tensor(self.resid[m][:, tc * 512:(tc + 1) * 512],
                                                   self.resid[m][:, tc * 512:(tc + 1) * 512],
                                                   ps_d[pr + mi][:], ALU.add)
                        self.mark("vector", inst)
                blk.vector(dv_dnB)
                exp_pe_done = self.cnt["sPE"]

            for m in range(ND):
                for tc in range(4):
                    it = m * 4 + tc
                    r = it % 4
                    dvf = self.cnt["sDV"]

                    def pe_db(t, m=m, tc=tc, r=r, dvf=dvf):
                        t.wait_ge(self.sem["sDV"], dvf)
                        self.mark("tensor", t.matmul(ps_rot[r][:],
                                                     dnb[0:NE, m * 128:(m + 1) * 128],
                                                     mTb[0:NE, tc * 512:(tc + 1) * 512],
                                                     start=True, stop=True))
                    blk.tensor(pe_db)
                    pedb = self.cnt["sPE"]

                    def dv_db(v, m=m, tc=tc, r=r, pedb=pedb):
                        v.wait_ge(self.sem["sPE"], pedb)
                        inst = v.tensor_tensor(self.resid[m][:, tc * 512:(tc + 1) * 512],
                                               self.resid[m][:, tc * 512:(tc + 1) * 512],
                                               ps_rot[r][:], ALU.add)
                        self.mark("vector", inst)
                    blk.vector(dv_db)
        self.barrier()

    def phase_out(self):
        nc, blk, d = self.nc, self.block, self.d
        with ExitStack() as ph:
            ytok = [ph.enter_context(nc.sbuf_tensor(f"out_ytok{i}", [128, DM], F32)) for i in range(2)]
            pst = [ph.enter_context(nc.psum_tensor(f"out_ps{i}", [128, 512], F32)) for i in range(2)]
            pe0, dv0, x0 = self.cnt["sPE"], self.cnt["sDV"], self.cnt["sX"]

            def pe_body(t):
                for ti in range(NT):
                    for half in range(2):
                        if ti >= 1:
                            t.wait_ge(self.sem["sDV"], dv0 + 2 * (ti - 1) + half + 1)
                        for j in range(4):
                            dj = half * 4 + j
                            inst = t.transpose(pst[half][:, j * 128:(j + 1) * 128],
                                               self.resid[dj][:, ti * 128:(ti + 1) * 128],
                                               self.identF[:])
                        self.mark("tensor", inst)
            blk.tensor(pe_body)

            def dv_body(v):
                for ti in range(NT):
                    if ti >= 2:
                        v.wait_ge(self.sem["sX"], x0 + 16 * (ti - 1))
                    for half in range(2):
                        v.wait_ge(self.sem["sPE"], pe0 + 2 * ti + half + 1)
                        inst = v.tensor_copy(ytok[ti % 2][:, half * 512:(half + 1) * 512],
                                             pst[half][:, 0:512])
                        self.mark("vector", inst)
            blk.vector(dv_body)

            def s_body(s):
                for ti in range(NT):
                    s.wait_ge(self.sem["sDV"], dv0 + 2 * ti + 2)
                    self.loadX(s, d["y_out"][ti * 128:(ti + 1) * 128, :], ytok[ti % 2][:])
                s.wait_ge(self.sem["sX"], self.cnt["sX"])
            blk.sync(s_body)


def host_constants(blocks):
    c = {}
    cvals = [0.0, 1.0, -0.5, LN_EPS, RMS_EPS, SCALE, -SCALE, 1.0 / DM]
    c["cvals"] = np.broadcast_to(np.asarray(cvals, np.float32), (128, len(cvals))).copy()
    c["identF"] = np.eye(128, dtype=np.float32)
    c["onesrowF"] = np.ones((1, 128), np.float32)
    c["onesrow"] = np.ones((1, T), np.float32)
    c["identB"] = np.eye(128, dtype=BF16NP)
    if "l0a" in blocks:
        q = np.arange(128)
        m = np.where(q[None, :] <= q[:, None], 0.0, -1e9).astype(np.float32)
        c["maskT"] = m.T.copy().astype(BF16NP)
    if "l0f" in blocks or "l1f" in blocks:
        s8 = np.zeros((NE, NE * 128), BF16NP)
        for e in range(NE):
            s8[e, e * 128:(e + 1) * 128] = 1.0
        c["sel8"] = s8
    if "l1a" in blocks:
        s16 = np.zeros((DS, DS * 128), BF16NP)
        for s in range(DS):
            s16[s, s * 128:(s + 1) * 128] = 1.0
        c["sel16"] = s16
    return c


def host_weights(inputs, blocks):
    w = {}
    f32 = np.float32
    asc = np.ascontiguousarray
    if "l0a" in blocks:
        w["wq_lat"] = asc(inputs["L0_attn_wq_lat"], dtype=f32)
        w["wk_lat"] = asc(inputs["L0_attn_wk_lat"], dtype=f32)
        w["wv_lat"] = asc(inputs["L0_attn_wv_lat"], dtype=f32)
        w["wq_heads"] = asc(inputs["L0_attn_wq_heads"], dtype=f32)
        w["wk_heads"] = asc(inputs["L0_attn_wk_heads"], dtype=f32)
        w["wv_heads"] = asc(inputs["L0_attn_wv_heads"], dtype=f32)
        w["w_proj"] = asc(inputs["L0_attn_w_proj"], dtype=f32)
        w["b_proj"] = asc(inputs["L0_attn_b_proj"].reshape(1, DM), dtype=f32)
    if "l0f" in blocks:
        w["f0_scale"] = asc(inputs["L0_ffn_norm_scale"].reshape(ND, 128), dtype=f32)
        w["f0_router"] = asc(inputs["L0_ffn_w_router"].T, dtype=f32)
        w["f0_up"] = asc(inputs["L0_ffn_w_up"].transpose(0, 2, 1), dtype=f32)
        w["f0_down"] = asc(inputs["L0_ffn_w_down"].transpose(0, 2, 1), dtype=f32)
    if "l1a" in blocks:
        w["a1_Win"] = asc(inputs["L1_attn_W_in"], dtype=f32)
        w["a1_bin"] = asc(inputs["L1_attn_b_in"].reshape(1, -1), dtype=f32)
        w["a1_Wx"] = asc(inputs["L1_attn_W_x"], dtype=f32)
        w["a1_bx"] = asc(inputs["L1_attn_b_x"].reshape(1, -1), dtype=f32)
        w["a1_Wdt"] = asc(inputs["L1_attn_W_dt"], dtype=f32)
        w["a1_bdt"] = asc(inputs["L1_attn_b_dt"].reshape(1, -1), dtype=f32)
        w["a1_Wout"] = asc(inputs["L1_attn_W_out"], dtype=f32)
        w["a1_bout"] = asc(inputs["L1_attn_b_out"].reshape(1, -1), dtype=f32)
        w["a1_Alog"] = asc(inputs["L1_attn_A_log"], dtype=f32)
        w["a1_D"] = asc(inputs["L1_attn_D_param"].reshape(NDI, 128), dtype=f32)
    if "l1f" in blocks:
        w["f1_gate"] = asc(inputs["L1_ffn_gate_w"].T, dtype=f32)
        w["f1_up"] = asc(inputs["L1_ffn_up_w"], dtype=f32)
        w["f1_upb"] = asc(inputs["L1_ffn_up_b"].reshape(NE * ND, 128), dtype=f32)
        w["f1_down"] = asc(inputs["L1_ffn_down_w"], dtype=f32)
        w["f1_downb"] = asc(inputs["L1_ffn_down_b"], dtype=f32)
        w["f1_shup"] = asc(inputs["L1_ffn_sh_up_w"], dtype=f32)
        w["f1_shupb"] = asc(inputs["L1_ffn_sh_up_b"].reshape(1, -1), dtype=f32)
        w["f1_shdown"] = asc(inputs["L1_ffn_sh_down_w"], dtype=f32)
        w["f1_shdownb"] = asc(inputs["L1_ffn_sh_down_b"].reshape(1, -1), dtype=f32)
    return w


def run_blocks(x_full, inputs, blocks):
    nc = Builder(blocks).build()
    w = host_weights(inputs, blocks)
    c = host_constants(blocks)
    in_maps = []
    for b in range(8):
        m = {"x_in": np.ascontiguousarray(x_full[b], dtype=np.float32)}
        m.update(w)
        m.update(c)
        in_maps.append(m)
    res = run_bass_kernel_spmd(nc, in_maps, core_ids=list(range(8)))
    return np.stack([r["y_out"] for r in res.results], axis=0)


def kernel(**inputs):
    x = np.asarray(inputs["L0_attn_x"], dtype=np.float32)
    return run_blocks(x, inputs, list(ALL_BLOCKS))
